# version 22
# baseline (speedup 1.0000x reference)
import numpy as np
from contextlib import ExitStack

import concourse.bass as bass
import concourse.mybir as mybir
from concourse.bass_utils import run_bass_kernel_spmd

F32 = mybir.dt.float32
BF16 = mybir.dt.bfloat16
I32 = mybir.dt.int32
AF = mybir.ActivationFunctionType
ALU = mybir.AluOpType
AX = mybir.AxisListType

B, L, D = 4, 4096, 1024
H, HD, H8 = 16, 64, 8
TH = L // 2
NT = L // 128
NTH = TH // 128
DC = D // 128
II = 2816
IC = II // 128
DH = 512
P3 = 3
QMAX = 127.0
EPS = 1e-5
RC2 = 12582912.0
RC2C = 12583041.0
EPS5 = 0.499969482421875
QHI = 127.49
QLO = -128.49

_CACHE = {}


class Sem:
    def __init__(self, h):
        self.h = h
        self.n = 0

    def inc(self, instr, k=1):
        instr.then_inc(self.h, k)
        self.n += k
        return self.n


class K:
    pass


def _mk_inputs(nc, k):
    d = {}

    def DI(name, shape, dt=F32):
        d[name] = nc.dram_tensor(name, shape, dt, kind="ExternalInput").ap()
    DI("x_full", [L, D]); DI("x_my", [TH, D])
    DI("wqT", [D, 512]); DI("wkT", [D, 512]); DI("wvT", [D, 512])
    DI("bq", [512]); DI("bk", [512]); DI("bv", [512])
    DI("woT", [D, D]); DI("bo", [D])
    DI("wg", [II, D]); DI("wu", [II, D]); DI("wd", [D, II])
    DI("clipx", [1, 1]); DI("clipd", [1, 1])
    DI("sw1T", [D, DH]); DI("sb1", [DH]); DI("sw2T", [DH, P3]); DI("sb2", [P3, 1])
    DI("pw1T", [P3 * D, DH]); DI("pb1", [P3 * DH]); DI("pw2T", [P3 * DH, H])
    DI("pb2", [P3 * H])
    DI("identf", [128, 128]); DI("identb", [128, 128], BF16)
    DI("onesf", [128, 1]); DI("onesr", [1, 128])
    DI("trimask", [128, 128], BF16)
    DI("toffr", [1, 1], I32)
    DI("hsel", [16, 8])
    k.din = d
    k.y = nc.dram_tensor("y", [TH, D], F32, kind="ExternalOutput").ap()
    k.dbg = {}
    if k.debug:
        def DO(name, shape, dt=F32):
            k.dbg[name] = nc.dram_tensor(name, shape, dt, kind="ExternalOutput").ap()
        DO("d_xnmy", [128, D]); DO("d_ixq", [128, 512], BF16); DO("d_gate", [128, 512])
        DO("d_h", [128, 512]); DO("d_hg", [1, H]); DO("d_xnfm", [128, 512], BF16)
        DO("d_q", [128, 256], BF16); DO("d_k", [128, 256], BF16); DO("d_v", [128, 520], BF16)
        DO("d_ao", [128, 512], BF16); DO("d_at", [128, 256], BF16); DO("d_dn", [128, 1024])
        DO("d_sc", [1, 16]); DO("d_nB", [1, 16], F32); DO("d_scg", [128, IC])
        DO("d_iwg", [128, 512], BF16); DO("d_pb", [128, 1024], BF16)
        DO("d_ixq7", [128, 512], BF16); DO("d_iwg7", [128, 512], BF16)
        DO("d_pv", [128, 8])
    k.h_spill = nc.dram_tensor("h_spill", [II, TH], F32, kind="Internal")
    k.ihg_d = nc.dram_tensor("ihg_d", [2 * IC * 128, 1024], BF16, kind="Internal")
    k.cc_ain = nc.dram_tensor("cc_ain", [512, L], BF16, kind="Internal")
    k.cc_aout = nc.dram_tensor("cc_aout", [1024, L], BF16, kind="Internal")
    k.mxn_in = nc.dram_tensor("mxn_in", [1, 16], F32, kind="Internal")
    k.mxn_out = nc.dram_tensor("mxn_out", [1, 16], F32, kind="Internal")
    k.mxh_in = nc.dram_tensor("mxh_in", [1, 16], F32, kind="Internal")
    k.mxh_out = nc.dram_tensor("mxh_out", [1, 16], F32, kind="Internal")


def _alloc_persistent(nc, k, en):
    k.identf = en(nc.sbuf_tensor("identf_t", [128, 128], F32, side="right"))
    k.identb = en(nc.sbuf_tensor("identb_t", [128, 128], BF16, side="right"))
    k.onesf = en(nc.sbuf_tensor("onesf_t", [128, 1], F32, side="right"))
    k.onesr = en(nc.sbuf_tensor("onesr_t", [1, 128], F32, side="right"))
    k.trimask = en(nc.sbuf_tensor("trimask_t", [128, 128], BF16, side="right"))
    k.bq = en(nc.sbuf_tensor("bq_t", [128, 4], F32, side="right"))
    k.bk = en(nc.sbuf_tensor("bk_t", [128, 4], F32, side="right"))
    k.bv_bc = en(nc.sbuf_tensor("bv_bc", [128, 512], F32, side="right"))
    k.bo_bc = en(nc.sbuf_tensor("bo_bc", [128, D], F32, side="right"))
    k.rsx = en(nc.sbuf_tensor("rsx_v", [128, 1], F32, side="right"))
    k.rsh = en(nc.sbuf_tensor("rsh_v", [128, 1], F32, side="right"))
    k.sxh = en(nc.sbuf_tensor("sxh_t", [1, 12], F32, side="right"))
    k.mxs = en(nc.sbuf_tensor("mxs_t", [1, 16], F32, side="right"))
    k.scg = en(nc.sbuf_tensor("scg_t", [128, IC], F32, side="right"))
    k.scu = en(nc.sbuf_tensor("scu_t", [128, IC], F32, side="right"))
    k.scbg = en(nc.sbuf_tensor("scbg_t", [128, IC], F32, side="right"))
    k.scbu = en(nc.sbuf_tensor("scbu_t", [128, IC], F32, side="right"))
    k.hg = en(nc.sbuf_tensor("hg_t", [1, H], F32, side="right"))
    k.negB = en(nc.sbuf_tensor("negB_t", [128, H8], F32, side="right"))
    k.stat = en(nc.sbuf_tensor("stat_t", [128, 16], F32, side="right"))
    k.trow = en(nc.sbuf_tensor("trow_t", [1, 128], F32, side="right"))
    k.gsc = en(nc.sbuf_tensor("gsc_t", [128, 64], F32, side="right"))
    k.ps_sm = None


def bcast_scalar(k, blk, src_ap, dst_col_ap):
    s_pe, s_dv = k.s_pe, k.s_dv
    nw = s_dv.n

    def f_pe(t):
        t.wait_ge(s_dv.h, nw)
        t.wait_ge(s_dv.h, k.ps_sm_free[0])
        s_pe.inc(t.matmul(k.ps_sm[:, 0:1], k.onesr[:], src_ap, start=True, stop=True))
    blk.tensor(f_pe)
    npe = s_pe.n

    def f_dv(v):
        v.wait_ge(s_pe.h, npe)
        s_dv.inc(v.tensor_copy(dst_col_ap, k.ps_sm[:, 0:1]))
    blk.vector(f_dv)
    k.ps_sm_free[0] = s_dv.n
    return s_dv.n


def _build(debug=False):
    nc = bass.Bass("TRN2", target_bir_lowering=False)
    k = K()
    k.debug = debug
    _mk_inputs(nc, k)
    stk = ExitStack()
    en = stk.enter_context
    k.s_sy = Sem(en(nc.semaphore("s_sy")))
    k.s_gp = Sem(en(nc.semaphore("s_gp")))
    k.s_cc = Sem(en(nc.semaphore("s_cc")))
    k.s_pe = Sem(en(nc.semaphore("s_pe")))
    k.s_dv = Sem(en(nc.semaphore("s_dv")))
    k.s_ac = Sem(en(nc.semaphore("s_ac")))
    k.s_st = Sem(en(nc.semaphore("s_st")))
    for nm in ("cst", "x0", "x1", "wst", "w0", "w1", "gb", "gw0", "gw1", "gl",
               "mx", "sp0", "sp1", "tp", "cca", "h0", "h1", "pa", "st0", "st1",
               "tg", "tu", "ho"):
        setattr(k, "s_" + nm, Sem(en(nc.semaphore("s_" + nm))))
    k.alls = [k.s_sy, k.s_gp, k.s_cc, k.s_pe, k.s_dv, k.s_ac, k.s_st] + [
        getattr(k, "s_" + nm) for nm in
        ("cst", "x0", "x1", "wst", "w0", "w1", "gb", "gw0", "gw1", "gl",
         "mx", "sp0", "sp1", "tp", "cca", "h0", "h1", "pa", "st0", "st1",
         "tg", "tu", "ho")]
    blk = en(nc.Block())
    k.blk = blk
    _alloc_persistent(nc, k, en)
    s_sy, s_gp, s_cc = k.s_sy, k.s_gp, k.s_cc
    s_pe, s_dv, s_ac, s_st = k.s_pe, k.s_dv, k.s_ac, k.s_st
    din = k.din

    def barrier():
        snap = [(s, s.n) for s in k.alls]
        for sec in (blk.sync, blk.vector, blk.scalar, blk.tensor, blk.gpsimd):
            def f(e, snap=snap):
                for s, n in snap:
                    if n > 0:
                        e.wait_ge(s.h, n)
            sec(f)
    k.barrier = barrier

    k.hsel = en(nc.sbuf_tensor("bh_hsel", [16, 8], F32, side="right"))

    def f_c(e):
        for dst, src in ((k.identf, din["identf"]), (k.identb, din["identb"]),
                         (k.onesf, din["onesf"]), (k.onesr, din["onesr"]),
                         (k.trimask, din["trimask"]), (k.hsel, din["hsel"])):
            k.s_cst.inc(e.dma_start(out=dst[:], in_=src), 16)
        for c in range(4):
            k.s_cst.inc(e.dma_start(out=k.bq[:, c:c + 1],
                                 in_=din["bq"][c * 128:(c + 1) * 128].rearrange("(p i) -> p i", p=128)), 16)
            k.s_cst.inc(e.dma_start(out=k.bk[:, c:c + 1],
                                 in_=din["bk"][c * 128:(c + 1) * 128].rearrange("(p i) -> p i", p=128)), 16)
        k.s_cst.inc(e.dma_start(out=k.sxh[0:1, 4:5], in_=din["clipx"]), 16)
        k.s_cst.inc(e.dma_start(out=k.sxh[0:1, 5:6], in_=din["clipd"]), 16)
    blk.sync(f_c)

    bvrow = en(nc.sbuf_tensor("bvrow", [1, 512], F32, side="right"))
    borow = en(nc.sbuf_tensor("borow", [1, D], F32, side="right"))

    def f_c2(e):
        k.s_cst.inc(e.dma_start(out=bvrow[:], in_=din["bv"].rearrange("(i f) -> i f", i=1)), 16)
        k.s_cst.inc(e.dma_start(out=borow[:], in_=din["bo"].rearrange("(i f) -> i f", i=1)), 16)
    blk.sync(f_c2)
    k.n_cst = k.s_cst.n
    n_rows = k.n_cst

    with nc.psum_tensor("ps_c", [128, 512], F32) as ps_c:
        dv_last = 0
        for j, (row, dst) in enumerate(((bvrow[:], k.bv_bc[:]),
                                        (borow[0:1, 0:512], k.bo_bc[:, 0:512]),
                                        (borow[0:1, 512:1024], k.bo_bc[:, 512:1024]))):
            def f_pe(t, row=row, j=j, dv_last=dv_last):
                t.wait_ge(k.s_cst.h, n_rows)
                if j > 0:
                    t.wait_ge(s_dv.h, dv_last)
                s_pe.inc(t.matmul(ps_c[:], k.onesr[:], row, start=True, stop=True))
            blk.tensor(f_pe)
            npe = s_pe.n

            def f_dv(v, dst=dst, npe=npe):
                v.wait_ge(s_pe.h, npe)
                s_dv.inc(v.tensor_copy(dst, ps_c[:]))
            blk.vector(f_dv)
            dv_last = s_dv.n

    def f_sg(s):
        s.wait_ge(k.s_cst.h, k.n_cst)
        s_ac.inc(s.activation(k.sxh[0:1, 6:7], k.sxh[0:1, 4:5], AF.Sigmoid))
        s.wait_ge(s_ac.h, s_ac.n)
        s_ac.inc(s.activation(k.sxh[0:1, 7:8], k.sxh[0:1, 5:6], AF.Sigmoid))
    blk.scalar(f_sg)

    k.p2xs = [en(nc.sbuf_tensor(f"p2xs{j}", [128, D], F32)) for j in range(2)]
    k.mv32 = en(nc.sbuf_tensor("mv32", [128, 2 * NT], F32))
    k.sd32 = en(nc.sbuf_tensor("sd32", [128, NT], F32))
    k.rs32 = en(nc.sbuf_tensor("rs32", [128, NT], F32))
    k.nm32 = en(nc.sbuf_tensor("nm32", [128, NT], F32))
    k.lnst2 = en(nc.sbuf_tensor("p2lnst", [128, 12], F32))

    with ExitStack() as r1:
        _p01(nc, k, r1.enter_context)
    barrier()
    with ExitStack() as r2o:
        k.Qfm = r2o.enter_context(nc.sbuf_tensor("Qfm", [128, 4 * L], BF16))
        k.Kfm = r2o.enter_context(nc.sbuf_tensor("Kfm", [128, 4 * L], BF16))
        k.Vtm = r2o.enter_context(nc.sbuf_tensor("Vtm", [128, NT * 520], BF16))
        with ExitStack() as r2:
            _p2(nc, k, r2.enter_context)
        barrier()
        with ExitStack() as r3:
            _p3(nc, k, r3.enter_context)
        barrier()
    with ExitStack() as r45:
        k.down_out = r45.enter_context(nc.sbuf_tensor("down_out", [128, NTH * D], F32))
        with ExitStack() as r4:
            _p4(nc, k, r4.enter_context)
        barrier()
        with ExitStack() as r5:
            _p5(nc, k, r5.enter_context)

    def f_end(e):
        if k.s_st0.n:
            e.wait_ge(k.s_st0.h, k.s_st0.n)
        if k.s_st1.n:
            e.wait_ge(k.s_st1.h, k.s_st1.n)
        if k.s_st.n:
            e.wait_ge(k.s_st.h, k.s_st.n)
    blk.sync(f_end)

    stk.close()
    return nc


def _p01(nc, k, en):
    blk, din = k.blk, k.din
    s_sy, s_gp, s_cc = k.s_sy, k.s_gp, k.s_cc
    s_pe, s_dv, s_ac = k.s_pe, k.s_dv, k.s_ac
    stat = k.stat

    k.ixq = en(nc.sbuf_tensor("ixq", [128, DC * TH], BF16))
    r_ps = ExitStack()
    ps_sm = r_ps.enter_context(nc.psum_tensor("ps_sm01", [128, 512], F32))
    k.ps_sm = ps_sm
    k.ps_sm_free = [0]

    wrow = [en(nc.sbuf_tensor(f"wrow{j}", [128, D], F32)) for j in range(2)]
    k.scri_g = en(nc.sbuf_tensor("scri_g", [128, IC], F32))
    k.scri_u = en(nc.sbuf_tensor("scri_u", [128, IC], F32))
    sc_ld, sc_dv = {}, {}

    def emit_scale_chunk(idx):
        wi, ic = divmod(idx, IC)
        w_dram = din["wg"] if wi == 0 else din["wu"]
        out_raw = k.scg if wi == 0 else k.scu
        j = idx % 2

        def f_l(e, idx=idx, ic=ic, j=j, w_dram=w_dram):
            if idx >= 2:
                e.wait_ge(s_dv.h, sc_dv[idx - 2])
            k.s_wst.inc(e.dma_start(out=wrow[j][:],
                                    in_=w_dram[ic * 128:(ic + 1) * 128, :]), 16)
            sc_ld[idx] = k.s_wst.n
        blk.sync(f_l)

        def f_m(v, idx=idx, ic=ic, j=j, out_raw=out_raw):
            v.wait_ge(k.s_wst.h, sc_ld[idx])
            s_dv.inc(v.reduce_max(out_raw[:, ic:ic + 1], wrow[j][:], axis=AX.X,
                                  apply_absolute_value=True))
            sc_dv[idx] = s_dv.n
        blk.vector(f_m)

    with ExitStack() as rA:
        enA = rA.enter_context
        xnmy = enA(nc.sbuf_tensor("xnmy", [128, NTH * D], F32))
        xs = [enA(nc.sbuf_tensor(f"p0xs{j}", [128, D], F32)) for j in range(2)]
        lnst = enA(nc.sbuf_tensor("p0lnst", [128, 24], F32))
        lnmv = enA(nc.sbuf_tensor("p0lnmv", [128, 4], F32))
        lnsc = enA(nc.sbuf_tensor("p0lnsc", [128, 8], F32))
        runmax = enA(nc.sbuf_tensor("runmax", [128, 1], F32))

        def f_i(v):
            s_dv.inc(v.memset(runmax[:], 0.0))
        blk.vector(f_i)

        sy_x = {}
        dv_st0 = {}
        dv_xn = {}
        sc_next = 0
        mv16 = enA(nc.sbuf_tensor("p0mv16", [128, 2 * NTH], F32))
        sd16 = enA(nc.sbuf_tensor("p0sd16", [128, NTH], F32))
        rs16 = enA(nc.sbuf_tensor("p0rs16", [128, NTH], F32))
        nm16 = enA(nc.sbuf_tensor("p0nm16", [128, NTH], F32))
        for i in range(NTH):
            j = i % 2

            def f_ld(e, i=i, j=j):
                if i >= 2:
                    e.wait_ge(s_dv.h, dv_st0[i - 2])
                sx = k.s_x0 if j == 0 else k.s_x1
                sx.inc(e.dma_start(out=xs[j][:], in_=din["x_my"][i * 128:(i + 1) * 128, :]), 16)
                sy_x[i] = (sx, sx.n)
            blk.sync(f_ld)

            def f_d1(v, i=i, j=j):
                v.wait_ge(sy_x[i][0].h, sy_x[i][1])
                s_dv.inc(v.bn_stats(lnst[:, 0:6], xs[j][:, 0:512]))
                s_dv.inc(v.bn_stats(lnst[:, 6:12], xs[j][:, 512:1024]))
                v.wait_ge(s_dv.h, s_dv.n)
                s_dv.inc(v.bn_aggr(mv16[:, 2 * i:2 * i + 2], lnst[:, 0:12]))
                dv_st0[i] = s_dv.n
            blk.vector(f_d1)

            for _sc in range(2):
                if sc_next < 2 * IC:
                    emit_scale_chunk(sc_next)
                    sc_next += 1
        while sc_next < 2 * IC:
            emit_scale_chunk(sc_next)
            sc_next += 1

        def f_v1(v):
            v.wait_ge(s_dv.h, dv_st0[NTH - 1])
            s_dv.inc(v.tensor_scalar(
                sd16.rearrange("p (one i) -> p one i", one=1),
                mv16.rearrange("p (i two) -> p two i", two=2)[:, 1:2, :],
                EPS, None, ALU.add))
        blk.vector(f_v1)
        n_v1 = s_dv.n

        def f_sq16(s):
            s.wait_ge(s_dv.h, n_v1)
            s_ac.inc(s.activation(sd16[:], sd16[:], AF.Sqrt))
        blk.scalar(f_sq16)
        n_sq16 = s_ac.n

        def f_v2(v):
            v.wait_ge(s_ac.h, n_sq16)
            s_dv.inc(v.reciprocal(rs16[:], sd16[:]))
            s_dv.inc(v.tensor_scalar(
                nm16.rearrange("p (one i) -> p one i", one=1),
                mv16.rearrange("p (i two) -> p two i", two=2)[:, 0:1, :],
                -1.0, None, ALU.mult))
        blk.vector(f_v2)
        n_lnf0 = s_dv.n

        sy_x2 = {}
        for i in range(NTH):
            j = i % 2
            tmax = lnsc[:, j * 4 + 3:j * 4 + 4]

            def f_ld2(e, i=i, j=j):
                if i < 2:
                    e.wait_ge(s_dv.h, n_lnf0)
                else:
                    e.wait_ge(s_dv.h, dv_xn[i - 2])
                sx = k.s_x0 if j == 0 else k.s_x1
                sx.inc(e.dma_start(out=xs[j][:], in_=din["x_my"][i * 128:(i + 1) * 128, :]), 16)
                sy_x2[i] = (sx, sx.n)
            blk.sync(f_ld2)

            def f_d3(v, i=i, j=j, tmax=tmax):
                v.wait_ge(sy_x2[i][0].h, sy_x2[i][1])
                s_dv.inc(v.tensor_scalar(xnmy[:, i * D:(i + 1) * D], xs[j][:],
                                         nm16[:, i:i + 1], rs16[:, i:i + 1],
                                         ALU.add, ALU.mult))
                v.wait_ge(s_dv.h, s_dv.n)
                s_dv.inc(v.reduce_max(tmax, xnmy[:, i * D:(i + 1) * D], axis=AX.X,
                                      apply_absolute_value=True))
                v.wait_ge(s_dv.h, s_dv.n)
                s_dv.inc(v.tensor_tensor(runmax[:], runmax[:], tmax, ALU.max))
                dv_xn[i] = s_dv.n
            blk.vector(f_d3)

        n_f = dv_xn[NTH - 1]

        def f_fp(t):
            t.wait_ge(s_dv.h, n_f)
            s_pe.inc(t.transpose(ps_sm[0:1, 0:128], runmax[:], k.identf[:]))
        blk.tensor(f_fp)
        n_fp = s_pe.n

        def f_f2(v):
            v.wait_ge(s_pe.h, n_fp)
            s_dv.inc(v.memset(k.mxs[:], 0.0))
            v.wait_ge(s_dv.h, s_dv.n)
            s_dv.inc(v.reduce_max(k.mxs[0:1, 0:1], ps_sm[0:1, 0:128], axis=AX.X))
        blk.vector(f_f2)
        n_mx = s_dv.n
        k.ps_sm_free[0] = n_mx

        def f_cc(g):
            g.wait_ge(s_dv.h, n_mx)
            k.s_mx.inc(g.dma_start(out=k.mxn_in.ap(), in_=k.mxs[:]), 16)
            g.wait_ge(k.s_mx.h, k.s_mx.n)
            s_cc.inc(g.collective_compute(
                "AllReduce", ALU.max, replica_groups=[list(range(8))],
                ins=[k.mxn_in.ap().opt()], outs=[k.mxn_out.ap().opt()]))
            g.wait_ge(s_cc.h, s_cc.n)
            k.s_mx.inc(g.dma_start(out=k.mxs[0:1, 0:1], in_=k.mxn_out.ap()[0:1, 0:1]), 16)
        blk.gpsimd(f_cc)
        n_ar1 = k.s_mx.n

        def f_sx(v):
            v.wait_ge(k.s_mx.h, n_ar1)
            v.wait_ge(s_ac.h, s_ac.n)
            s_dv.inc(v.tensor_tensor(k.sxh[0:1, 0:1], k.sxh[0:1, 6:7], k.mxs[0:1, 0:1], ALU.mult))
            v.wait_ge(s_dv.h, s_dv.n)
            s_dv.inc(v.tensor_scalar(k.sxh[0:1, 0:1], k.sxh[0:1, 0:1], 1.0 / QMAX, None, ALU.mult))
            v.wait_ge(s_dv.h, s_dv.n)
            s_dv.inc(v.reciprocal(k.sxh[0:1, 2:3], k.sxh[0:1, 0:1]))
        blk.vector(f_sx)
        bcast_scalar(k, blk, k.sxh[0:1, 2:3], k.rsx[:])

        with ExitStack() as rQ:
            enQ = rQ.enter_context
            qa = enQ(nc.sbuf_tensor("qa", [128, 2 * D], F32))
            qi = enQ(nc.sbuf_tensor("qi", [128, 2 * D], I32))
            qn = enQ(nc.sbuf_tensor("qn", [128, 2 * D], BF16))
            qq = enQ(nc.sbuf_tensor("qq", [128, 2 * D], BF16))
            ps_t1 = enQ(nc.psum_tensor("ps_t1", [128, 512], BF16))
            ps_t2 = enQ(nc.psum_tensor("ps_t2", [128, 512], BF16))
            n_rsx = s_dv.n
            dv_q2 = {}
            pe_tp = {}
            dv_ix = {}
            for i2 in range(NTH // 2):
                def f_q(v, i2=i2):
                    if i2 == 0:
                        v.wait_ge(s_dv.h, n_rsx)
                    if i2 >= 1:
                        v.wait_ge(s_pe.h, pe_tp[2 * i2 - 1])
                    s_dv.inc(v.tensor_scalar(qa[:], xnmy[:, 2 * i2 * D:(2 * i2 + 2) * D],
                                             k.rsx[:], QHI, ALU.mult, ALU.min))
                    v.wait_ge(s_dv.h, s_dv.n)
                    s_dv.inc(v.tensor_scalar(qa[:], qa[:], QLO, RC2C, ALU.max, ALU.add))
                    v.wait_ge(s_dv.h, s_dv.n)
                    s_dv.inc(v.tensor_scalar(qi[:], qa[:], RC2, EPS5, ALU.subtract, ALU.add))
                    v.wait_ge(s_dv.h, s_dv.n)
                    s_dv.inc(v.tensor_copy(qn[:], qi[:]))
                    v.wait_ge(s_dv.h, s_dv.n)
                    s_dv.inc(v.tensor_scalar(qq[:], qn[:], 129.0, None, ALU.subtract))
                    dv_q2[2 * i2] = dv_q2[2 * i2 + 1] = s_dv.n
                blk.vector(f_q)

                for i in (2 * i2, 2 * i2 + 1):
                    def f_tp(t, i=i):
                        t.wait_ge(s_dv.h, dv_q2[i])
                        if i >= 1:
                            t.wait_ge(s_dv.h, dv_ix[i - 1])
                        for dc in range(DC):
                            ps = ps_t1 if dc < 4 else ps_t2
                            s_pe.inc(t.transpose(
                                ps[:, (dc % 4) * 128:(dc % 4 + 1) * 128],
                                qq[:, (i % 2) * D + dc * 128:(i % 2) * D + (dc + 1) * 128],
                                k.identb[:]))
                        pe_tp[i] = s_pe.n
                    blk.tensor(f_tp)

                    def f_ev(v, i=i):
                        v.wait_ge(s_pe.h, pe_tp[i])
                        for half in range(2):
                            ps = ps_t1 if half == 0 else ps_t2
                            s_dv.inc(v.tensor_copy(
                                k.ixq.rearrange("p (dc t) -> p dc t", dc=DC)[
                                    :, half * 4:half * 4 + 4, i * 128:(i + 1) * 128],
                                ps.rearrange("p (c t) -> p c t", c=4)))
                        dv_ix[i] = s_dv.n
                    blk.vector(f_ev)
            k.n_ixq = dv_ix[NTH - 1]
            if k.debug:
                def f_dbg(e):
                    e.wait_ge(s_dv.h, k.n_ixq)
                    k.s_st.inc(e.dma_start(out=k.dbg["d_xnmy"], in_=xnmy[:, 0:D]), 16)
                    k.s_st.inc(e.dma_start(out=k.dbg["d_ixq"], in_=k.ixq[:, 0:512]), 16)
                    k.s_st.inc(e.dma_start(out=k.dbg["d_ixq7"], in_=k.ixq[:, 7 * TH:7 * TH + 512]), 16)
                blk.sync(f_dbg)
        k.barrier()

    def f_scq(v):
        v.wait_ge(s_dv.h, s_dv.n)
        s_dv.inc(v.tensor_scalar(k.scg[:], k.scg[:], 1.0 / QMAX, None, ALU.mult))
        s_dv.inc(v.tensor_scalar(k.scu[:], k.scu[:], 1.0 / QMAX, None, ALU.mult))
        v.wait_ge(s_dv.h, s_dv.n)
        s_dv.inc(v.reciprocal(k.scri_g[:], k.scg[:]))
        s_dv.inc(v.reciprocal(k.scri_u[:], k.scu[:]))
    blk.vector(f_scq)
    n_scq = s_dv.n

    sxv = en(nc.sbuf_tensor("sxv", [128, 1], F32))
    bcast_scalar(k, blk, k.sxh[0:1, 0:1], sxv[:])

    def f_cmb(v):
        v.wait_ge(s_dv.h, s_dv.n)
        s_dv.inc(v.tensor_scalar(k.scbg[:], k.scg[:], sxv[:], None, ALU.mult))
        v.wait_ge(s_dv.h, s_dv.n)
        s_dv.inc(v.tensor_scalar(k.scbu[:], k.scu[:], sxv[:], None, ALU.mult))
    blk.vector(f_cmb)
    n_cmb = s_dv.n

    r_ps.close()
    r_gu = ExitStack()
    en_gu = r_gu.enter_context
    ps_g = [en_gu(nc.psum_tensor(f"ps_g{j}", [128, 512], F32)) for j in range(2)]
    ps_u = [en_gu(nc.psum_tensor(f"ps_u{j}", [128, 512], F32)) for j in range(2)]
    ps_tw = [en_gu(nc.psum_tensor(f"ps_tw{j}", [128, 1024], BF16)) for j in range(4)]
    sg = [en(nc.sbuf_tensor(f"sg_s{j}", [128, 512], F32)) for j in range(2)]
    ht = [en(nc.sbuf_tensor(f"ht{j}", [128, 512], F32)) for j in range(2)]
    hmax = en(nc.sbuf_tensor("hmax", [128, 1], F32))
    wqld = [en_gu(nc.sbuf_tensor(f"wqld{j}", [128, D], F32)) for j in range(4)]
    qa_w = en_gu(nc.sbuf_tensor("qa_w", [128, 2 * D], F32))
    qi_w = en_gu(nc.sbuf_tensor("qi_w", [128, 2 * D], I32))
    qn_w = en_gu(nc.sbuf_tensor("qn_w", [128, 2 * D], BF16))
    qt = [en_gu(nc.sbuf_tensor(f"qt{j}", [128, 2 * D], BF16)) for j in range(2)]
    iwgr = [en_gu(nc.sbuf_tensor(f"iwgr{j}", [128, D], BF16)) for j in range(2)]
    iwur = [en_gu(nc.sbuf_tensor(f"iwur{j}", [128, D], BF16)) for j in range(2)]

    def f_hm(v):
        s_dv.inc(v.memset(hmax[:], 0.0))
    blk.vector(f_hm)
    n_hm0 = s_dv.n

    pe_gu, ac_si, dv_h, gp_h = {}, {}, {}, {}
    sy_wq, dv_ch, pe_tw, dv_twe = {}, {}, {}, {}

    def emit_quant(ic):
        j = ic % 2
        for wi, w_dram in enumerate((din["wg"], din["wu"])):
            slot = j * 2 + wi

            def f_wl(e, ic=ic, wi=wi, slot=slot, w_dram=w_dram):
                if ic >= 2:
                    e.wait_ge(s_dv.h, dv_ch[(ic - 2, 0)])
                k.s_wst.inc(e.dma_start(out=wqld[slot][:],
                                        in_=w_dram[ic * 128:(ic + 1) * 128, :]), 16)
                sy_wq[(ic, wi)] = k.s_wst.n
            blk.sync(f_wl)

        def f_ch(v, ic=ic, j=j):
            v.wait_ge(k.s_wst.h, sy_wq[(ic, 1)])
            if ic == 0:
                v.wait_ge(s_dv.h, n_scq)
            if ic >= 2:
                v.wait_ge(s_pe.h, pe_tw[ic - 2])
            s_dv.inc(v.tensor_scalar(qa_w[:, 0:D], wqld[j * 2][:],
                                     k.scri_g[:, ic:ic + 1], QHI, ALU.mult, ALU.min))
            s_dv.inc(v.tensor_scalar(qa_w[:, D:2 * D], wqld[j * 2 + 1][:],
                                     k.scri_u[:, ic:ic + 1], QHI, ALU.mult, ALU.min))
            v.wait_ge(s_dv.h, s_dv.n)
            s_dv.inc(v.tensor_scalar(qa_w[:], qa_w[:], QLO, RC2C, ALU.max, ALU.add))
            v.wait_ge(s_dv.h, s_dv.n)
            s_dv.inc(v.tensor_scalar(qi_w[:], qa_w[:], RC2, EPS5, ALU.subtract, ALU.add))
            v.wait_ge(s_dv.h, s_dv.n)
            s_dv.inc(v.tensor_copy(qn_w[:], qi_w[:]))
            v.wait_ge(s_dv.h, s_dv.n)
            s_dv.inc(v.tensor_scalar(qt[j][:], qn_w[:], 129.0, None, ALU.subtract))
            dv_ch[(ic, 0)] = dv_ch[(ic, 1)] = s_dv.n
        blk.vector(f_ch)

    def emit_quant_tw(ic):
        j = ic % 2

        def f_tw(t, ic=ic, j=j):
            t.wait_ge(s_dv.h, dv_ch[(ic, 0)])
            if ic >= 2:
                t.wait_ge(s_dv.h, dv_twe[ic - 2])
            for wi in range(2):
                for dc in range(DC):
                    s_pe.inc(t.transpose(ps_tw[j * 2 + wi][:, dc * 128:(dc + 1) * 128],
                                         qt[j][:, wi * D + dc * 128:wi * D + (dc + 1) * 128],
                                         k.identb[:]))
            pe_tw[ic] = s_pe.n
        blk.tensor(f_tw)

        def f_twe(v, ic=ic, j=j):
            v.wait_ge(s_pe.h, pe_tw[ic])
            if ic >= 2:
                v.wait_ge(s_pe.h, pe_gu[(ic - 2) * 4 + 3])
            s_dv.inc(v.tensor_copy(iwgr[j][:], ps_tw[j * 2][:]))
            s_dv.inc(v.tensor_copy(iwur[j][:], ps_tw[j * 2 + 1][:]))
            dv_twe[ic] = s_dv.n
        blk.vector(f_twe)

    sy_xf, dv_stf = {}, {}

    def emit_p2stat(i):
        j3 = i % 2

        def f_ldf(e, i=i, j3=j3):
            if i >= 2:
                e.wait_ge(s_dv.h, dv_stf[i - 2])
            sx = k.s_x0 if (i % 2) == 0 else k.s_x1
            sx.inc(e.dma_start(out=k.p2xs[j3][:],
                               in_=din["x_full"][i * 128:(i + 1) * 128, :]), 16)
            sy_xf[i] = (sx, sx.n)
        blk.sync(f_ldf)

        def f_stf(v, i=i, j3=j3):
            v.wait_ge(sy_xf[i][0].h, sy_xf[i][1])
            s_dv.inc(v.bn_stats(k.lnst2[:, 0:6], k.p2xs[j3][:, 0:512]))
            s_dv.inc(v.bn_stats(k.lnst2[:, 6:12], k.p2xs[j3][:, 512:1024]))
            v.wait_ge(s_dv.h, s_dv.n)
            s_dv.inc(v.bn_aggr(k.mv32[:, 2 * i:2 * i + 2], k.lnst2[:]))
            dv_stf[i] = s_dv.n
        blk.vector(f_stf)

    emit_quant(0)
    emit_quant_tw(0)
    emit_quant(1)
    emit_quant_tw(1)
    inst = 0
    p2s_next = 0
    for ic in range(IC):
        j = ic % 2
        if ic + 2 < IC:
            emit_quant(ic + 2)
        for _p2s in range(2):
            if p2s_next < NT:
                emit_p2stat(p2s_next)
                p2s_next += 1
        for tc in range(4):
            pj = inst % 2
            bsg = ps_g[pj]
            bsu = ps_u[pj]

            def f_mm(t, ic=ic, tc=tc, j=j, inst=inst, bsg=bsg, bsu=bsu):
                if inst == 0:
                    t.wait_ge(s_dv.h, max(n_cmb, k.n_ixq))
                if tc == 0:
                    t.wait_ge(s_dv.h, dv_twe[ic])
                if inst >= 2:
                    t.wait_ge(s_ac.h, ac_si[inst - 2])
                    t.wait_ge(s_dv.h, dv_h[inst - 2])
                for dc in range(DC):
                    t.matmul(bsg[:],
                             iwgr[j][:, dc * 128:(dc + 1) * 128],
                             k.ixq[:, dc * TH + tc * 512:dc * TH + (tc + 1) * 512],
                             start=(dc == 0), stop=(dc == DC - 1))
                for dc in range(DC):
                    mm = t.matmul(bsu[:],
                                  iwur[j][:, dc * 128:(dc + 1) * 128],
                                  k.ixq[:, dc * TH + tc * 512:dc * TH + (tc + 1) * 512],
                                  start=(dc == 0), stop=(dc == DC - 1))
                s_pe.inc(mm)
                pe_gu[inst] = s_pe.n
            blk.tensor(f_mm)

            key = inst
            jj = key % 2

            def f_si(s, ic=ic, inst=inst, key=key, jj=jj, bsg=bsg):
                s.wait_ge(s_pe.h, pe_gu[inst])
                if key < 2:
                    s.wait_ge(s_dv.h, n_cmb)
                if key >= 2:
                    s.wait_ge(s_dv.h, dv_h[key - 2])
                s_ac.inc(s.activation(sg[jj][:], bsg[:], AF.Silu,
                                      scale=k.scbg[:, ic:ic + 1]))
                ac_si[key] = s_ac.n
            blk.scalar(f_si)

            def f_h(v, ic=ic, key=key, jj=jj, bsu=bsu):
                v.wait_ge(s_ac.h, ac_si[key])
                if key >= 2:
                    v.wait_ge(gp_h[key - 2][0].h, gp_h[key - 2][1])
                if key == 0:
                    v.wait_ge(s_dv.h, n_hm0)
                s_dv.inc(v.scalar_tensor_tensor(ht[jj][:], bsu[:],
                                                k.scbu[:, ic:ic + 1], sg[jj][:],
                                                ALU.mult, ALU.mult))
                v.wait_ge(s_dv.h, s_dv.n)
                s_dv.inc(v.reduce_max(stat[:, 4:5], ht[jj][:], axis=AX.X,
                                      apply_absolute_value=True))
                v.wait_ge(s_dv.h, s_dv.n)
                s_dv.inc(v.tensor_tensor(hmax[:], hmax[:], stat[:, 4:5], ALU.max))
                dv_h[key] = s_dv.n
            blk.vector(f_h)

            def f_sp(gp, ic=ic, tc=tc, key=key, jj=jj):
                gp.wait_ge(s_dv.h, dv_h[key])
                sp = k.s_sp0 if jj == 0 else k.s_sp1
                sp.inc(gp.dma_start(
                    out=k.h_spill.ap()[ic * 128:(ic + 1) * 128, tc * 512:(tc + 1) * 512],
                    in_=ht[jj][:]), 16)
                gp_h[key] = (sp, sp.n)
            blk.gpsimd(f_sp)

            inst += 1
        if ic + 2 < IC:
            emit_quant_tw(ic + 2)

    def f_v1f(v):
        v.wait_ge(s_dv.h, dv_stf[NT - 1])
        s_dv.inc(v.tensor_scalar(
            k.sd32.rearrange("p (one i) -> p one i", one=1),
            k.mv32.rearrange("p (i two) -> p two i", two=2)[:, 1:2, :],
            EPS, None, ALU.add))
    blk.vector(f_v1f)
    n_v1f = s_dv.n

    def f_sq32(s):
        s.wait_ge(s_dv.h, n_v1f)
        s_ac.inc(s.activation(k.sd32[:], k.sd32[:], AF.Sqrt))
    blk.scalar(f_sq32)
    n_sq32 = s_ac.n

    def f_v2f(v):
        v.wait_ge(s_ac.h, n_sq32)
        s_dv.inc(v.reciprocal(k.rs32[:], k.sd32[:]))
        s_dv.inc(v.tensor_scalar(
            k.nm32.rearrange("p (one i) -> p one i", one=1),
            k.mv32.rearrange("p (i two) -> p two i", two=2)[:, 0:1, :],
            -1.0, None, ALU.mult))
    blk.vector(f_v2f)
    k.n_lnf2 = s_dv.n

    r_gu.close()
    ps_sm = en(nc.psum_tensor("ps_sm01b", [128, 512], F32))
    k.ps_sm = ps_sm
    n_hf = dv_h[inst - 1]

    def f_hfp(t):
        t.wait_ge(s_dv.h, max(n_hf, k.ps_sm_free[0]))
        s_pe.inc(t.transpose(ps_sm[0:1, 0:128], hmax[:], k.identf[:]))
    blk.tensor(f_hfp)
    n_hfp = s_pe.n

    def f_hf2(v):
        v.wait_ge(s_pe.h, n_hfp)
        s_dv.inc(v.memset(k.mxs[:], 0.0))
        v.wait_ge(s_dv.h, s_dv.n)
        s_dv.inc(v.reduce_max(k.mxs[0:1, 0:1], ps_sm[0:1, 0:128], axis=AX.X))
    blk.vector(f_hf2)
    n_hmx = s_dv.n
    k.ps_sm_free[0] = n_hmx

    def f_hcc(g):
        g.wait_ge(s_dv.h, n_hmx)
        k.s_mx.inc(g.dma_start(out=k.mxh_in.ap(), in_=k.mxs[:]), 16)
        g.wait_ge(k.s_mx.h, k.s_mx.n)
        s_cc.inc(g.collective_compute(
            "AllReduce", ALU.max, replica_groups=[list(range(8))],
            ins=[k.mxh_in.ap().opt()], outs=[k.mxh_out.ap().opt()]))
        g.wait_ge(s_cc.h, s_cc.n)
        k.s_mx.inc(g.dma_start(out=k.mxs[0:1, 1:2], in_=k.mxh_out.ap()[0:1, 0:1]), 16)
    blk.gpsimd(f_hcc)
    n_ar2 = k.s_mx.n

    def f_sh(v):
        v.wait_ge(k.s_mx.h, n_ar2)
        s_dv.inc(v.tensor_tensor(k.sxh[0:1, 1:2], k.sxh[0:1, 7:8], k.mxs[0:1, 1:2], ALU.mult))
        v.wait_ge(s_dv.h, s_dv.n)
        s_dv.inc(v.tensor_scalar(k.sxh[0:1, 1:2], k.sxh[0:1, 1:2], 1.0 / QMAX, None, ALU.mult))
        v.wait_ge(s_dv.h, s_dv.n)
        s_dv.inc(v.reciprocal(k.sxh[0:1, 3:4], k.sxh[0:1, 1:2]))
    blk.vector(f_sh)
    bcast_scalar(k, blk, k.sxh[0:1, 3:4], k.rsh[:])
    k.n_rsh = s_dv.n
    k.n_sp0, k.n_sp1 = k.s_sp0.n, k.s_sp1.n
    k.ps_sm = None


def _p2(nc, k, en):
    blk, din = k.blk, k.din
    s_sy, s_pe, s_dv, s_ac = k.s_sy, k.s_pe, k.s_dv, k.s_ac
    stat = k.stat
    Qfm, Kfm, Vtm = k.Qfm, k.Kfm, k.Vtm

    xnfm = en(nc.sbuf_tensor("xnfm", [128, DC * L], BF16))
    ps_sm = en(nc.psum_tensor("ps_sm2", [128, 512], F32))
    k.ps_sm = ps_sm
    k.ps_sm_free = [0]
    wbf = en(nc.sbuf_tensor("wbf", [128, DC * 512], BF16))
    wst = [en(nc.sbuf_tensor(f"wst{j}", [128, 512], F32)) for j in range(2)]
    ps_q = [en(nc.psum_tensor(f"ps_q{j}", [128, 512], F32)) for j in range(2)]
    xm = en(nc.sbuf_tensor("xm_t", [128, DC], F32))

    rLN = ExitStack()
    enL = rLN.enter_context
    xs = k.p2xs
    rs32, nm32 = k.rs32, k.nm32
    n_lnf = k.n_lnf2
    xb = [enL(nc.sbuf_tensor(f"p2xb{j}", [128, D], BF16)) for j in range(2)]
    ps_t1 = enL(nc.psum_tensor("p2ps_t1", [128, 512], BF16))
    ps_t2 = enL(nc.psum_tensor("p2ps_t2", [128, 512], BF16))

    def load_w_bf(wT, n_prev_done):
        sy_w, dv_w = {}, {}
        for dc in range(DC):
            jj = dc % 2

            def f_l(e, dc=dc, jj=jj):
                if dc == 0:
                    e.wait_ge(s_pe.h, n_prev_done)
                if dc >= 2:
                    e.wait_ge(s_dv.h, dv_w[dc - 2])
                sw = k.s_w0 if jj == 0 else k.s_w1
                sw.inc(e.dma_start(out=wst[jj][:], in_=wT[dc * 128:(dc + 1) * 128, :]), 16)
                sy_w[dc] = (sw, sw.n)
            blk.sync(f_l)

            def f_c(v, dc=dc, jj=jj):
                v.wait_ge(sy_w[dc][0].h, sy_w[dc][1])
                s_dv.inc(v.tensor_copy(wbf[:, dc * 512:(dc + 1) * 512], wst[jj][:]))
                dv_w[dc] = s_dv.n
            blk.vector(f_c)
        return dv_w[DC - 1]

    n_wk = load_w_bf(din["wkT"], 0)
    sy_x2, dv_xb, pe_tp, dv_fm = {}, {}, {}, {}

    def emit_ln2(i):
        j = i % 2

        def f_ld2(e, i=i, j=j):
            if i < 2:
                e.wait_ge(s_dv.h, n_lnf)
            else:
                e.wait_ge(s_dv.h, dv_xb[i - 2])
            sx = k.s_x0 if j == 0 else k.s_x1
            sx.inc(e.dma_start(out=xs[j][:], in_=din["x_full"][i * 128:(i + 1) * 128, :]), 16)
            sy_x2[i] = (sx, sx.n)
        blk.sync(f_ld2)

        def f_n(v, i=i, j=j):
            v.wait_ge(sy_x2[i][0].h, sy_x2[i][1])
            if i < 2:
                v.wait_ge(s_dv.h, n_lnf)
            if i >= 2:
                v.wait_ge(s_pe.h, pe_tp[i - 2])
            s_dv.inc(v.tensor_scalar(xb[j][:], xs[j][:], nm32[:, i:i + 1],
                                     rs32[:, i:i + 1], ALU.add, ALU.mult))
            dv_xb[i] = s_dv.n
        blk.vector(f_n)

        def f_tp(t, i=i, j=j):
            t.wait_ge(s_dv.h, dv_xb[i])
            if i >= 1:
                t.wait_ge(s_dv.h, dv_fm[i - 1])
            for dc in range(DC):
                ps = ps_t1 if dc < 4 else ps_t2
                s_pe.inc(t.transpose(ps[:, (dc % 4) * 128:(dc % 4 + 1) * 128],
                                     xb[j][:, dc * 128:(dc + 1) * 128], k.identb[:]))
            pe_tp[i] = s_pe.n
        blk.tensor(f_tp)

        def f_ev(v, i=i):
            v.wait_ge(s_pe.h, pe_tp[i])
            for half in range(2):
                ps = ps_t1 if half == 0 else ps_t2
                s_dv.inc(v.tensor_copy(
                    xnfm.rearrange("p (dc t) -> p dc t", dc=DC)[:, half * 4:half * 4 + 4,
                                                                i * 128:(i + 1) * 128],
                    ps.rearrange("p (c t) -> p c t", c=4)))
            dv_fm[i] = s_dv.n
        blk.vector(f_ev)

    for i in range(8):
        emit_ln2(i)
    kk_pe, kk_ac = {}, {}
    ii = 0
    for tcc in range(8):
        if tcc < 6:
            for i in range(4 * tcc + 8, 4 * tcc + 12):
                emit_ln2(i)
        for oc in range(4):
            j = ii % 2

            def f_mmk(t, tcc=tcc, oc=oc, j=j, ii=ii):
                if ii == 0:
                    t.wait_ge(s_dv.h, n_wk)
                if oc == 0:
                    t.wait_ge(s_dv.h, dv_fm[4 * tcc + 3])
                if ii >= 2:
                    t.wait_ge(s_ac.h, kk_ac[ii - 2])
                for dc in range(DC):
                    mm = t.matmul(ps_q[j][:],
                                  wbf[:, dc * 512 + oc * 128:dc * 512 + (oc + 1) * 128],
                                  xnfm[:, dc * L + tcc * 512:dc * L + (tcc + 1) * 512],
                                  start=(dc == 0), stop=(dc == DC - 1))
                s_pe.inc(mm)
                kk_pe[ii] = s_pe.n
            blk.tensor(f_mmk)

            def f_evk(s, tcc=tcc, oc=oc, j=j, ii=ii):
                s.wait_ge(s_pe.h, kk_pe[ii])
                s_ac.inc(s.activation(Kfm[:, oc * L + tcc * 512:oc * L + (tcc + 1) * 512],
                                      ps_q[j][:], AF.Identity, bias=k.bk[:, oc:oc + 1]))
                kk_ac[ii] = s_ac.n
            blk.scalar(f_evk)
            ii += 1
    pe_k, ac_k = kk_pe[31], kk_ac[31]
    n_xnfm = dv_fm[NT - 1]

    def f_vm(v):
        s_dv.inc(v.memset(Vtm[:], 1.0))
    blk.vector(f_vm)
    n_vm = s_dv.n
    n_w = load_w_bf(din["wvT"], pe_k)
    pe_v, dv_v = {}, {}
    for tt in range(NT):
        j = tt % 2

        def f_vmm(t, tt=tt, j=j):
            if tt < 2:
                t.wait_ge(s_dv.h, max(n_w, n_vm))
                t.wait_ge(s_ac.h, ac_k)
            if tt >= 2:
                t.wait_ge(s_dv.h, dv_v[tt - 2])
            for dc in range(DC):
                mm = t.matmul(ps_q[j][:], xnfm[:, dc * L + tt * 128:dc * L + (tt + 1) * 128],
                              wbf[:, dc * 512:(dc + 1) * 512],
                              start=(dc == 0), stop=(dc == DC - 1))
            s_pe.inc(mm)
            pe_v[tt] = s_pe.n
        blk.tensor(f_vmm)

        def f_vev(v, tt=tt, j=j):
            v.wait_ge(s_pe.h, pe_v[tt])
            s_dv.inc(v.scalar_tensor_tensor(
                Vtm[:, tt * 520:(tt + 1) * 520].rearrange("p (h c) -> p h c", c=65)[:, :, 0:64],
                ps_q[j].rearrange("p (h c) -> p h c", c=64),
                1.0,
                k.bv_bc.rearrange("p (h c) -> p h c", c=64),
                ALU.mult, ALU.add))
            dv_v[tt] = s_dv.n
        blk.vector(f_vev)
    k.n_qkv = dv_v[NT - 1]
    rLN.close()

    def f_xm(v):
        v.wait_ge(s_dv.h, n_xnfm)
        for dc in range(DC):
            s_dv.inc(v.reduce_sum(xm[:, dc:dc + 1], xnfm[:, dc * L:(dc + 1) * L], axis=AX.X))
        v.wait_ge(s_dv.h, s_dv.n)
        s_dv.inc(v.tensor_scalar(xm[:], xm[:], 1.0 / L, None, ALU.mult))
    blk.vector(f_xm)
    n_xm = s_dv.n

    rG = ExitStack()
    _gating(nc, k, rG.enter_context, xm, n_xm)
    rG.close()

    gq_t = en(nc.sbuf_tensor("gq_t", [128, 4], F32))
    bqg_t = en(nc.sbuf_tensor("bqg_t", [128, 4], F32))
    n_gq = _myhg_sel(nc, k, en)
    pe_gq = {}
    for oc in range(4):
        for half in range(2):
            def f_gqm(t, oc=oc, half=half, n_gq=n_gq):
                if oc == 0 and half == 0:
                    t.wait_ge(s_dv.h, max(n_gq, k.ps_sm_free[0]))
                s_pe.inc(t.matmul(ps_sm[half * 64:half * 64 + 64, 32 + oc:33 + oc],
                                  k.onesr[0:1, 0:64],
                                  k.myhg[0:1, oc * 2 + half:oc * 2 + half + 1],
                                  start=True, stop=True))
                pe_gq[(oc, half)] = s_pe.n
            blk.tensor(f_gqm)

    def f_gqe(v):
        v.wait_ge(s_pe.h, pe_gq[(3, 1)])
        s_dv.inc(v.tensor_copy(gq_t[:], ps_sm[:, 32:36]))
        v.wait_ge(s_dv.h, s_dv.n)
        s_dv.inc(v.tensor_tensor(bqg_t[:], k.bq[:], gq_t[:], ALU.mult))
    blk.vector(f_gqe)
    n_gqe = s_dv.n
    k.ps_sm_free[0] = max(k.ps_sm_free[0], n_gqe)

    n_w = load_w_bf(din["wqT"], pe_v[NT - 1])
    pe_p, ac_e = {}, {}
    ii = 0
    for oc in range(4):
        for tcc in range(8):
            j = ii % 2

            def f_mm(t, oc=oc, tcc=tcc, j=j, ii=ii, n_w=n_w):
                if ii == 0:
                    t.wait_ge(s_dv.h, max(n_w, n_xnfm))
                if ii < 2:
                    t.wait_ge(s_dv.h, k.n_qkv)
                if ii >= 2:
                    t.wait_ge(s_ac.h, ac_e[ii - 2])
                for dc in range(DC):
                    mm = t.matmul(ps_q[j][:],
                                  wbf[:, dc * 512 + oc * 128:dc * 512 + (oc + 1) * 128],
                                  xnfm[:, dc * L + tcc * 512:dc * L + (tcc + 1) * 512],
                                  start=(dc == 0), stop=(dc == DC - 1))
                s_pe.inc(mm)
                pe_p[ii] = s_pe.n
            blk.tensor(f_mm)

            def f_ev(s, oc=oc, tcc=tcc, j=j, ii=ii):
                s.wait_ge(s_pe.h, pe_p[ii])
                if ii == 0:
                    s.wait_ge(s_dv.h, n_gqe)
                s_ac.inc(s.activation(Qfm[:, oc * L + tcc * 512:oc * L + (tcc + 1) * 512],
                                      ps_q[j][:], AF.Identity,
                                      bias=bqg_t[:, oc:oc + 1], scale=gq_t[:, oc:oc + 1]))
                ac_e[ii] = s_ac.n
            blk.scalar(f_ev)
            ii += 1

    _bh(nc, k, en)

    if k.debug:
        def f_dbg(e):
            e.wait_ge(s_dv.h, s_dv.n)
            e.wait_ge(s_ac.h, s_ac.n)
            k.s_st.inc(e.dma_start(out=k.dbg["d_xnfm"], in_=xnfm[:, 0:512]), 16)
            k.s_st.inc(e.dma_start(out=k.dbg["d_q"], in_=Qfm[:, 0:256]), 16)
            k.s_st.inc(e.dma_start(out=k.dbg["d_k"], in_=Kfm[:, 0:256]), 16)
            k.s_st.inc(e.dma_start(out=k.dbg["d_v"], in_=Vtm[:, 0:520]), 16)
            k.s_st.inc(e.dma_start(out=k.dbg["d_hg"], in_=k.hg[:]), 16)
            k.s_st.inc(e.dma_start(out=k.dbg["d_nB"][0:1, 0:8],
                                   in_=k.negB[0:1, 0:8]), 16)
            k.s_st.inc(e.dma_start(out=k.dbg["d_nB"][0:1, 8:16],
                                   in_=k.gk[0:1, 0:8]), 16)
        blk.sync(f_dbg)
    k.ps_sm = None


def _gating(nc, k, en, xm, n_xm):
    blk, din = k.blk, k.din
    s_sy, s_pe, s_dv, s_ac = k.s_sy, k.s_pe, k.s_dv, k.s_ac
    ps_sm = k.ps_sm
    gsc = k.gsc

    w1 = [en(nc.sbuf_tensor(f"gw1{j}", [128, DH], F32)) for j in range(2)]
    brow = en(nc.sbuf_tensor("gbrow", [1, DH], F32))
    arow = en(nc.sbuf_tensor("garow", [1, DH], F32))
    act1 = en(nc.sbuf_tensor("gact1", [128, 4], F32))
    w2 = en(nc.sbuf_tensor("gw2", [128, 16], F32))
    bsm = en(nc.sbuf_tensor("gbsm", [16, 1], F32))
    pol = en(nc.sbuf_tensor("gpol", [1, 4], F32))

    def mlp_in(w_dram, b_dram_flat, wait_dv):
        def f_b(e):
            e.wait_ge(k.s_gb.h, k.s_gb.n)
            k.s_gb.inc(e.dma_start(out=brow[:],
                                 in_=b_dram_flat.rearrange("(i f) -> i f", i=1)), 16)
        blk.sync(f_b)
        n_b = k.s_gb.n
        sy_w, pe_c = {}, {}
        for dc in range(DC):
            jj = dc % 2

            def f_l(e, dc=dc, jj=jj):
                if dc >= 2:
                    e.wait_ge(s_pe.h, pe_c[dc - 2])
                gw = k.s_gw0 if jj == 0 else k.s_gw1
                gw.inc(e.dma_start(out=w1[jj][:], in_=w_dram[dc * 128:(dc + 1) * 128, :]), 16)
                sy_w[dc] = (gw, gw.n)
            blk.sync(f_l)

            def f_m(t, dc=dc, jj=jj):
                t.wait_ge(sy_w[dc][0].h, sy_w[dc][1])
                if dc == 0:
                    t.wait_ge(s_dv.h, max(wait_dv, k.ps_sm_free[0]))
                mm = t.matmul(ps_sm[0:1, 0:512], xm[:, dc:dc + 1], w1[jj][:],
                              start=(dc == 0), stop=(dc == DC - 1))
                s_pe.inc(mm)
                pe_c[dc] = s_pe.n
            blk.tensor(f_m)

        def f_e(v):
            v.wait_ge(s_pe.h, pe_c[DC - 1])
            v.wait_ge(k.s_gb.h, n_b)
            s_dv.inc(v.tensor_tensor(arow[:], ps_sm[0:1, 0:512], brow[:], ALU.add))
        blk.vector(f_e)
        n_row = s_dv.n
        def f_t(t):
            t.wait_ge(s_dv.h, n_row)
            for cc in range(4):
                mm = t.transpose(ps_sm[0:128, 4 + cc:5 + cc],
                                 arow[0:1, cc * 128:(cc + 1) * 128], k.identf[0:1, 0:1])
            s_pe.inc(mm)
        blk.tensor(f_t)
        n_tp = s_pe.n

        def f_e2(v):
            v.wait_ge(s_pe.h, n_tp)
            s_dv.inc(v.tensor_copy(act1[:], ps_sm[:, 4:8]))
        blk.vector(f_e2)
        k.ps_sm_free[0] = s_dv.n
        return s_dv.n

    def softmax_row(src_ap, n_el, out_ap, wait_pe):
        def f1(v):
            v.wait_ge(s_pe.h, wait_pe)
            s_dv.inc(v.tensor_copy(gsc[0:1, 0:n_el], src_ap))
            v.wait_ge(s_dv.h, s_dv.n)
            s_dv.inc(v.reduce_max(gsc[0:1, 32:33], gsc[0:1, 0:n_el], axis=AX.X, negate=True))
        blk.vector(f1)
        n1 = s_dv.n
        k.ps_sm_free[0] = max(k.ps_sm_free[0], n1)

        def f2(s):
            s.wait_ge(s_dv.h, n1)
            s_ac.inc(s.activation(gsc[0:1, 0:n_el], gsc[0:1, 0:n_el], AF.Exp,
                                  bias=gsc[0:1, 32:33]))
        blk.scalar(f2)
        n2 = s_ac.n

        def f3(v):
            v.wait_ge(s_ac.h, n2)
            s_dv.inc(v.reduce_sum(gsc[0:1, 33:34], gsc[0:1, 0:n_el], axis=AX.X))
            v.wait_ge(s_dv.h, s_dv.n)
            s_dv.inc(v.reciprocal(gsc[0:1, 34:35], gsc[0:1, 33:34]))
            v.wait_ge(s_dv.h, s_dv.n)
            s_dv.inc(v.tensor_scalar(out_ap, gsc[0:1, 0:n_el], gsc[0:1, 34:35], None, ALU.mult))
        blk.vector(f3)
        return s_dv.n

    def logits16(w2T_dram, b_dram, nb_rows, src_cols, wait_ac, psl):
        sy_w, pe_c = {}, {}
        for cc in range(4):
            def f_l(e, cc=cc):
                if cc >= 1:
                    e.wait_ge(s_pe.h, pe_c[cc - 1])
                else:
                    e.wait_ge(k.s_gl.h, k.s_gl.n)
                k.s_gl.inc(e.dma_start(out=w2[:, 0:nb_rows],
                                     in_=w2T_dram[cc * 128:(cc + 1) * 128, :]), 16)
                sy_w[cc] = k.s_gl.n
            blk.sync(f_l)

            def f_m(t, cc=cc):
                t.wait_ge(k.s_gl.h, sy_w[cc])
                t.wait_ge(s_ac.h, wait_ac)
                s_pe.inc(t.matmul(psl, w2[:, 0:nb_rows], src_cols[:, cc:cc + 1],
                                  start=(cc == 0), stop=(cc == 3)))
                pe_c[cc] = s_pe.n
            blk.tensor(f_m)

        def f_b(e):
            e.wait_ge(k.s_gb.h, k.s_gb.n)
            k.s_gb.inc(e.dma_start(out=bsm[0:nb_rows, :], in_=b_dram), 16)
        blk.sync(f_b)
        n_b = k.s_gb.n

        def f_e(v):
            v.wait_ge(s_pe.h, pe_c[3])
            v.wait_ge(k.s_gb.h, n_b)
            s_dv.inc(v.tensor_tensor(bsm[0:nb_rows, :], psl, bsm[0:nb_rows, :], ALU.add))
        blk.vector(f_e)
        return s_dv.n

    n1 = mlp_in(din["sw1T"], din["sb1"], n_xm)

    def f_g(s):
        s.wait_ge(s_dv.h, n1)
        s_ac.inc(s.activation(act1[:], act1[:], AF.Gelu))
    blk.scalar(f_g)
    n_g = s_ac.n
    n_lg = logits16(din["sw2T"], din["sb2"], 3, act1, n_g, ps_sm[0:3, 8:9])

    def f_t3(t):
        t.wait_ge(s_dv.h, n_lg)
        s_pe.inc(t.transpose(ps_sm[0:1, 12:15], bsm[0:3, :], k.identf[0:3, 0:3]))
    blk.tensor(f_t3)
    n_t3 = s_pe.n
    n_pol = softmax_row(ps_sm[0:1, 12:15], 3, pol[0:1, 0:3], n_t3)
    k.ps_sm_free[0] = max(k.ps_sm_free[0], n_pol)

    last = n_pol
    for p in range(P3):
        np1 = mlp_in(din["pw1T"][p * D:(p + 1) * D, :],
                     din["pb1"][p * DH:(p + 1) * DH], max(last, n_xm))

        def f_gp(s, np1=np1):
            s.wait_ge(s_dv.h, np1)
            s_ac.inc(s.activation(act1[:], act1[:], AF.Gelu))
        blk.scalar(f_gp)
        n_gp = s_ac.n
        n_gl = logits16(din["pw2T"][p * DH:(p + 1) * DH, :],
                        din["pb2"][p * H:(p + 1) * H].rearrange("(p i) -> p i", p=16),
                        16, act1, n_gp, ps_sm[0:16, 8:9])

        def f_th(t, n_gl=n_gl):
            t.wait_ge(s_dv.h, n_gl)
            s_pe.inc(t.transpose(ps_sm[0:1, 16:32], bsm[0:16, :], k.identf[0:16, 0:16]))
        blk.tensor(f_th)
        n_th = s_pe.n
        n_gt = softmax_row(ps_sm[0:1, 16:32], 16, gsc[0:1, 40:56], n_th)

        def f_acc(v, p=p, n_gt=n_gt):
            v.wait_ge(s_dv.h, n_gt)
            if p == 0:
                s_dv.inc(v.tensor_scalar(k.hg[:], gsc[0:1, 40:56], pol[0:1, p:p + 1], None, ALU.mult))
            else:
                s_dv.inc(v.tensor_scalar(gsc[0:1, 16:32], gsc[0:1, 40:56], pol[0:1, p:p + 1],
                                         None, ALU.mult))
                v.wait_ge(s_dv.h, s_dv.n)
                s_dv.inc(v.tensor_tensor(k.hg[:], k.hg[:], gsc[0:1, 16:32], ALU.add))
        blk.vector(f_acc)
        last = s_dv.n
    k.n_hg = last
    k.ps_sm_free[0] = max(k.ps_sm_free[0], last)


def _myhg_sel(nc, k, en):
    blk, din = k.blk, k.din
    s_sy, s_pe, s_dv = k.s_sy, k.s_pe, k.s_dv
    ps_sm, stat = k.ps_sm, k.stat
    hsel = k.hsel
    k.myhg = en(nc.sbuf_tensor("bh_myhg", [1, 8], F32))
    n_hs = k.n_cst

    def f_hgt(t):
        t.wait_ge(s_dv.h, max(k.n_hg, k.ps_sm_free[0]))
        s_pe.inc(t.transpose(ps_sm[0:16, 0:1], k.hg[:], k.identf[0:1, 0:1]))
    blk.tensor(f_hgt)
    n_hgt = s_pe.n

    def f_e1(v):
        v.wait_ge(s_pe.h, n_hgt)
        s_dv.inc(v.tensor_copy(stat[0:16, 6:7], ps_sm[0:16, 0:1]))
    blk.vector(f_e1)
    n_e1 = s_dv.n

    def f_sel(t):
        t.wait_ge(s_dv.h, n_e1)
        t.wait_ge(k.s_cst.h, n_hs)
        s_pe.inc(t.matmul(ps_sm[0:8, 2:3], hsel[:], stat[0:16, 6:7], start=True, stop=True))
    blk.tensor(f_sel)
    n_sel = s_pe.n

    def f_e2(v):
        v.wait_ge(s_pe.h, n_sel)
        s_dv.inc(v.tensor_copy(stat[0:8, 7:8], ps_sm[0:8, 2:3]))
    blk.vector(f_e2)
    n_e2 = s_dv.n

    def f_t8(t):
        t.wait_ge(s_dv.h, n_e2)
        s_pe.inc(t.transpose(ps_sm[0:1, 4:12], stat[0:8, 7:8], k.identf[0:8, 0:8]))
    blk.tensor(f_t8)
    n_t8 = s_pe.n

    def f_e3(v):
        v.wait_ge(s_pe.h, n_t8)
        s_dv.inc(v.tensor_scalar(k.myhg[:], ps_sm[0:1, 4:12], 0.125, None, ALU.mult))
    blk.vector(f_e3)
    k.ps_sm_free[0] = max(k.ps_sm_free[0], s_dv.n)
    return s_dv.n


def _bh(nc, k, en):
    blk = k.blk
    s_pe, s_dv, s_ac = k.s_pe, k.s_dv, k.s_ac
    ps_sm = k.ps_sm
    mxq = en(nc.sbuf_tensor("bh_mxq", [128, 4], F32))
    mxk = en(nc.sbuf_tensor("bh_mxk", [128, 4], F32))
    nrm = en(nc.sbuf_tensor("bh_nrm", [4, 256], F32))
    n2 = en(nc.sbuf_tensor("bh_n2", [4, 8], F32))
    brow = en(nc.sbuf_tensor("bh_brow", [1, 4], F32))

    def f_rm(v):
        v.wait_ge(s_dv.h, k.n_qkv)
        v.wait_ge(s_ac.h, s_ac.n)
        for ch in range(4):
            s_dv.inc(v.reduce_max(mxq[:, ch:ch + 1], k.Qfm[:, ch * L:(ch + 1) * L],
                                  axis=AX.X, apply_absolute_value=True))
            s_dv.inc(v.reduce_max(mxk[:, ch:ch + 1], k.Kfm[:, ch * L:(ch + 1) * L],
                                  axis=AX.X, apply_absolute_value=True))
    blk.vector(f_rm)
    n_rm = s_dv.n

    def f_tq(t):
        t.wait_ge(s_dv.h, max(n_rm, k.ps_sm_free[0]))
        s_pe.inc(t.transpose(ps_sm[0:4, 0:128], mxq[:, 0:4], k.identf[:]))
        s_pe.inc(t.transpose(ps_sm[0:4, 128:256], mxk[:, 0:4], k.identf[:]))
    blk.tensor(f_tq)
    n_tq = s_pe.n

    def f_ev(v):
        v.wait_ge(s_pe.h, n_tq)
        s_dv.inc(v.tensor_copy(nrm[:], ps_sm[0:4, 0:256]))
        v.wait_ge(s_dv.h, s_dv.n)
        s_dv.inc(v.tensor_tensor(nrm[:], nrm[:], nrm[:], ALU.mult))
        v.wait_ge(s_dv.h, s_dv.n)
        for q4 in range(4):
            s_dv.inc(v.reduce_sum(n2[:, q4:q4 + 1], nrm[:, q4 * 64:(q4 + 1) * 64],
                                  axis=AX.X))
        v.wait_ge(s_dv.h, s_dv.n)
        s_dv.inc(v.tensor_tensor(n2[:, 0:2], n2[:, 0:2], n2[:, 2:4], ALU.mult))
        v.wait_ge(s_dv.h, s_dv.n)
        s_dv.inc(v.reduce_max(n2[:, 4:5], n2[:, 0:2], axis=AX.X))
    blk.vector(f_ev)
    n_ev = s_dv.n
    k.ps_sm_free[0] = max(k.ps_sm_free[0], n_ev)

    def f_sq(s):
        s.wait_ge(s_dv.h, n_ev)
        s_ac.inc(s.activation(n2[:, 5:6], n2[:, 4:5], AF.Sqrt))
    blk.scalar(f_sq)
    n_sq = s_ac.n

    def f_ng(v):
        v.wait_ge(s_ac.h, n_sq)
        s_dv.inc(v.tensor_scalar(n2[:, 6:7], n2[:, 5:6], -1.0, None, ALU.mult))
    blk.vector(f_ng)
    n_ng = s_dv.n

    def f_tp2(t):
        t.wait_ge(s_dv.h, max(n_ng, k.ps_sm_free[0]))
        s_pe.inc(t.transpose(ps_sm[0:1, 300:304], n2[:, 6:7], k.identf[0:4, 0:4]))
    blk.tensor(f_tp2)
    n_tp2 = s_pe.n

    def f_ev2(v):
        v.wait_ge(s_pe.h, n_tp2)
        s_dv.inc(v.tensor_copy(brow[:], ps_sm[0:1, 300:304]))
    blk.vector(f_ev2)
    n_ev2 = s_dv.n

    def f_bc(t):
        t.wait_ge(s_dv.h, n_ev2)
        s_pe.inc(t.matmul(ps_sm[:, 304:308], k.onesr[:], brow[:], start=True, stop=True))
    blk.tensor(f_bc)
    n_bc = s_pe.n

    def f_bc2(v):
        v.wait_ge(s_pe.h, n_bc)
        s_dv.inc(v.tensor_copy(k.negB[:, 0:4], ps_sm[:, 304:308]))
    blk.vector(f_bc2)
    k.n_bh = s_dv.n
    k.ps_sm_free[0] = max(k.ps_sm_free[0], s_dv.n)


def _p3(nc, k, en):
    blk = k.blk
    s_sy, s_gp, s_cc = k.s_sy, k.s_gp, k.s_cc
    s_pe, s_dv, s_ac = k.s_pe, k.s_dv, k.s_ac
    Qfm, Kfm, Vtm = k.Qfm, k.Kfm, k.Vtm

    ps_s = [en(nc.psum_tensor(f"ps_s{j}", [128, 1024], F32)) for j in range(2)]
    ps_pv = [[en(nc.psum_tensor(f"ps_pv{h}_{j}", [128, 260], F32)) for j in range(2)]
             for h in range(2)]
    probs = [en(nc.sbuf_tensor(f"probs{j}", [128, 1024], BF16)) for j in range(2)]
    aoh2 = en(nc.sbuf_tensor("aoh2", [128, NT * 128], BF16))
    aT = en(nc.sbuf_tensor("aT", [128, 4 * L], BF16))
    rcp = en(nc.sbuf_tensor("rcp", [128, 8], F32))
    k.aT = aT
    QC = 8

    hstr2 = [en(nc.sbuf_tensor(f"hstr2{j}", [128, 1024], F32)) for j in range(2)]
    h_qa = en(nc.sbuf_tensor("h_qa", [128, 1024], F32))
    h_qi = en(nc.sbuf_tensor("h_qi", [128, 1024], I32))
    h_qo = [en(nc.sbuf_tensor(f"h_qo{j}", [128, 1024], BF16)) for j in range(2)]
    k.hq_out = {}
    hq_ld, hq_dv1, hq_dvc = {}, {}, {}

    def emit_hquant(ci):
        th2, ic = divmod(ci, IC)
        j2 = ci % 2
        sh = k.s_h0 if j2 == 0 else k.s_h1

        def f_hl(gp, ci=ci, th2=th2, ic=ic, j2=j2, sh=sh):
            if ci < 2:
                gp.wait_ge(k.s_sp0.h, k.n_sp0)
                gp.wait_ge(k.s_sp1.h, k.n_sp1)
            else:
                gp.wait_ge(s_dv.h, hq_dv1[ci - 2])
            sh.inc(gp.dma_start(
                out=hstr2[j2][:],
                in_=k.h_spill.ap()[ic * 128:(ic + 1) * 128,
                                   th2 * 1024:(th2 + 1) * 1024]), 16)
            hq_ld[ci] = (sh, sh.n)
        blk.gpsimd(f_hl)

        def f_hq(v, ci=ci, j2=j2):
            v.wait_ge(hq_ld[ci][0].h, hq_ld[ci][1])
            if ci < 2:
                v.wait_ge(s_dv.h, k.n_rsh)
            if ci >= 2:
                v.wait_ge(k.s_ho.h, k.hq_out[ci - 2])
            s_dv.inc(v.tensor_scalar(h_qa[:], hstr2[j2][:], k.rsh[:], QHI,
                                     ALU.mult, ALU.min))
            hq_dv1[ci] = s_dv.n
            v.wait_ge(s_dv.h, s_dv.n)
            s_dv.inc(v.tensor_scalar(h_qa[:], h_qa[:], QLO, RC2C, ALU.max, ALU.add))
            v.wait_ge(s_dv.h, s_dv.n)
            s_dv.inc(v.tensor_scalar(h_qi[:], h_qa[:], RC2, EPS5, ALU.subtract, ALU.add))
            v.wait_ge(s_dv.h, s_dv.n)
            s_dv.inc(v.tensor_copy(h_qo[j2][:], h_qi[:]))
            v.wait_ge(s_dv.h, s_dv.n)
            s_dv.inc(v.tensor_scalar(h_qo[j2][:], h_qo[j2][:], 129.0, None, ALU.subtract))
            hq_dvc[ci] = s_dv.n
        blk.vector(f_hq)

        def f_ho(e, ci=ci, j2=j2):
            e.wait_ge(s_dv.h, hq_dvc[ci])
            k.s_ho.inc(e.dma_start(
                out=k.ihg_d.ap()[ci * 128:(ci + 1) * 128, :], in_=h_qo[j2][:]), 16)
            k.hq_out[ci] = k.s_ho.n
        blk.sync(f_ho)

    hq_next = [0]

    ac_ex = {}
    dv_mk = {}
    dv_fin = {}
    sy_tp = {}
    prev_ac_end = {-1: 0}
    prev_pe_end = {-1: 0}

    for pair in range(4):
        ch = pair
        for qc in range(QC):
            bi = pair * QC + qc
            kts = (qc + 1) * 4
            diag0 = qc * 4
            setid = bi % 2

            def qlo(kt, diag0=diag0):
                return max(0, kt - diag0) * 128

            pe0 = s_pe.n
            peQK, pePV = {}, {}
            for kt in range(kts):
                peQK[kt] = pe0 + (1 if kt == 0 else 2 * kt)
                pePV[kt] = pe0 + (2 * kt + 3 if kt < kts - 1 else 2 * kts)

            def f_ex(s, pair=pair, qc=qc, bi=bi, kts=kts, diag0=diag0,
                     peQK=dict(peQK)):
                for kt in range(kts):
                    s.wait_ge(s_pe.h, peQK[kt])
                    if kt < 2:
                        s.wait_ge(s_pe.h, prev_pe_end[bi - 1])
                    ql = qlo(kt)
                    s_ac.inc(s.activation(
                        probs[kt % 2].rearrange("p (h q) -> p h q", h=2)[:, :, ql:512],
                        ps_s[kt % 2].rearrange("p (h q) -> p h q", h=2)[:, :, ql:512],
                        AF.Exp, bias=k.negB[:, pair:pair + 1]))
                    ac_ex[(bi, kt)] = s_ac.n
            blk.scalar(f_ex)
            prev_ac_end[bi] = s_ac.n

            def f_mk(v, bi=bi, kts=kts, diag0=diag0):
                for kt in range(diag0, kts):
                    qsub = kt - diag0
                    v.wait_ge(s_ac.h, ac_ex[(bi, kt)])
                    pb = probs[kt % 2]
                    for hh2 in range(2):
                        o = hh2 * 512 + qsub * 128
                        s_dv.inc(v.tensor_tensor(pb[:, o:o + 128], pb[:, o:o + 128],
                                                 k.trimask[:], ALU.mult))
                    dv_mk[(bi, kt)] = s_dv.n
            blk.vector(f_mk)

            def f_pe(t, pair=pair, qc=qc, bi=bi, kts=kts, diag0=diag0, ch=ch,
                     setid=setid, peQK=dict(peQK), pePV=dict(pePV)):
                if bi == 0:
                    t.wait_ge(s_dv.h, k.n_bh)
                if bi >= 2:
                    t.wait_ge(s_dv.h, dv_fin[bi - 2])
                def emit_qk(kt):
                    if kt >= 2:
                        t.wait_ge(s_ac.h, ac_ex[(bi, kt - 2)])
                    elif bi > 0:
                        t.wait_ge(s_ac.h, prev_ac_end[bi - 1])
                    ql = qlo(kt)
                    assert s_pe.n == peQK[kt] - 1
                    for hh2 in range(2):
                        mm = t.matmul(
                            ps_s[kt % 2][:, hh2 * 512 + ql:(hh2 + 1) * 512],
                            Kfm[hh2 * 64:hh2 * 64 + 64,
                                ch * L + kt * 128:ch * L + (kt + 1) * 128],
                            Qfm[hh2 * 64:hh2 * 64 + 64,
                                ch * L + qc * 512 + ql:ch * L + (qc + 1) * 512],
                            start=True, stop=True, tile_position=(hh2 * 64, 0))
                    s_pe.inc(mm)
                    assert s_pe.n == peQK[kt]

                def emit_pv(kt):
                    if kt >= diag0:
                        t.wait_ge(s_dv.h, dv_mk[(bi, kt)])
                    else:
                        t.wait_ge(s_ac.h, ac_ex[(bi, kt)])
                    pb = probs[kt % 2]
                    mm = None
                    for hh2 in range(2):
                        h = pair * 2 + hh2
                        for qsub in range(max(0, kt - diag0), 4):
                            mm = t.matmul(
                                ps_pv[hh2][setid][:, qsub * 65:qsub * 65 + 65],
                                pb[:, hh2 * 512 + qsub * 128:hh2 * 512 + (qsub + 1) * 128],
                                Vtm[:, kt * 520 + h * 65:kt * 520 + (h + 1) * 65],
                                start=(kt == 0 and qsub == 0),
                                stop=(kt == diag0 + qsub), skip_group_check=True)
                    s_pe.inc(mm)
                    assert s_pe.n == pePV[kt]

                emit_qk(0)
                for kt in range(1, kts):
                    emit_qk(kt)
                    emit_pv(kt - 1)
                emit_pv(kts - 1)
            blk.tensor(f_pe)
            prev_pe_end[bi] = s_pe.n
            n_pv = s_pe.n

            def f_fin(v, pair=pair, qc=qc, bi=bi, setid=setid, n_pv=n_pv):
                v.wait_ge(s_pe.h, n_pv)
                if pair >= 1 and qc == 0:
                    v.wait_ge(k.s_tp.h, sy_tp[pair - 1])
                for hh2 in range(2):
                    for qsub in range(4):
                        s_dv.inc(v.reciprocal(
                            rcp[:, hh2 * 4 + qsub:hh2 * 4 + qsub + 1],
                            ps_pv[hh2][setid][:, qsub * 65 + 64:qsub * 65 + 65]))
                v.wait_ge(s_dv.h, s_dv.n)
                for hh2 in range(2):
                    for qsub in range(4):
                        qt = qc * 4 + qsub
                        s_dv.inc(v.tensor_scalar(
                            aoh2[:, qt * 128 + hh2 * 64:qt * 128 + hh2 * 64 + 64],
                            ps_pv[hh2][setid][:, qsub * 65:qsub * 65 + 64],
                            rcp[:, hh2 * 4 + qsub:hh2 * 4 + qsub + 1], None, ALU.mult))
                dv_fin[bi] = s_dv.n
            blk.vector(f_fin)

            for _hq in range(2):
                if hq_next[0] < 2 * IC:
                    emit_hquant(hq_next[0])
                    hq_next[0] += 1

        n_pair = s_dv.n

        def f_tp(e, pair=pair, ch=ch, n_pair=n_pair):
            e.wait_ge(s_dv.h, n_pair)
            for qt in range(NT):
                k.s_tp.inc(e.dma_start_transpose(
                    aT[:, ch * L + qt * 128:ch * L + (qt + 1) * 128],
                    aoh2[:, qt * 128:(qt + 1) * 128]), 16)
        blk.sync(f_tp)
        sy_tp[pair] = k.s_tp.n

        def f_st(gp, pair=pair, ch=ch):
            gp.wait_ge(k.s_tp.h, sy_tp[pair])
            k.s_cca.inc(gp.dma_start(out=k.cc_ain.ap()[ch * 128:(ch + 1) * 128, :],
                                     in_=aT[:, ch * L:(ch + 1) * L]), 16)
            gp.wait_ge(k.s_cca.h, k.s_cca.n)
            s_cc.inc(gp.collective_compute(
                "AllGather", ALU.bypass,
                replica_groups=[[0, 1], [2, 3], [4, 5], [6, 7]],
                ins=[k.cc_ain.ap()[ch * 128:(ch + 1) * 128, :].opt()],
                outs=[k.cc_aout.ap()[ch * 256:(ch + 1) * 256, :].opt()]))
        blk.gpsimd(f_st)
    k.n_ccag = s_cc.n

    if k.debug:
        def f_dbg(e):
            e.wait_ge(s_dv.h, s_dv.n)
            e.wait_ge(k.s_tp.h, n_at)
            k.s_st.inc(e.dma_start(out=k.dbg["d_ao"], in_=aoh2[:, 0:512]), 16)
            k.s_st.inc(e.dma_start(out=k.dbg["d_at"], in_=aT[:, 0:256]), 16)
        blk.sync(f_dbg)


def _p4(nc, k, en):
    blk, din = k.blk, k.din
    s_sy, s_gp, s_pe, s_dv, s_ac = k.s_sy, k.s_gp, k.s_pe, k.s_dv, k.s_ac
    stat = k.stat

    iwd = en(nc.sbuf_tensor("iwd", [128, IC * D], BF16))
    sdb = en(nc.sbuf_tensor("sdb", [128, D], F32))

    with ExitStack() as rd:
        enD = rd.enter_context
        wdl = [enD(nc.sbuf_tensor(f"wdl{j}", [128, II], F32)) for j in range(2)]
        qa_d = enD(nc.sbuf_tensor("qa_d", [128, II], F32))
        qi_d = enD(nc.sbuf_tensor("qi_d", [128, II], I32))
        qn_d = enD(nc.sbuf_tensor("qn_d", [128, II], BF16))
        qt_d = enD(nc.sbuf_tensor("qt_d", [128, II], BF16))
        scdw = enD(nc.sbuf_tensor("scdw", [128, 8], F32))
        scds = enD(nc.sbuf_tensor("scds", [128, 8], F32))
        ps_sm = enD(nc.psum_tensor("ps_sm4", [128, 512], F32))
        ps_td = [enD(nc.psum_tensor(f"ps_td{j}", [128, 1024], BF16)) for j in range(2)]

        sy_d, dv_m = {}, {}
        for db in range(DC):
            j = db % 2

            def f_l(e, db=db, j=j):
                if db >= 2:
                    e.wait_ge(s_dv.h, dv_m[db - 2])
                k.s_wst.inc(e.dma_start(out=wdl[j][:],
                                        in_=din["wd"][db * 128:(db + 1) * 128, :]), 16)
                sy_d[db] = k.s_wst.n
            blk.sync(f_l)

            def f_m(v, db=db, j=j):
                v.wait_ge(k.s_wst.h, sy_d[db])
                s_dv.inc(v.reduce_max(scdw[:, db:db + 1], wdl[j][:], axis=AX.X,
                                      apply_absolute_value=True))
                dv_m[db] = s_dv.n
            blk.vector(f_m)
        n_wdl_free = dv_m[DC - 1]

        def f_sc(v):
            v.wait_ge(s_dv.h, n_wdl_free)
            s_dv.inc(v.tensor_scalar(scdw[:], scdw[:], 1.0 / QMAX, None, ALU.mult))
        blk.vector(f_sc)

        shv = enD(nc.sbuf_tensor("shv", [128, 1], F32))
        k.ps_sm = ps_sm
        k.ps_sm_free = [0]
        bcast_scalar(k, blk, k.sxh[0:1, 1:2], shv[:])

        def f_ss(v):
            v.wait_ge(s_dv.h, s_dv.n)
            s_dv.inc(v.tensor_scalar(scds[:], scdw[:], shv[:], None, ALU.mult))
            v.wait_ge(s_dv.h, s_dv.n)
            s_dv.inc(v.reciprocal(scdw[:], scdw[:]))
        blk.vector(f_ss)
        n_scr = s_dv.n

        dv_bc = {}
        for db in range(DC):
            def f_t1(t, db=db):
                t.wait_ge(s_dv.h, n_scr if db == 0 else dv_bc[db - 1])
                t.wait_ge(s_dv.h, k.ps_sm_free[0])
                s_pe.inc(t.transpose(ps_sm[0:1, 128:256], scds[:, db:db + 1], k.identf[:]))
                pe_a[db] = s_pe.n
            pe_a = {}
            blk.tensor(f_t1)

            def f_e1(v, db=db):
                v.wait_ge(s_pe.h, pe_a[db])
                s_dv.inc(v.tensor_copy(k.trow[:], ps_sm[0:1, 128:256]))
                dv_a[db] = s_dv.n
            dv_a = {}
            blk.vector(f_e1)

            def f_t2(t, db=db):
                t.wait_ge(s_dv.h, dv_a[db])
                s_pe.inc(t.matmul(ps_sm[:, 128:256], k.onesr[:], k.trow[:],
                                  start=True, stop=True))
                pe_b[db] = s_pe.n
            pe_b = {}
            blk.tensor(f_t2)

            def f_e2(v, db=db):
                v.wait_ge(s_pe.h, pe_b[db])
                s_dv.inc(v.tensor_copy(sdb[:, db * 128:(db + 1) * 128], ps_sm[:, 128:256]))
                dv_bc[db] = s_dv.n
            blk.vector(f_e2)
        k.ps_sm_free[0] = dv_bc[DC - 1]

        sy_d2, dv_q1, dv_q2, pe_td, dv_te = {}, {}, {}, {}, {}
        for db in range(DC):
            j = db % 2

            def f_l2(e, db=db, j=j):
                if db < 2:
                    e.wait_ge(s_dv.h, n_wdl_free)
                else:
                    e.wait_ge(s_dv.h, dv_q1[db - 2])
                k.s_wst.inc(e.dma_start(out=wdl[j][:],
                                        in_=din["wd"][db * 128:(db + 1) * 128, :]), 16)
                sy_d2[db] = k.s_wst.n
            blk.sync(f_l2)

            def f_q(v, db=db, j=j):
                v.wait_ge(k.s_wst.h, sy_d2[db])
                if db >= 1:
                    v.wait_ge(s_pe.h, pe_td[(db - 1, 2)])
                s_dv.inc(v.tensor_scalar(qa_d[:], wdl[j][:], scdw[:, db:db + 1],
                                         QHI, ALU.mult, ALU.min))
                dv_q1[db] = s_dv.n
                v.wait_ge(s_dv.h, s_dv.n)
                s_dv.inc(v.tensor_scalar(qa_d[:], qa_d[:], QLO, RC2C, ALU.max, ALU.add))
                v.wait_ge(s_dv.h, s_dv.n)
                s_dv.inc(v.tensor_scalar(qi_d[:], qa_d[:], RC2, EPS5, ALU.subtract, ALU.add))
                v.wait_ge(s_dv.h, s_dv.n)
                s_dv.inc(v.tensor_copy(qn_d[:], qi_d[:]))
                v.wait_ge(s_dv.h, s_dv.n)
                s_dv.inc(v.tensor_scalar(qt_d[:], qn_d[:], 129.0, None, ALU.subtract))
                dv_q2[db] = s_dv.n
            blk.vector(f_q)

            for r3 in range(3):
                i0, i1 = r3 * 8, min((r3 + 1) * 8, IC)
                gr = db * 3 + r3
                pj = gr % 2

                def f_td(t, db=db, r3=r3, i0=i0, i1=i1, gr=gr, pj=pj):
                    t.wait_ge(s_dv.h, dv_q2[db])
                    if gr >= 2:
                        t.wait_ge(s_dv.h, dv_te[divmod(gr - 2, 3)])
                    for ic in range(i0, i1):
                        s_pe.inc(t.transpose(
                            ps_td[pj][:, (ic - i0) * 128:(ic - i0 + 1) * 128],
                            qt_d[:, ic * 128:(ic + 1) * 128], k.identb[:]))
                    pe_td[(db, r3)] = s_pe.n
                blk.tensor(f_td)

                def f_te(v, db=db, r3=r3, i0=i0, i1=i1, pj=pj):
                    v.wait_ge(s_pe.h, pe_td[(db, r3)])
                    nn = i1 - i0
                    s_dv.inc(v.tensor_copy(
                        iwd.rearrange("p (ic d) -> p ic d", d=D)[:, i0:i1,
                                                                 db * 128:(db + 1) * 128],
                        ps_td[pj].rearrange("p (c t) -> p c t", c=8)[:, 0:nn, :]))
                    dv_te[(db, r3)] = s_dv.n
                blk.vector(f_te)
        n_iwd = dv_te[(DC - 1, 2)]
        k.ps_sm = None
    k.barrier()

    ihg = en(nc.sbuf_tensor("ihg", [128, IC * 1024], BF16))
    ps_d = [en(nc.psum_tensor(f"ps_d{j}", [128, 512], F32)) for j in range(8)]

    pe_dn, dv_dn = {}, {}
    for th2 in range(2):
        gp_h = {}
        for ic in range(IC):
            j = ic % 2
            ci = th2 * IC + ic

            def f_hl(gp, th2=th2, ic=ic, j=j, ci=ci):
                if th2 >= 1 and ic < 2:
                    gp.wait_ge(s_pe.h, pe_dn[(th2 - 1, 3)])
                gp.wait_ge(k.s_ho.h, k.hq_out[ci])
                sh = k.s_h0 if j == 0 else k.s_h1
                sh.inc(gp.dma_start(
                    out=ihg[:, ic * 1024:(ic + 1) * 1024],
                    in_=k.ihg_d.ap()[ci * 128:(ci + 1) * 128, :]), 16)
                gp_h[ic] = (sh, sh.n)
            blk.gpsimd(f_hl)
        ld_h0, ld_h1 = k.s_h0.n, k.s_h1.n

        for g4 in range(4):
            tg = th2 * 4 + g4
            bs = (g4 % 2) * 4

            def f_mm(t, th2=th2, g4=g4, tg=tg, bs=bs, ld_h0=ld_h0, ld_h1=ld_h1):
                if g4 == 0:
                    t.wait_ge(k.s_h0.h, ld_h0)
                    t.wait_ge(k.s_h1.h, ld_h1)
                if tg == 0:
                    t.wait_ge(s_dv.h, n_iwd)
                if g4 >= 2 or th2 >= 1:
                    pt = (th2, g4 - 2) if g4 >= 2 else (th2 - 1, g4 + 2)
                    t.wait_ge(s_dv.h, dv_dn[pt])
                mm = None
                for tt in range(2):
                    for half in range(2):
                        for ic in range(IC):
                            mm = t.matmul(
                                ps_d[bs + tt * 2 + half][:],
                                ihg[:, ic * 1024 + g4 * 256 + tt * 128:
                                    ic * 1024 + g4 * 256 + (tt + 1) * 128],
                                iwd[:, ic * D + half * 512:ic * D + (half + 1) * 512],
                                start=(ic == 0), stop=(ic == IC - 1))
                s_pe.inc(mm)
                pe_dn[(th2, g4)] = s_pe.n
            blk.tensor(f_mm)

            def f_de(v, th2=th2, g4=g4, tg=tg, bs=bs):
                v.wait_ge(s_pe.h, pe_dn[(th2, g4)])
                for tt in range(2):
                    for half in range(2):
                        col = (tg * 2 + tt) * D + half * 512
                        s_dv.inc(v.tensor_tensor(k.down_out[:, col:col + 512],
                                                 ps_d[bs + tt * 2 + half][:],
                                                 sdb[:, half * 512:(half + 1) * 512], ALU.mult))
                dv_dn[(th2, g4)] = s_dv.n
            blk.vector(f_de)
    k.n_down = dv_dn[(1, 3)]

    if k.debug:
        def f_dbg(e):
            e.wait_ge(s_dv.h, k.n_down)
            k.s_st.inc(e.dma_start(out=k.dbg["d_dn"], in_=k.down_out[:, 0:1024]), 16)
            k.s_st.inc(e.dma_start(out=k.dbg["d_sc"][0:1, 0:12], in_=k.sxh[:]), 16)
        blk.sync(f_dbg)


def _p5(nc, k, en):
    blk, din = k.blk, k.din
    s_sy, s_gp, s_cc = k.s_sy, k.s_gp, k.s_cc
    s_pe, s_dv, s_st = k.s_pe, k.s_dv, k.s_st

    paT = en(nc.sbuf_tensor("paT", [128, DC * TH], BF16))
    wobf = en(nc.sbuf_tensor("wobf", [128, DC * D], BF16))
    wos = [en(nc.sbuf_tensor(f"wos{j}", [128, D], F32)) for j in range(2)]
    xfr = [en(nc.sbuf_tensor(f"xfr{j}", [128, D], F32)) for j in range(2)]
    acc = [en(nc.sbuf_tensor(f"acc{j}", [128, D], F32)) for j in range(2)]
    offt = en(nc.sbuf_tensor("offt", [1, 1], I32))
    ps_o = [en(nc.psum_tensor(f"ps_o{j}", [128, 512], F32)) for j in range(4)]

    sy_wo, dv_wo = {}, {}
    for dc in range(DC):
        j = dc % 2

        def f_l(e, dc=dc, j=j):
            if dc >= 2:
                e.wait_ge(s_dv.h, dv_wo[dc - 2])
            sw = k.s_w0 if j == 0 else k.s_w1
            sw.inc(e.dma_start(out=wos[j][:], in_=din["woT"][dc * 128:(dc + 1) * 128, :]), 16)
            sy_wo[dc] = (sw, sw.n)
        blk.sync(f_l)

        def f_c(v, dc=dc, j=j):
            v.wait_ge(sy_wo[dc][0].h, sy_wo[dc][1])
            s_dv.inc(v.tensor_copy(wobf[:, dc * D:(dc + 1) * D], wos[j][:]))
            dv_wo[dc] = s_dv.n
        blk.vector(f_c)
    n_wo = dv_wo[DC - 1]

    def f_pa(gp):
        gp.wait_ge(s_cc.h, k.n_ccag)
        k.s_pa.inc(gp.dma_start(out=offt[:], in_=din["toffr"]), 16)
        gp.wait_ge(k.s_pa.h, k.s_pa.n)
        off = gp.value_load(offt[0:1, 0:1])
        for c in range(DC):
            row = (c % 4) * 256 + (c // 4) * 128
            k.s_pa.inc(gp.dma_start(
                out=paT[:, c * TH:(c + 1) * TH],
                in_=k.cc_aout.ap()[row:row + 128, bass.ts(off, TH)]), 16)
    blk.gpsimd(f_pa)
    n_pa = k.s_pa.n

    pe_op, dv_fin, st_y, sy_xr = {}, {}, {}, {}
    for tt in range(NTH):
        j = tt % 2
        pss = ps_o[j * 2:j * 2 + 2]

        def f_xl(e, tt=tt, j=j):
            if tt >= 2:
                e.wait_ge(s_dv.h, dv_fin[tt - 2])
            sx = k.s_x0 if j == 0 else k.s_x1
            sx.inc(e.dma_start(out=xfr[j][:], in_=din["x_my"][tt * 128:(tt + 1) * 128, :]), 16)
            sy_xr[tt] = (sx, sx.n)
        blk.sync(f_xl)

        def f_mm(t, tt=tt, pss=pss):
            if tt == 0:
                t.wait_ge(k.s_pa.h, n_pa)
                t.wait_ge(s_dv.h, n_wo)
            if tt >= 2:
                t.wait_ge(s_dv.h, dv_fin[tt - 2])
            mm = None
            for half in range(2):
                for oc in range(DC):
                    mm = t.matmul(pss[half][:],
                                  paT[:, oc * TH + tt * 128:oc * TH + (tt + 1) * 128],
                                  wobf[:, oc * D + half * 512:oc * D + (half + 1) * 512],
                                  start=(oc == 0), stop=(oc == DC - 1))
            s_pe.inc(mm)
            pe_op[tt] = s_pe.n
        blk.tensor(f_mm)

        def f_fin(v, tt=tt, j=j, pss=pss):
            v.wait_ge(s_pe.h, pe_op[tt])
            v.wait_ge(sy_xr[tt][0].h, sy_xr[tt][1])
            v.wait_ge(s_dv.h, k.n_down)
            if tt >= 2:
                v.wait_ge(st_y[tt - 2][0].h, st_y[tt - 2][1])
            for half in range(2):
                sl = slice(half * 512, (half + 1) * 512)
                s_dv.inc(v.tensor_tensor(acc[j][:, sl], pss[half][:], k.bo_bc[:, sl], ALU.add))
                v.wait_ge(s_dv.h, s_dv.n)
                s_dv.inc(v.tensor_tensor(acc[j][:, sl], acc[j][:, sl],
                                         k.down_out[:, tt * D + half * 512:tt * D + (half + 1) * 512],
                                         ALU.add))
                v.wait_ge(s_dv.h, s_dv.n)
                s_dv.inc(v.tensor_tensor(acc[j][:, sl], acc[j][:, sl], xfr[j][:, sl], ALU.add))
            dv_fin[tt] = s_dv.n
        blk.vector(f_fin)

        def f_st(e, tt=tt, j=j):
            e.wait_ge(s_dv.h, dv_fin[tt])
            ss = k.s_st0 if j == 0 else k.s_st1
            ss.inc(e.dma_start(out=k.y[tt * 128:(tt + 1) * 128, :], in_=acc[j][:]), 16)
            st_y[tt] = (ss, ss.n)
        blk.sync(f_st)


def _host_inputs(inputs):
    f32 = np.float32
    import ml_dtypes
    bf16 = ml_dtypes.bfloat16
    x = np.ascontiguousarray(np.asarray(inputs["L0_attn_x"], f32))
    wq = np.asarray(inputs["L0_attn_wq"], f32)
    wk = np.asarray(inputs["L0_attn_wk"], f32)
    wv = np.asarray(inputs["L0_attn_wv"], f32)
    wo = np.asarray(inputs["L0_attn_wo"], f32)
    bq = np.asarray(inputs["L0_attn_bq"], f32)
    bk = np.asarray(inputs["L0_attn_bk"], f32)
    bv = np.asarray(inputs["L0_attn_bv"], f32)
    bo = np.asarray(inputs["L0_attn_bo"], f32)
    pw1 = np.asarray(inputs["L0_attn_pw1"], f32)
    pb1 = np.asarray(inputs["L0_attn_pb1"], f32)
    pw2 = np.asarray(inputs["L0_attn_pw2"], f32)
    pb2 = np.asarray(inputs["L0_attn_pb2"], f32)
    sw1 = np.asarray(inputs["L0_attn_sw1"], f32)
    sb1 = np.asarray(inputs["L0_attn_sb1"], f32)
    sw2 = np.asarray(inputs["L0_attn_sw2"], f32)
    sb2 = np.asarray(inputs["L0_attn_sb2"], f32)

    common = {
        "woT": np.ascontiguousarray(wo.T), "bo": bo,
        "wg": np.ascontiguousarray(np.asarray(inputs["L0_ffn_w_gate"], f32)),
        "wu": np.ascontiguousarray(np.asarray(inputs["L0_ffn_w_up"], f32)),
        "wd": np.ascontiguousarray(np.asarray(inputs["L0_ffn_w_down"], f32)),
        "clipx": np.asarray(inputs["L0_ffn_clip_x"], f32).reshape(1, 1),
        "clipd": np.asarray(inputs["L0_ffn_clip_down"], f32).reshape(1, 1),
        "sw1T": np.ascontiguousarray(sw1.T), "sb1": sb1,
        "sw2T": np.ascontiguousarray(sw2.T), "sb2": sb2.reshape(P3, 1),
        "pw1T": np.ascontiguousarray(pw1.transpose(0, 2, 1).reshape(P3 * D, DH)),
        "pb1": np.ascontiguousarray(pb1.reshape(P3 * DH)),
        "pw2T": np.ascontiguousarray(pw2.transpose(0, 2, 1).reshape(P3 * DH, H)),
        "pb2": np.ascontiguousarray(pb2.reshape(P3 * H)),
        "identf": np.eye(128, dtype=f32),
        "identb": np.eye(128).astype(bf16),
        "onesf": np.ones((128, 1), f32),
        "onesr": np.ones((1, 128), f32),
        "trimask": np.triu(np.ones((128, 128))).astype(bf16),
    }
    maps = []
    for core in range(8):
        bi, hh = divmod(core, 2)
        osl = slice(hh * 512, (hh + 1) * 512)
        hsel = np.zeros((16, 8), f32)
        for jj in range(8):
            hsel[hh * 8 + jj, jj] = 1.0
        m = dict(common)
        m.update({
            "x_full": np.ascontiguousarray(x[bi]),
            "x_my": np.ascontiguousarray(x[bi, hh * TH:(hh + 1) * TH]),
            "wqT": np.ascontiguousarray(wq[osl].T),
            "wkT": np.ascontiguousarray(wk[osl].T),
            "wvT": np.ascontiguousarray(wv[osl].T),
            "bq": np.ascontiguousarray(bq[osl]),
            "bk": np.ascontiguousarray(bk[osl]),
            "bv": np.ascontiguousarray(bv[osl]),
            "toffr": np.array([[hh]], np.int32),
            "hsel": hsel,
        })
        maps.append(m)
    return maps


def build(debug=False):
    return _build(debug=debug)


def kernel(**inputs):
    if "nc" not in _CACHE:
        _CACHE["nc"] = _build(debug=False)
    nc = _CACHE["nc"]
    maps = _host_inputs(inputs)
    res = run_bass_kernel_spmd(nc, maps, core_ids=list(range(8)))
    out = np.empty((B, L, D), np.float32)
    for core in range(8):
        bi, hh = divmod(core, 2)
        out[bi, hh * TH:(hh + 1) * TH] = res.results[core]["y"]
    return out


# version 23
# speedup vs baseline: 1.0126x; 1.0126x over previous
import numpy as np
from contextlib import ExitStack

import concourse.bass as bass
import concourse.mybir as mybir
from concourse.bass_utils import run_bass_kernel_spmd

F32 = mybir.dt.float32
BF16 = mybir.dt.bfloat16
I32 = mybir.dt.int32
AF = mybir.ActivationFunctionType
ALU = mybir.AluOpType
AX = mybir.AxisListType

B, L, D = 4, 4096, 1024
H, HD, H8 = 16, 64, 8
TH = L // 2
NT = L // 128
NTH = TH // 128
DC = D // 128
II = 2816
IC = II // 128
DH = 512
P3 = 3
QMAX = 127.0
EPS = 1e-5
RC2 = 12582912.0
RC2C = 12583041.0
EPS5 = 0.499969482421875
QHI = 127.49
QLO = -128.49

_CACHE = {}


class Sem:
    def __init__(self, h):
        self.h = h
        self.n = 0

    def inc(self, instr, k=1):
        instr.then_inc(self.h, k)
        self.n += k
        return self.n


class K:
    pass


def _mk_inputs(nc, k):
    d = {}

    def DI(name, shape, dt=F32):
        d[name] = nc.dram_tensor(name, shape, dt, kind="ExternalInput").ap()
    DI("x_full", [L, D]); DI("x_my", [TH, D])
    DI("wqT", [D, 512]); DI("wkT", [D, 512]); DI("wvT", [D, 512])
    DI("bq", [512]); DI("bk", [512]); DI("bv", [512])
    DI("woT", [D, D]); DI("bo", [D])
    DI("wg", [II, D]); DI("wu", [II, D]); DI("wd", [D, II])
    DI("clipx", [1, 1]); DI("clipd", [1, 1])
    DI("sw1T", [D, DH]); DI("sb1", [DH]); DI("sw2T", [DH, P3]); DI("sb2", [P3, 1])
    DI("pw1T", [P3 * D, DH]); DI("pb1", [P3 * DH]); DI("pw2T", [P3 * DH, H])
    DI("pb2", [P3 * H])
    DI("identf", [128, 128]); DI("identb", [128, 128], BF16)
    DI("onesf", [128, 1]); DI("onesr", [1, 128])
    DI("trimask", [128, 128], BF16)
    DI("toffr", [1, 1], I32)
    DI("hsel", [16, 8])
    k.din = d
    k.y = nc.dram_tensor("y", [TH, D], F32, kind="ExternalOutput").ap()
    k.dbg = {}
    if k.debug:
        def DO(name, shape, dt=F32):
            k.dbg[name] = nc.dram_tensor(name, shape, dt, kind="ExternalOutput").ap()
        DO("d_xnmy", [128, D]); DO("d_ixq", [128, 512], BF16); DO("d_gate", [128, 512])
        DO("d_h", [128, 512]); DO("d_hg", [1, H]); DO("d_xnfm", [128, 512], BF16)
        DO("d_q", [128, 256], BF16); DO("d_k", [128, 256], BF16); DO("d_v", [128, 520], BF16)
        DO("d_ao", [128, 512], BF16); DO("d_at", [128, 256], BF16); DO("d_dn", [128, 1024])
        DO("d_sc", [1, 16]); DO("d_nB", [1, 16], F32); DO("d_scg", [128, IC])
        DO("d_iwg", [128, 512], BF16); DO("d_pb", [128, 1024], BF16)
        DO("d_ixq7", [128, 512], BF16); DO("d_iwg7", [128, 512], BF16)
        DO("d_pv", [128, 8])
    k.h_spill = nc.dram_tensor("h_spill", [II, TH], F32, kind="Internal")
    k.ihg_d = nc.dram_tensor("ihg_d", [2 * IC * 128, 1024], BF16, kind="Internal")
    k.cc_ain = nc.dram_tensor("cc_ain", [512, L], BF16, kind="Internal")
    k.cc_aout = nc.dram_tensor("cc_aout", [1024, L], BF16, kind="Internal")
    k.mxn_in = nc.dram_tensor("mxn_in", [1, 16], F32, kind="Internal")
    k.mxn_out = nc.dram_tensor("mxn_out", [1, 16], F32, kind="Internal")
    k.mxh_in = nc.dram_tensor("mxh_in", [1, 16], F32, kind="Internal")
    k.mxh_out = nc.dram_tensor("mxh_out", [1, 16], F32, kind="Internal")


def _alloc_persistent(nc, k, en):
    k.identf = en(nc.sbuf_tensor("identf_t", [128, 128], F32, side="right"))
    k.identb = en(nc.sbuf_tensor("identb_t", [128, 128], BF16, side="right"))
    k.onesf = en(nc.sbuf_tensor("onesf_t", [128, 1], F32, side="right"))
    k.onesr = en(nc.sbuf_tensor("onesr_t", [1, 128], F32, side="right"))
    k.trimask = en(nc.sbuf_tensor("trimask_t", [128, 128], BF16, side="right"))
    k.bq = en(nc.sbuf_tensor("bq_t", [128, 4], F32, side="right"))
    k.bk = en(nc.sbuf_tensor("bk_t", [128, 4], F32, side="right"))
    k.bv_bc = en(nc.sbuf_tensor("bv_bc", [128, 512], F32, side="right"))
    k.bo_bc = en(nc.sbuf_tensor("bo_bc", [128, D], F32, side="right"))
    k.rsx = en(nc.sbuf_tensor("rsx_v", [128, 1], F32, side="right"))
    k.rsh = en(nc.sbuf_tensor("rsh_v", [128, 1], F32, side="right"))
    k.sxh = en(nc.sbuf_tensor("sxh_t", [1, 12], F32, side="right"))
    k.mxs = en(nc.sbuf_tensor("mxs_t", [1, 16], F32, side="right"))
    k.scg = en(nc.sbuf_tensor("scg_t", [128, IC], F32, side="right"))
    k.scu = en(nc.sbuf_tensor("scu_t", [128, IC], F32, side="right"))
    k.scbg = en(nc.sbuf_tensor("scbg_t", [128, IC], F32, side="right"))
    k.scbu = en(nc.sbuf_tensor("scbu_t", [128, IC], F32, side="right"))
    k.hg = en(nc.sbuf_tensor("hg_t", [1, H], F32, side="right"))
    k.negB = en(nc.sbuf_tensor("negB_t", [128, H8], F32, side="right"))
    k.stat = en(nc.sbuf_tensor("stat_t", [128, 16], F32, side="right"))
    k.trow = en(nc.sbuf_tensor("trow_t", [1, 128], F32, side="right"))
    k.gsc = en(nc.sbuf_tensor("gsc_t", [128, 64], F32, side="right"))
    k.ps_sm = None


def bcast_scalar(k, blk, src_ap, dst_col_ap):
    s_pe, s_dv = k.s_pe, k.s_dv
    nw = s_dv.n

    def f_pe(t):
        t.wait_ge(s_dv.h, nw)
        t.wait_ge(s_dv.h, k.ps_sm_free[0])
        s_pe.inc(t.matmul(k.ps_sm[:, 0:1], k.onesr[:], src_ap, start=True, stop=True))
    blk.tensor(f_pe)
    npe = s_pe.n

    def f_dv(v):
        v.wait_ge(s_pe.h, npe)
        s_dv.inc(v.tensor_copy(dst_col_ap, k.ps_sm[:, 0:1]))
    blk.vector(f_dv)
    k.ps_sm_free[0] = s_dv.n
    return s_dv.n


def _build(debug=False):
    nc = bass.Bass("TRN2", target_bir_lowering=False)
    k = K()
    k.debug = debug
    _mk_inputs(nc, k)
    stk = ExitStack()
    en = stk.enter_context
    k.s_sy = Sem(en(nc.semaphore("s_sy")))
    k.s_gp = Sem(en(nc.semaphore("s_gp")))
    k.s_cc = Sem(en(nc.semaphore("s_cc")))
    k.s_pe = Sem(en(nc.semaphore("s_pe")))
    k.s_dv = Sem(en(nc.semaphore("s_dv")))
    k.s_ac = Sem(en(nc.semaphore("s_ac")))
    k.s_st = Sem(en(nc.semaphore("s_st")))
    for nm in ("cst", "x0", "x1", "wst", "w0", "w1", "gb", "gw0", "gw1", "gl",
               "mx", "sp0", "sp1", "tp", "cca", "h0", "h1", "pa", "st0", "st1",
               "tg", "tu", "ho", "ws2", "ho2"):
        setattr(k, "s_" + nm, Sem(en(nc.semaphore("s_" + nm))))
    k.alls = [k.s_sy, k.s_gp, k.s_cc, k.s_pe, k.s_dv, k.s_ac, k.s_st] + [
        getattr(k, "s_" + nm) for nm in
        ("cst", "x0", "x1", "wst", "w0", "w1", "gb", "gw0", "gw1", "gl",
         "mx", "sp0", "sp1", "tp", "cca", "h0", "h1", "pa", "st0", "st1",
         "tg", "tu", "ho", "ws2", "ho2")]
    blk = en(nc.Block())
    k.blk = blk
    _alloc_persistent(nc, k, en)
    s_sy, s_gp, s_cc = k.s_sy, k.s_gp, k.s_cc
    s_pe, s_dv, s_ac, s_st = k.s_pe, k.s_dv, k.s_ac, k.s_st
    din = k.din

    def barrier():
        snap = [(s, s.n) for s in k.alls]
        for sec in (blk.sync, blk.vector, blk.scalar, blk.tensor, blk.gpsimd):
            def f(e, snap=snap):
                for s, n in snap:
                    if n > 0:
                        e.wait_ge(s.h, n)
            sec(f)
    k.barrier = barrier

    k.hsel = en(nc.sbuf_tensor("bh_hsel", [16, 8], F32, side="right"))

    def f_c(e):
        for dst, src in ((k.identf, din["identf"]), (k.identb, din["identb"]),
                         (k.onesf, din["onesf"]), (k.onesr, din["onesr"]),
                         (k.trimask, din["trimask"]), (k.hsel, din["hsel"])):
            k.s_cst.inc(e.dma_start(out=dst[:], in_=src), 16)
        for c in range(4):
            k.s_cst.inc(e.dma_start(out=k.bq[:, c:c + 1],
                                 in_=din["bq"][c * 128:(c + 1) * 128].rearrange("(p i) -> p i", p=128)), 16)
            k.s_cst.inc(e.dma_start(out=k.bk[:, c:c + 1],
                                 in_=din["bk"][c * 128:(c + 1) * 128].rearrange("(p i) -> p i", p=128)), 16)
        k.s_cst.inc(e.dma_start(out=k.sxh[0:1, 4:5], in_=din["clipx"]), 16)
        k.s_cst.inc(e.dma_start(out=k.sxh[0:1, 5:6], in_=din["clipd"]), 16)
    blk.sync(f_c)

    bvrow = en(nc.sbuf_tensor("bvrow", [1, 512], F32, side="right"))
    borow = en(nc.sbuf_tensor("borow", [1, D], F32, side="right"))

    def f_c2(e):
        k.s_cst.inc(e.dma_start(out=bvrow[:], in_=din["bv"].rearrange("(i f) -> i f", i=1)), 16)
        k.s_cst.inc(e.dma_start(out=borow[:], in_=din["bo"].rearrange("(i f) -> i f", i=1)), 16)
    blk.sync(f_c2)
    k.n_cst = k.s_cst.n
    n_rows = k.n_cst

    with nc.psum_tensor("ps_c", [128, 512], F32) as ps_c:
        dv_last = 0
        for j, (row, dst) in enumerate(((bvrow[:], k.bv_bc[:]),
                                        (borow[0:1, 0:512], k.bo_bc[:, 0:512]),
                                        (borow[0:1, 512:1024], k.bo_bc[:, 512:1024]))):
            def f_pe(t, row=row, j=j, dv_last=dv_last):
                t.wait_ge(k.s_cst.h, n_rows)
                if j > 0:
                    t.wait_ge(s_dv.h, dv_last)
                s_pe.inc(t.matmul(ps_c[:], k.onesr[:], row, start=True, stop=True))
            blk.tensor(f_pe)
            npe = s_pe.n

            def f_dv(v, dst=dst, npe=npe):
                v.wait_ge(s_pe.h, npe)
                s_dv.inc(v.tensor_copy(dst, ps_c[:]))
            blk.vector(f_dv)
            dv_last = s_dv.n

    def f_sg(s):
        s.wait_ge(k.s_cst.h, k.n_cst)
        s_ac.inc(s.activation(k.sxh[0:1, 6:7], k.sxh[0:1, 4:5], AF.Sigmoid))
        s.wait_ge(s_ac.h, s_ac.n)
        s_ac.inc(s.activation(k.sxh[0:1, 7:8], k.sxh[0:1, 5:6], AF.Sigmoid))
    blk.scalar(f_sg)

    k.p2xs = [en(nc.sbuf_tensor(f"p2xs{j}", [128, D], F32)) for j in range(2)]
    k.mv32 = en(nc.sbuf_tensor("mv32", [128, 2 * NT], F32))
    k.sd32 = en(nc.sbuf_tensor("sd32", [128, NT], F32))
    k.rs32 = en(nc.sbuf_tensor("rs32", [128, NT], F32))
    k.nm32 = en(nc.sbuf_tensor("nm32", [128, NT], F32))
    k.lnst2 = en(nc.sbuf_tensor("p2lnst", [128, 12], F32))

    with ExitStack() as r1:
        _p01(nc, k, r1.enter_context)
    barrier()
    with ExitStack() as r2o:
        k.Qfm = r2o.enter_context(nc.sbuf_tensor("Qfm", [128, 4 * L], BF16))
        k.Kfm = r2o.enter_context(nc.sbuf_tensor("Kfm", [128, 4 * L], BF16))
        k.Vtm = r2o.enter_context(nc.sbuf_tensor("Vtm", [128, NT * 520], BF16))
        with ExitStack() as r2:
            _p2(nc, k, r2.enter_context)
        barrier()
        with ExitStack() as r3:
            _p3(nc, k, r3.enter_context)
        barrier()
    with ExitStack() as r45:
        k.down_out = r45.enter_context(nc.sbuf_tensor("down_out", [128, NTH * D], F32))
        with ExitStack() as r4:
            _p4(nc, k, r4.enter_context)
        barrier()
        with ExitStack() as r5:
            _p5(nc, k, r5.enter_context)

    def f_end(e):
        if k.s_st0.n:
            e.wait_ge(k.s_st0.h, k.s_st0.n)
        if k.s_st1.n:
            e.wait_ge(k.s_st1.h, k.s_st1.n)
        if k.s_st.n:
            e.wait_ge(k.s_st.h, k.s_st.n)
    blk.sync(f_end)

    stk.close()
    return nc


def _p01(nc, k, en):
    blk, din = k.blk, k.din
    s_sy, s_gp, s_cc = k.s_sy, k.s_gp, k.s_cc
    s_pe, s_dv, s_ac = k.s_pe, k.s_dv, k.s_ac
    stat = k.stat

    k.ixq = en(nc.sbuf_tensor("ixq", [128, DC * TH], BF16))
    r_ps = ExitStack()
    ps_sm = r_ps.enter_context(nc.psum_tensor("ps_sm01", [128, 512], F32))
    k.ps_sm = ps_sm
    k.ps_sm_free = [0]

    wrow = [en(nc.sbuf_tensor(f"wrow{j}", [128, D], F32)) for j in range(2)]
    k.scri_g = en(nc.sbuf_tensor("scri_g", [128, IC], F32))
    k.scri_u = en(nc.sbuf_tensor("scri_u", [128, IC], F32))
    sc_ld, sc_dv = {}, {}

    def emit_scale_chunk(idx):
        wi, ic = divmod(idx, IC)
        w_dram = din["wg"] if wi == 0 else din["wu"]
        out_raw = k.scg if wi == 0 else k.scu
        j = idx % 2

        def f_l(e, idx=idx, ic=ic, j=j, w_dram=w_dram):
            if idx >= 2:
                e.wait_ge(s_dv.h, sc_dv[idx - 2])
            sw = k.s_wst if j == 0 else k.s_ws2
            sw.inc(e.dma_start(out=wrow[j][:],
                               in_=w_dram[ic * 128:(ic + 1) * 128, :]), 16)
            sc_ld[idx] = (sw, sw.n)
        blk.sync(f_l)

        def f_m(v, idx=idx, ic=ic, j=j, out_raw=out_raw):
            v.wait_ge(sc_ld[idx][0].h, sc_ld[idx][1])
            s_dv.inc(v.reduce_max(out_raw[:, ic:ic + 1], wrow[j][:], axis=AX.X,
                                  apply_absolute_value=True))
            sc_dv[idx] = s_dv.n
        blk.vector(f_m)

    with ExitStack() as rA:
        enA = rA.enter_context
        xnmy = enA(nc.sbuf_tensor("xnmy", [128, NTH * D], F32))
        xs = [enA(nc.sbuf_tensor(f"p0xs{j}", [128, D], F32)) for j in range(2)]
        lnst = enA(nc.sbuf_tensor("p0lnst", [128, 24], F32))
        lnmv = enA(nc.sbuf_tensor("p0lnmv", [128, 4], F32))
        lnsc = enA(nc.sbuf_tensor("p0lnsc", [128, 8], F32))
        runmax = enA(nc.sbuf_tensor("runmax", [128, 1], F32))

        def f_i(v):
            s_dv.inc(v.memset(runmax[:], 0.0))
        blk.vector(f_i)

        sy_x = {}
        dv_st0 = {}
        dv_xn = {}
        sc_next = 0
        mv16 = enA(nc.sbuf_tensor("p0mv16", [128, 2 * NTH], F32))
        sd16 = enA(nc.sbuf_tensor("p0sd16", [128, NTH], F32))
        rs16 = enA(nc.sbuf_tensor("p0rs16", [128, NTH], F32))
        nm16 = enA(nc.sbuf_tensor("p0nm16", [128, NTH], F32))
        for i in range(NTH):
            j = i % 2

            def f_ld(e, i=i, j=j):
                if i >= 2:
                    e.wait_ge(s_dv.h, dv_st0[i - 2])
                sx = k.s_x0 if j == 0 else k.s_x1
                sx.inc(e.dma_start(out=xs[j][:], in_=din["x_my"][i * 128:(i + 1) * 128, :]), 16)
                sy_x[i] = (sx, sx.n)
            blk.sync(f_ld)

            def f_d1(v, i=i, j=j):
                v.wait_ge(sy_x[i][0].h, sy_x[i][1])
                s_dv.inc(v.bn_stats(lnst[:, 0:6], xs[j][:, 0:512]))
                s_dv.inc(v.bn_stats(lnst[:, 6:12], xs[j][:, 512:1024]))
                v.wait_ge(s_dv.h, s_dv.n)
                s_dv.inc(v.bn_aggr(mv16[:, 2 * i:2 * i + 2], lnst[:, 0:12]))
                dv_st0[i] = s_dv.n
            blk.vector(f_d1)

            for _sc in range(2):
                if sc_next < 2 * IC:
                    emit_scale_chunk(sc_next)
                    sc_next += 1
        while sc_next < 2 * IC:
            emit_scale_chunk(sc_next)
            sc_next += 1

        def f_v1(v):
            v.wait_ge(s_dv.h, dv_st0[NTH - 1])
            s_dv.inc(v.tensor_scalar(
                sd16.rearrange("p (one i) -> p one i", one=1),
                mv16.rearrange("p (i two) -> p two i", two=2)[:, 1:2, :],
                EPS, None, ALU.add))
        blk.vector(f_v1)
        n_v1 = s_dv.n

        def f_sq16(s):
            s.wait_ge(s_dv.h, n_v1)
            s_ac.inc(s.activation(sd16[:], sd16[:], AF.Sqrt))
        blk.scalar(f_sq16)
        n_sq16 = s_ac.n

        def f_v2(v):
            v.wait_ge(s_ac.h, n_sq16)
            s_dv.inc(v.reciprocal(rs16[:], sd16[:]))
            s_dv.inc(v.tensor_scalar(
                nm16.rearrange("p (one i) -> p one i", one=1),
                mv16.rearrange("p (i two) -> p two i", two=2)[:, 0:1, :],
                -1.0, None, ALU.mult))
        blk.vector(f_v2)
        n_lnf0 = s_dv.n

        sy_x2 = {}
        for i in range(NTH):
            j = i % 2
            tmax = lnsc[:, j * 4 + 3:j * 4 + 4]

            def f_ld2(e, i=i, j=j):
                if i < 2:
                    e.wait_ge(s_dv.h, n_lnf0)
                else:
                    e.wait_ge(s_dv.h, dv_xn[i - 2])
                sx = k.s_x0 if j == 0 else k.s_x1
                sx.inc(e.dma_start(out=xs[j][:], in_=din["x_my"][i * 128:(i + 1) * 128, :]), 16)
                sy_x2[i] = (sx, sx.n)
            blk.sync(f_ld2)

            def f_d3(v, i=i, j=j, tmax=tmax):
                v.wait_ge(sy_x2[i][0].h, sy_x2[i][1])
                s_dv.inc(v.tensor_scalar(xnmy[:, i * D:(i + 1) * D], xs[j][:],
                                         nm16[:, i:i + 1], rs16[:, i:i + 1],
                                         ALU.add, ALU.mult))
                v.wait_ge(s_dv.h, s_dv.n)
                s_dv.inc(v.reduce_max(tmax, xnmy[:, i * D:(i + 1) * D], axis=AX.X,
                                      apply_absolute_value=True))
                v.wait_ge(s_dv.h, s_dv.n)
                s_dv.inc(v.tensor_tensor(runmax[:], runmax[:], tmax, ALU.max))
                dv_xn[i] = s_dv.n
            blk.vector(f_d3)

        n_f = dv_xn[NTH - 1]

        def f_fp(t):
            t.wait_ge(s_dv.h, n_f)
            s_pe.inc(t.transpose(ps_sm[0:1, 0:128], runmax[:], k.identf[:]))
        blk.tensor(f_fp)
        n_fp = s_pe.n

        def f_f2(v):
            v.wait_ge(s_pe.h, n_fp)
            s_dv.inc(v.memset(k.mxs[:], 0.0))
            v.wait_ge(s_dv.h, s_dv.n)
            s_dv.inc(v.reduce_max(k.mxs[0:1, 0:1], ps_sm[0:1, 0:128], axis=AX.X))
        blk.vector(f_f2)
        n_mx = s_dv.n
        k.ps_sm_free[0] = n_mx

        def f_cc(g):
            g.wait_ge(s_dv.h, n_mx)
            k.s_mx.inc(g.dma_start(out=k.mxn_in.ap(), in_=k.mxs[:]), 16)
            g.wait_ge(k.s_mx.h, k.s_mx.n)
            s_cc.inc(g.collective_compute(
                "AllReduce", ALU.max, replica_groups=[list(range(8))],
                ins=[k.mxn_in.ap().opt()], outs=[k.mxn_out.ap().opt()]))
            g.wait_ge(s_cc.h, s_cc.n)
            k.s_mx.inc(g.dma_start(out=k.mxs[0:1, 0:1], in_=k.mxn_out.ap()[0:1, 0:1]), 16)
        blk.gpsimd(f_cc)
        n_ar1 = k.s_mx.n

        def f_sx(v):
            v.wait_ge(k.s_mx.h, n_ar1)
            v.wait_ge(s_ac.h, s_ac.n)
            s_dv.inc(v.tensor_tensor(k.sxh[0:1, 0:1], k.sxh[0:1, 6:7], k.mxs[0:1, 0:1], ALU.mult))
            v.wait_ge(s_dv.h, s_dv.n)
            s_dv.inc(v.tensor_scalar(k.sxh[0:1, 0:1], k.sxh[0:1, 0:1], 1.0 / QMAX, None, ALU.mult))
            v.wait_ge(s_dv.h, s_dv.n)
            s_dv.inc(v.reciprocal(k.sxh[0:1, 2:3], k.sxh[0:1, 0:1]))
        blk.vector(f_sx)
        bcast_scalar(k, blk, k.sxh[0:1, 2:3], k.rsx[:])

        with ExitStack() as rQ:
            enQ = rQ.enter_context
            qa = enQ(nc.sbuf_tensor("qa", [128, 2 * D], F32))
            qi = enQ(nc.sbuf_tensor("qi", [128, 2 * D], I32))
            qn = enQ(nc.sbuf_tensor("qn", [128, 2 * D], BF16))
            qq = enQ(nc.sbuf_tensor("qq", [128, 2 * D], BF16))
            ps_t1 = enQ(nc.psum_tensor("ps_t1", [128, 512], BF16))
            ps_t2 = enQ(nc.psum_tensor("ps_t2", [128, 512], BF16))
            n_rsx = s_dv.n
            dv_q2 = {}
            pe_tp = {}
            dv_ix = {}
            for i2 in range(NTH // 2):
                def f_q(v, i2=i2):
                    if i2 == 0:
                        v.wait_ge(s_dv.h, n_rsx)
                    if i2 >= 1:
                        v.wait_ge(s_pe.h, pe_tp[2 * i2 - 1])
                    s_dv.inc(v.tensor_scalar(qa[:], xnmy[:, 2 * i2 * D:(2 * i2 + 2) * D],
                                             k.rsx[:], QHI, ALU.mult, ALU.min))
                    v.wait_ge(s_dv.h, s_dv.n)
                    s_dv.inc(v.tensor_scalar(qa[:], qa[:], QLO, RC2C, ALU.max, ALU.add))
                    v.wait_ge(s_dv.h, s_dv.n)
                    s_dv.inc(v.tensor_scalar(qi[:], qa[:], RC2, EPS5, ALU.subtract, ALU.add))
                    v.wait_ge(s_dv.h, s_dv.n)
                    s_dv.inc(v.tensor_copy(qn[:], qi[:]))
                    v.wait_ge(s_dv.h, s_dv.n)
                    s_dv.inc(v.tensor_scalar(qq[:], qn[:], 129.0, None, ALU.subtract))
                    dv_q2[2 * i2] = dv_q2[2 * i2 + 1] = s_dv.n
                blk.vector(f_q)

                for i in (2 * i2, 2 * i2 + 1):
                    def f_tp(t, i=i):
                        t.wait_ge(s_dv.h, dv_q2[i])
                        if i >= 1:
                            t.wait_ge(s_dv.h, dv_ix[i - 1])
                        for dc in range(DC):
                            ps = ps_t1 if dc < 4 else ps_t2
                            s_pe.inc(t.transpose(
                                ps[:, (dc % 4) * 128:(dc % 4 + 1) * 128],
                                qq[:, (i % 2) * D + dc * 128:(i % 2) * D + (dc + 1) * 128],
                                k.identb[:]))
                        pe_tp[i] = s_pe.n
                    blk.tensor(f_tp)

                    def f_ev(v, i=i):
                        v.wait_ge(s_pe.h, pe_tp[i])
                        for half in range(2):
                            ps = ps_t1 if half == 0 else ps_t2
                            s_dv.inc(v.tensor_copy(
                                k.ixq.rearrange("p (dc t) -> p dc t", dc=DC)[
                                    :, half * 4:half * 4 + 4, i * 128:(i + 1) * 128],
                                ps.rearrange("p (c t) -> p c t", c=4)))
                        dv_ix[i] = s_dv.n
                    blk.vector(f_ev)
            k.n_ixq = dv_ix[NTH - 1]
            if k.debug:
                def f_dbg(e):
                    e.wait_ge(s_dv.h, k.n_ixq)
                    k.s_st.inc(e.dma_start(out=k.dbg["d_xnmy"], in_=xnmy[:, 0:D]), 16)
                    k.s_st.inc(e.dma_start(out=k.dbg["d_ixq"], in_=k.ixq[:, 0:512]), 16)
                    k.s_st.inc(e.dma_start(out=k.dbg["d_ixq7"], in_=k.ixq[:, 7 * TH:7 * TH + 512]), 16)
                blk.sync(f_dbg)
        k.barrier()

    def f_scq(v):
        v.wait_ge(s_dv.h, s_dv.n)
        s_dv.inc(v.tensor_scalar(k.scg[:], k.scg[:], 1.0 / QMAX, None, ALU.mult))
        s_dv.inc(v.tensor_scalar(k.scu[:], k.scu[:], 1.0 / QMAX, None, ALU.mult))
        v.wait_ge(s_dv.h, s_dv.n)
        s_dv.inc(v.reciprocal(k.scri_g[:], k.scg[:]))
        s_dv.inc(v.reciprocal(k.scri_u[:], k.scu[:]))
    blk.vector(f_scq)
    n_scq = s_dv.n

    sxv = en(nc.sbuf_tensor("sxv", [128, 1], F32))
    bcast_scalar(k, blk, k.sxh[0:1, 0:1], sxv[:])

    def f_cmb(v):
        v.wait_ge(s_dv.h, s_dv.n)
        s_dv.inc(v.tensor_scalar(k.scbg[:], k.scg[:], sxv[:], None, ALU.mult))
        v.wait_ge(s_dv.h, s_dv.n)
        s_dv.inc(v.tensor_scalar(k.scbu[:], k.scu[:], sxv[:], None, ALU.mult))
    blk.vector(f_cmb)
    n_cmb = s_dv.n

    r_ps.close()
    r_gu = ExitStack()
    en_gu = r_gu.enter_context
    ps_g = [en_gu(nc.psum_tensor(f"ps_g{j}", [128, 512], F32)) for j in range(2)]
    ps_u = [en_gu(nc.psum_tensor(f"ps_u{j}", [128, 512], F32)) for j in range(2)]
    ps_tw = [en_gu(nc.psum_tensor(f"ps_tw{j}", [128, 1024], BF16)) for j in range(4)]
    sg = [en(nc.sbuf_tensor(f"sg_s{j}", [128, 512], F32)) for j in range(2)]
    ht = [en(nc.sbuf_tensor(f"ht{j}", [128, 512], F32)) for j in range(2)]
    hmax = en(nc.sbuf_tensor("hmax", [128, 1], F32))
    wqld = [en_gu(nc.sbuf_tensor(f"wqld{j}", [128, D], F32)) for j in range(4)]
    qa_w = en_gu(nc.sbuf_tensor("qa_w", [128, 2 * D], F32))
    qi_w = en_gu(nc.sbuf_tensor("qi_w", [128, 2 * D], I32))
    qn_w = en_gu(nc.sbuf_tensor("qn_w", [128, 2 * D], BF16))
    qt = [en_gu(nc.sbuf_tensor(f"qt{j}", [128, 2 * D], BF16)) for j in range(2)]
    iwgr = [en_gu(nc.sbuf_tensor(f"iwgr{j}", [128, D], BF16)) for j in range(2)]
    iwur = [en_gu(nc.sbuf_tensor(f"iwur{j}", [128, D], BF16)) for j in range(2)]

    def f_hm(v):
        s_dv.inc(v.memset(hmax[:], 0.0))
    blk.vector(f_hm)
    n_hm0 = s_dv.n

    pe_gu, ac_si, dv_h, gp_h = {}, {}, {}, {}
    sy_wq, dv_ch, pe_tw, dv_twe = {}, {}, {}, {}

    def emit_quant(ic):
        j = ic % 2
        for wi, w_dram in enumerate((din["wg"], din["wu"])):
            slot = j * 2 + wi

            def f_wl(e, ic=ic, wi=wi, slot=slot, w_dram=w_dram):
                if ic >= 2:
                    e.wait_ge(s_dv.h, dv_ch[(ic - 2, 0)])
                sw = k.s_wst if ic % 2 == 0 else k.s_ws2
                sw.inc(e.dma_start(out=wqld[slot][:],
                                   in_=w_dram[ic * 128:(ic + 1) * 128, :]), 16)
                sy_wq[(ic, wi)] = (sw, sw.n)
            blk.sync(f_wl)

        def f_ch(v, ic=ic, j=j):
            v.wait_ge(sy_wq[(ic, 1)][0].h, sy_wq[(ic, 1)][1])
            if ic == 0:
                v.wait_ge(s_dv.h, n_scq)
            if ic >= 2:
                v.wait_ge(s_pe.h, pe_tw[ic - 2])
            s_dv.inc(v.tensor_scalar(qa_w[:, 0:D], wqld[j * 2][:],
                                     k.scri_g[:, ic:ic + 1], QHI, ALU.mult, ALU.min))
            s_dv.inc(v.tensor_scalar(qa_w[:, D:2 * D], wqld[j * 2 + 1][:],
                                     k.scri_u[:, ic:ic + 1], QHI, ALU.mult, ALU.min))
            v.wait_ge(s_dv.h, s_dv.n)
            s_dv.inc(v.tensor_scalar(qa_w[:], qa_w[:], QLO, RC2C, ALU.max, ALU.add))
            v.wait_ge(s_dv.h, s_dv.n)
            s_dv.inc(v.tensor_scalar(qi_w[:], qa_w[:], RC2, EPS5, ALU.subtract, ALU.add))
            v.wait_ge(s_dv.h, s_dv.n)
            s_dv.inc(v.tensor_copy(qn_w[:], qi_w[:]))
            v.wait_ge(s_dv.h, s_dv.n)
            s_dv.inc(v.tensor_scalar(qt[j][:], qn_w[:], 129.0, None, ALU.subtract))
            dv_ch[(ic, 0)] = dv_ch[(ic, 1)] = s_dv.n
        blk.vector(f_ch)

    def emit_quant_tw(ic):
        j = ic % 2

        def f_tw(t, ic=ic, j=j):
            t.wait_ge(s_dv.h, dv_ch[(ic, 0)])
            if ic >= 2:
                t.wait_ge(s_dv.h, dv_twe[ic - 2])
            for wi in range(2):
                for dc in range(DC):
                    s_pe.inc(t.transpose(ps_tw[j * 2 + wi][:, dc * 128:(dc + 1) * 128],
                                         qt[j][:, wi * D + dc * 128:wi * D + (dc + 1) * 128],
                                         k.identb[:]))
            pe_tw[ic] = s_pe.n
        blk.tensor(f_tw)

        def f_twe(v, ic=ic, j=j):
            v.wait_ge(s_pe.h, pe_tw[ic])
            if ic >= 2:
                v.wait_ge(s_pe.h, pe_gu[(ic - 2) * 4 + 3])
            s_dv.inc(v.tensor_copy(iwgr[j][:], ps_tw[j * 2][:]))
            s_dv.inc(v.tensor_copy(iwur[j][:], ps_tw[j * 2 + 1][:]))
            dv_twe[ic] = s_dv.n
        blk.vector(f_twe)

    sy_xf, dv_stf = {}, {}

    def emit_p2stat(i):
        j3 = i % 2

        def f_ldf(e, i=i, j3=j3):
            if i >= 2:
                e.wait_ge(s_dv.h, dv_stf[i - 2])
            sx = k.s_x0 if (i % 2) == 0 else k.s_x1
            sx.inc(e.dma_start(out=k.p2xs[j3][:],
                               in_=din["x_full"][i * 128:(i + 1) * 128, :]), 16)
            sy_xf[i] = (sx, sx.n)
        blk.sync(f_ldf)

        def f_stf(v, i=i, j3=j3):
            v.wait_ge(sy_xf[i][0].h, sy_xf[i][1])
            s_dv.inc(v.bn_stats(k.lnst2[:, 0:6], k.p2xs[j3][:, 0:512]))
            s_dv.inc(v.bn_stats(k.lnst2[:, 6:12], k.p2xs[j3][:, 512:1024]))
            v.wait_ge(s_dv.h, s_dv.n)
            s_dv.inc(v.bn_aggr(k.mv32[:, 2 * i:2 * i + 2], k.lnst2[:]))
            dv_stf[i] = s_dv.n
        blk.vector(f_stf)

    emit_quant(0)
    emit_quant_tw(0)
    emit_quant(1)
    emit_quant_tw(1)
    inst = 0
    p2s_next = 0
    for ic in range(IC):
        j = ic % 2
        if ic + 2 < IC:
            emit_quant(ic + 2)
        for _p2s in range(2):
            if p2s_next < NT:
                emit_p2stat(p2s_next)
                p2s_next += 1
        for tc in range(4):
            pj = inst % 2
            bsg = ps_g[pj]
            bsu = ps_u[pj]

            def f_mm(t, ic=ic, tc=tc, j=j, inst=inst, bsg=bsg, bsu=bsu):
                if inst == 0:
                    t.wait_ge(s_dv.h, max(n_cmb, k.n_ixq))
                if tc == 0:
                    t.wait_ge(s_dv.h, dv_twe[ic])
                if inst >= 2:
                    t.wait_ge(s_ac.h, ac_si[inst - 2])
                    t.wait_ge(s_dv.h, dv_h[inst - 2])
                for dc in range(DC):
                    t.matmul(bsg[:],
                             iwgr[j][:, dc * 128:(dc + 1) * 128],
                             k.ixq[:, dc * TH + tc * 512:dc * TH + (tc + 1) * 512],
                             start=(dc == 0), stop=(dc == DC - 1))
                for dc in range(DC):
                    mm = t.matmul(bsu[:],
                                  iwur[j][:, dc * 128:(dc + 1) * 128],
                                  k.ixq[:, dc * TH + tc * 512:dc * TH + (tc + 1) * 512],
                                  start=(dc == 0), stop=(dc == DC - 1))
                s_pe.inc(mm)
                pe_gu[inst] = s_pe.n
            blk.tensor(f_mm)

            key = inst
            jj = key % 2

            def f_si(s, ic=ic, inst=inst, key=key, jj=jj, bsg=bsg):
                s.wait_ge(s_pe.h, pe_gu[inst])
                if key < 2:
                    s.wait_ge(s_dv.h, n_cmb)
                if key >= 2:
                    s.wait_ge(s_dv.h, dv_h[key - 2])
                s_ac.inc(s.activation(sg[jj][:], bsg[:], AF.Silu,
                                      scale=k.scbg[:, ic:ic + 1]))
                ac_si[key] = s_ac.n
            blk.scalar(f_si)

            def f_h(v, ic=ic, key=key, jj=jj, bsu=bsu):
                v.wait_ge(s_ac.h, ac_si[key])
                if key >= 2:
                    v.wait_ge(gp_h[key - 2][0].h, gp_h[key - 2][1])
                if key == 0:
                    v.wait_ge(s_dv.h, n_hm0)
                s_dv.inc(v.scalar_tensor_tensor(ht[jj][:], bsu[:],
                                                k.scbu[:, ic:ic + 1], sg[jj][:],
                                                ALU.mult, ALU.mult))
                v.wait_ge(s_dv.h, s_dv.n)
                s_dv.inc(v.reduce_max(stat[:, 4:5], ht[jj][:], axis=AX.X,
                                      apply_absolute_value=True))
                v.wait_ge(s_dv.h, s_dv.n)
                s_dv.inc(v.tensor_tensor(hmax[:], hmax[:], stat[:, 4:5], ALU.max))
                dv_h[key] = s_dv.n
            blk.vector(f_h)

            def f_sp(gp, ic=ic, tc=tc, key=key, jj=jj):
                gp.wait_ge(s_dv.h, dv_h[key])
                sp = k.s_sp0 if jj == 0 else k.s_sp1
                sp.inc(gp.dma_start(
                    out=k.h_spill.ap()[ic * 128:(ic + 1) * 128, tc * 512:(tc + 1) * 512],
                    in_=ht[jj][:]), 16)
                gp_h[key] = (sp, sp.n)
            blk.gpsimd(f_sp)

            inst += 1
        if ic + 2 < IC:
            emit_quant_tw(ic + 2)

    def f_v1f(v):
        v.wait_ge(s_dv.h, dv_stf[NT - 1])
        s_dv.inc(v.tensor_scalar(
            k.sd32.rearrange("p (one i) -> p one i", one=1),
            k.mv32.rearrange("p (i two) -> p two i", two=2)[:, 1:2, :],
            EPS, None, ALU.add))
    blk.vector(f_v1f)
    n_v1f = s_dv.n

    def f_sq32(s):
        s.wait_ge(s_dv.h, n_v1f)
        s_ac.inc(s.activation(k.sd32[:], k.sd32[:], AF.Sqrt))
    blk.scalar(f_sq32)
    n_sq32 = s_ac.n

    def f_v2f(v):
        v.wait_ge(s_ac.h, n_sq32)
        s_dv.inc(v.reciprocal(k.rs32[:], k.sd32[:]))
        s_dv.inc(v.tensor_scalar(
            k.nm32.rearrange("p (one i) -> p one i", one=1),
            k.mv32.rearrange("p (i two) -> p two i", two=2)[:, 0:1, :],
            -1.0, None, ALU.mult))
    blk.vector(f_v2f)
    k.n_lnf2 = s_dv.n

    r_gu.close()
    ps_sm = en(nc.psum_tensor("ps_sm01b", [128, 512], F32))
    k.ps_sm = ps_sm
    n_hf = dv_h[inst - 1]

    def f_hfp(t):
        t.wait_ge(s_dv.h, max(n_hf, k.ps_sm_free[0]))
        s_pe.inc(t.transpose(ps_sm[0:1, 0:128], hmax[:], k.identf[:]))
    blk.tensor(f_hfp)
    n_hfp = s_pe.n

    def f_hf2(v):
        v.wait_ge(s_pe.h, n_hfp)
        s_dv.inc(v.memset(k.mxs[:], 0.0))
        v.wait_ge(s_dv.h, s_dv.n)
        s_dv.inc(v.reduce_max(k.mxs[0:1, 0:1], ps_sm[0:1, 0:128], axis=AX.X))
    blk.vector(f_hf2)
    n_hmx = s_dv.n
    k.ps_sm_free[0] = n_hmx

    def f_hcc(g):
        g.wait_ge(s_dv.h, n_hmx)
        k.s_mx.inc(g.dma_start(out=k.mxh_in.ap(), in_=k.mxs[:]), 16)
        g.wait_ge(k.s_mx.h, k.s_mx.n)
        s_cc.inc(g.collective_compute(
            "AllReduce", ALU.max, replica_groups=[list(range(8))],
            ins=[k.mxh_in.ap().opt()], outs=[k.mxh_out.ap().opt()]))
        g.wait_ge(s_cc.h, s_cc.n)
        k.s_mx.inc(g.dma_start(out=k.mxs[0:1, 1:2], in_=k.mxh_out.ap()[0:1, 0:1]), 16)
    blk.gpsimd(f_hcc)
    n_ar2 = k.s_mx.n

    def f_sh(v):
        v.wait_ge(k.s_mx.h, n_ar2)
        s_dv.inc(v.tensor_tensor(k.sxh[0:1, 1:2], k.sxh[0:1, 7:8], k.mxs[0:1, 1:2], ALU.mult))
        v.wait_ge(s_dv.h, s_dv.n)
        s_dv.inc(v.tensor_scalar(k.sxh[0:1, 1:2], k.sxh[0:1, 1:2], 1.0 / QMAX, None, ALU.mult))
        v.wait_ge(s_dv.h, s_dv.n)
        s_dv.inc(v.reciprocal(k.sxh[0:1, 3:4], k.sxh[0:1, 1:2]))
    blk.vector(f_sh)
    bcast_scalar(k, blk, k.sxh[0:1, 3:4], k.rsh[:])
    k.n_rsh = s_dv.n
    k.n_sp0, k.n_sp1 = k.s_sp0.n, k.s_sp1.n
    k.ps_sm = None


def _p2(nc, k, en):
    blk, din = k.blk, k.din
    s_sy, s_pe, s_dv, s_ac = k.s_sy, k.s_pe, k.s_dv, k.s_ac
    stat = k.stat
    Qfm, Kfm, Vtm = k.Qfm, k.Kfm, k.Vtm

    xnfm = en(nc.sbuf_tensor("xnfm", [128, DC * L], BF16))
    ps_sm = en(nc.psum_tensor("ps_sm2", [128, 512], F32))
    k.ps_sm = ps_sm
    k.ps_sm_free = [0]
    wbf = en(nc.sbuf_tensor("wbf", [128, DC * 512], BF16))
    wst = [en(nc.sbuf_tensor(f"wst{j}", [128, 512], F32)) for j in range(2)]
    ps_q = [en(nc.psum_tensor(f"ps_q{j}", [128, 512], F32)) for j in range(2)]
    xm = en(nc.sbuf_tensor("xm_t", [128, DC], F32))

    rLN = ExitStack()
    enL = rLN.enter_context
    xs = k.p2xs
    rs32, nm32 = k.rs32, k.nm32
    n_lnf = k.n_lnf2
    xb = [enL(nc.sbuf_tensor(f"p2xb{j}", [128, D], BF16)) for j in range(2)]
    ps_t1 = enL(nc.psum_tensor("p2ps_t1", [128, 512], BF16))
    ps_t2 = enL(nc.psum_tensor("p2ps_t2", [128, 512], BF16))

    def load_w_bf(wT, n_prev_done):
        sy_w, dv_w = {}, {}
        for dc in range(DC):
            jj = dc % 2

            def f_l(e, dc=dc, jj=jj):
                if dc == 0:
                    e.wait_ge(s_pe.h, n_prev_done)
                if dc >= 2:
                    e.wait_ge(s_dv.h, dv_w[dc - 2])
                sw = k.s_w0 if jj == 0 else k.s_w1
                sw.inc(e.dma_start(out=wst[jj][:], in_=wT[dc * 128:(dc + 1) * 128, :]), 16)
                sy_w[dc] = (sw, sw.n)
            blk.sync(f_l)

            def f_c(v, dc=dc, jj=jj):
                v.wait_ge(sy_w[dc][0].h, sy_w[dc][1])
                s_dv.inc(v.tensor_copy(wbf[:, dc * 512:(dc + 1) * 512], wst[jj][:]))
                dv_w[dc] = s_dv.n
            blk.vector(f_c)
        return dv_w[DC - 1]

    n_wk = load_w_bf(din["wkT"], 0)
    sy_x2, dv_xb, pe_tp, dv_fm = {}, {}, {}, {}

    def emit_ln2(i):
        j = i % 2

        def f_ld2(e, i=i, j=j):
            if i < 2:
                e.wait_ge(s_dv.h, n_lnf)
            else:
                e.wait_ge(s_dv.h, dv_xb[i - 2])
            sx = k.s_x0 if j == 0 else k.s_x1
            sx.inc(e.dma_start(out=xs[j][:], in_=din["x_full"][i * 128:(i + 1) * 128, :]), 16)
            sy_x2[i] = (sx, sx.n)
        blk.sync(f_ld2)

        def f_n(v, i=i, j=j):
            v.wait_ge(sy_x2[i][0].h, sy_x2[i][1])
            if i < 2:
                v.wait_ge(s_dv.h, n_lnf)
            if i >= 2:
                v.wait_ge(s_pe.h, pe_tp[i - 2])
            s_dv.inc(v.tensor_scalar(xb[j][:], xs[j][:], nm32[:, i:i + 1],
                                     rs32[:, i:i + 1], ALU.add, ALU.mult))
            dv_xb[i] = s_dv.n
        blk.vector(f_n)

        def f_tp(t, i=i, j=j):
            t.wait_ge(s_dv.h, dv_xb[i])
            if i >= 1:
                t.wait_ge(s_dv.h, dv_fm[i - 1])
            for dc in range(DC):
                ps = ps_t1 if dc < 4 else ps_t2
                s_pe.inc(t.transpose(ps[:, (dc % 4) * 128:(dc % 4 + 1) * 128],
                                     xb[j][:, dc * 128:(dc + 1) * 128], k.identb[:]))
            pe_tp[i] = s_pe.n
        blk.tensor(f_tp)

        def f_ev(v, i=i):
            v.wait_ge(s_pe.h, pe_tp[i])
            for half in range(2):
                ps = ps_t1 if half == 0 else ps_t2
                s_dv.inc(v.tensor_copy(
                    xnfm.rearrange("p (dc t) -> p dc t", dc=DC)[:, half * 4:half * 4 + 4,
                                                                i * 128:(i + 1) * 128],
                    ps.rearrange("p (c t) -> p c t", c=4)))
            dv_fm[i] = s_dv.n
        blk.vector(f_ev)

    for i in range(8):
        emit_ln2(i)
    kk_pe, kk_ac = {}, {}
    ii = 0
    for tcc in range(8):
        if tcc < 6:
            for i in range(4 * tcc + 8, 4 * tcc + 12):
                emit_ln2(i)
        for oc in range(4):
            j = ii % 2

            def f_mmk(t, tcc=tcc, oc=oc, j=j, ii=ii):
                if ii == 0:
                    t.wait_ge(s_dv.h, n_wk)
                if oc == 0:
                    t.wait_ge(s_dv.h, dv_fm[4 * tcc + 3])
                if ii >= 2:
                    t.wait_ge(s_ac.h, kk_ac[ii - 2])
                for dc in range(DC):
                    mm = t.matmul(ps_q[j][:],
                                  wbf[:, dc * 512 + oc * 128:dc * 512 + (oc + 1) * 128],
                                  xnfm[:, dc * L + tcc * 512:dc * L + (tcc + 1) * 512],
                                  start=(dc == 0), stop=(dc == DC - 1))
                s_pe.inc(mm)
                kk_pe[ii] = s_pe.n
            blk.tensor(f_mmk)

            def f_evk(s, tcc=tcc, oc=oc, j=j, ii=ii):
                s.wait_ge(s_pe.h, kk_pe[ii])
                s_ac.inc(s.activation(Kfm[:, oc * L + tcc * 512:oc * L + (tcc + 1) * 512],
                                      ps_q[j][:], AF.Identity, bias=k.bk[:, oc:oc + 1]))
                kk_ac[ii] = s_ac.n
            blk.scalar(f_evk)
            ii += 1
    pe_k, ac_k = kk_pe[31], kk_ac[31]
    n_xnfm = dv_fm[NT - 1]

    def f_vm(v):
        s_dv.inc(v.memset(Vtm[:], 1.0))
    blk.vector(f_vm)
    n_vm = s_dv.n
    n_w = load_w_bf(din["wvT"], pe_k)
    pe_v, dv_v = {}, {}
    for tt in range(NT):
        j = tt % 2

        def f_vmm(t, tt=tt, j=j):
            if tt < 2:
                t.wait_ge(s_dv.h, max(n_w, n_vm))
                t.wait_ge(s_ac.h, ac_k)
            if tt >= 2:
                t.wait_ge(s_dv.h, dv_v[tt - 2])
            for dc in range(DC):
                mm = t.matmul(ps_q[j][:], xnfm[:, dc * L + tt * 128:dc * L + (tt + 1) * 128],
                              wbf[:, dc * 512:(dc + 1) * 512],
                              start=(dc == 0), stop=(dc == DC - 1))
            s_pe.inc(mm)
            pe_v[tt] = s_pe.n
        blk.tensor(f_vmm)

        def f_vev(v, tt=tt, j=j):
            v.wait_ge(s_pe.h, pe_v[tt])
            s_dv.inc(v.scalar_tensor_tensor(
                Vtm[:, tt * 520:(tt + 1) * 520].rearrange("p (h c) -> p h c", c=65)[:, :, 0:64],
                ps_q[j].rearrange("p (h c) -> p h c", c=64),
                1.0,
                k.bv_bc.rearrange("p (h c) -> p h c", c=64),
                ALU.mult, ALU.add))
            dv_v[tt] = s_dv.n
        blk.vector(f_vev)
    k.n_qkv = dv_v[NT - 1]
    rLN.close()

    def f_xm(v):
        v.wait_ge(s_dv.h, n_xnfm)
        for dc in range(DC):
            s_dv.inc(v.reduce_sum(xm[:, dc:dc + 1], xnfm[:, dc * L:(dc + 1) * L], axis=AX.X))
        v.wait_ge(s_dv.h, s_dv.n)
        s_dv.inc(v.tensor_scalar(xm[:], xm[:], 1.0 / L, None, ALU.mult))
    blk.vector(f_xm)
    n_xm = s_dv.n

    rG = ExitStack()
    _gating(nc, k, rG.enter_context, xm, n_xm)
    rG.close()

    gq_t = en(nc.sbuf_tensor("gq_t", [128, 4], F32))
    bqg_t = en(nc.sbuf_tensor("bqg_t", [128, 4], F32))
    n_gq = _myhg_sel(nc, k, en)
    pe_gq = {}
    for oc in range(4):
        for half in range(2):
            def f_gqm(t, oc=oc, half=half, n_gq=n_gq):
                if oc == 0 and half == 0:
                    t.wait_ge(s_dv.h, max(n_gq, k.ps_sm_free[0]))
                s_pe.inc(t.matmul(ps_sm[half * 64:half * 64 + 64, 32 + oc:33 + oc],
                                  k.onesr[0:1, 0:64],
                                  k.myhg[0:1, oc * 2 + half:oc * 2 + half + 1],
                                  start=True, stop=True))
                pe_gq[(oc, half)] = s_pe.n
            blk.tensor(f_gqm)

    def f_gqe(v):
        v.wait_ge(s_pe.h, pe_gq[(3, 1)])
        s_dv.inc(v.tensor_copy(gq_t[:], ps_sm[:, 32:36]))
        v.wait_ge(s_dv.h, s_dv.n)
        s_dv.inc(v.tensor_tensor(bqg_t[:], k.bq[:], gq_t[:], ALU.mult))
    blk.vector(f_gqe)
    n_gqe = s_dv.n
    k.ps_sm_free[0] = max(k.ps_sm_free[0], n_gqe)

    n_w = load_w_bf(din["wqT"], pe_v[NT - 1])
    pe_p, ac_e = {}, {}
    ii = 0
    for oc in range(4):
        for tcc in range(8):
            j = ii % 2

            def f_mm(t, oc=oc, tcc=tcc, j=j, ii=ii, n_w=n_w):
                if ii == 0:
                    t.wait_ge(s_dv.h, max(n_w, n_xnfm))
                if ii < 2:
                    t.wait_ge(s_dv.h, k.n_qkv)
                if ii >= 2:
                    t.wait_ge(s_ac.h, ac_e[ii - 2])
                for dc in range(DC):
                    mm = t.matmul(ps_q[j][:],
                                  wbf[:, dc * 512 + oc * 128:dc * 512 + (oc + 1) * 128],
                                  xnfm[:, dc * L + tcc * 512:dc * L + (tcc + 1) * 512],
                                  start=(dc == 0), stop=(dc == DC - 1))
                s_pe.inc(mm)
                pe_p[ii] = s_pe.n
            blk.tensor(f_mm)

            def f_ev(s, oc=oc, tcc=tcc, j=j, ii=ii):
                s.wait_ge(s_pe.h, pe_p[ii])
                if ii == 0:
                    s.wait_ge(s_dv.h, n_gqe)
                s_ac.inc(s.activation(Qfm[:, oc * L + tcc * 512:oc * L + (tcc + 1) * 512],
                                      ps_q[j][:], AF.Identity,
                                      bias=bqg_t[:, oc:oc + 1], scale=gq_t[:, oc:oc + 1]))
                ac_e[ii] = s_ac.n
            blk.scalar(f_ev)
            ii += 1

    _bh(nc, k, en)

    if k.debug:
        def f_dbg(e):
            e.wait_ge(s_dv.h, s_dv.n)
            e.wait_ge(s_ac.h, s_ac.n)
            k.s_st.inc(e.dma_start(out=k.dbg["d_xnfm"], in_=xnfm[:, 0:512]), 16)
            k.s_st.inc(e.dma_start(out=k.dbg["d_q"], in_=Qfm[:, 0:256]), 16)
            k.s_st.inc(e.dma_start(out=k.dbg["d_k"], in_=Kfm[:, 0:256]), 16)
            k.s_st.inc(e.dma_start(out=k.dbg["d_v"], in_=Vtm[:, 0:520]), 16)
            k.s_st.inc(e.dma_start(out=k.dbg["d_hg"], in_=k.hg[:]), 16)
            k.s_st.inc(e.dma_start(out=k.dbg["d_nB"][0:1, 0:8],
                                   in_=k.negB[0:1, 0:8]), 16)
            k.s_st.inc(e.dma_start(out=k.dbg["d_nB"][0:1, 8:16],
                                   in_=k.gk[0:1, 0:8]), 16)
        blk.sync(f_dbg)
    k.ps_sm = None


def _gating(nc, k, en, xm, n_xm):
    blk, din = k.blk, k.din
    s_sy, s_pe, s_dv, s_ac = k.s_sy, k.s_pe, k.s_dv, k.s_ac
    ps_sm = k.ps_sm
    gsc = k.gsc

    w1 = [en(nc.sbuf_tensor(f"gw1{j}", [128, DH], F32)) for j in range(2)]
    brow = en(nc.sbuf_tensor("gbrow", [1, DH], F32))
    arow = en(nc.sbuf_tensor("garow", [1, DH], F32))
    act1 = en(nc.sbuf_tensor("gact1", [128, 4], F32))
    w2 = en(nc.sbuf_tensor("gw2", [128, 16], F32))
    bsm = en(nc.sbuf_tensor("gbsm", [16, 1], F32))
    pol = en(nc.sbuf_tensor("gpol", [1, 4], F32))

    def mlp_in(w_dram, b_dram_flat, wait_dv):
        def f_b(e):
            e.wait_ge(k.s_gb.h, k.s_gb.n)
            k.s_gb.inc(e.dma_start(out=brow[:],
                                 in_=b_dram_flat.rearrange("(i f) -> i f", i=1)), 16)
        blk.sync(f_b)
        n_b = k.s_gb.n
        sy_w, pe_c = {}, {}
        for dc in range(DC):
            jj = dc % 2

            def f_l(e, dc=dc, jj=jj):
                if dc >= 2:
                    e.wait_ge(s_pe.h, pe_c[dc - 2])
                gw = k.s_gw0 if jj == 0 else k.s_gw1
                gw.inc(e.dma_start(out=w1[jj][:], in_=w_dram[dc * 128:(dc + 1) * 128, :]), 16)
                sy_w[dc] = (gw, gw.n)
            blk.sync(f_l)

            def f_m(t, dc=dc, jj=jj):
                t.wait_ge(sy_w[dc][0].h, sy_w[dc][1])
                if dc == 0:
                    t.wait_ge(s_dv.h, max(wait_dv, k.ps_sm_free[0]))
                mm = t.matmul(ps_sm[0:1, 0:512], xm[:, dc:dc + 1], w1[jj][:],
                              start=(dc == 0), stop=(dc == DC - 1))
                s_pe.inc(mm)
                pe_c[dc] = s_pe.n
            blk.tensor(f_m)

        def f_e(v):
            v.wait_ge(s_pe.h, pe_c[DC - 1])
            v.wait_ge(k.s_gb.h, n_b)
            s_dv.inc(v.tensor_tensor(arow[:], ps_sm[0:1, 0:512], brow[:], ALU.add))
        blk.vector(f_e)
        n_row = s_dv.n
        def f_t(t):
            t.wait_ge(s_dv.h, n_row)
            for cc in range(4):
                mm = t.transpose(ps_sm[0:128, 4 + cc:5 + cc],
                                 arow[0:1, cc * 128:(cc + 1) * 128], k.identf[0:1, 0:1])
            s_pe.inc(mm)
        blk.tensor(f_t)
        n_tp = s_pe.n

        def f_e2(v):
            v.wait_ge(s_pe.h, n_tp)
            s_dv.inc(v.tensor_copy(act1[:], ps_sm[:, 4:8]))
        blk.vector(f_e2)
        k.ps_sm_free[0] = s_dv.n
        return s_dv.n

    def softmax_row(src_ap, n_el, out_ap, wait_pe):
        def f1(v):
            v.wait_ge(s_pe.h, wait_pe)
            s_dv.inc(v.tensor_copy(gsc[0:1, 0:n_el], src_ap))
            v.wait_ge(s_dv.h, s_dv.n)
            s_dv.inc(v.reduce_max(gsc[0:1, 32:33], gsc[0:1, 0:n_el], axis=AX.X, negate=True))
        blk.vector(f1)
        n1 = s_dv.n
        k.ps_sm_free[0] = max(k.ps_sm_free[0], n1)

        def f2(s):
            s.wait_ge(s_dv.h, n1)
            s_ac.inc(s.activation(gsc[0:1, 0:n_el], gsc[0:1, 0:n_el], AF.Exp,
                                  bias=gsc[0:1, 32:33]))
        blk.scalar(f2)
        n2 = s_ac.n

        def f3(v):
            v.wait_ge(s_ac.h, n2)
            s_dv.inc(v.reduce_sum(gsc[0:1, 33:34], gsc[0:1, 0:n_el], axis=AX.X))
            v.wait_ge(s_dv.h, s_dv.n)
            s_dv.inc(v.reciprocal(gsc[0:1, 34:35], gsc[0:1, 33:34]))
            v.wait_ge(s_dv.h, s_dv.n)
            s_dv.inc(v.tensor_scalar(out_ap, gsc[0:1, 0:n_el], gsc[0:1, 34:35], None, ALU.mult))
        blk.vector(f3)
        return s_dv.n

    def logits16(w2T_dram, b_dram, nb_rows, src_cols, wait_ac, psl):
        sy_w, pe_c = {}, {}
        for cc in range(4):
            def f_l(e, cc=cc):
                if cc >= 1:
                    e.wait_ge(s_pe.h, pe_c[cc - 1])
                else:
                    e.wait_ge(k.s_gl.h, k.s_gl.n)
                k.s_gl.inc(e.dma_start(out=w2[:, 0:nb_rows],
                                     in_=w2T_dram[cc * 128:(cc + 1) * 128, :]), 16)
                sy_w[cc] = k.s_gl.n
            blk.sync(f_l)

            def f_m(t, cc=cc):
                t.wait_ge(k.s_gl.h, sy_w[cc])
                t.wait_ge(s_ac.h, wait_ac)
                s_pe.inc(t.matmul(psl, w2[:, 0:nb_rows], src_cols[:, cc:cc + 1],
                                  start=(cc == 0), stop=(cc == 3)))
                pe_c[cc] = s_pe.n
            blk.tensor(f_m)

        def f_b(e):
            e.wait_ge(k.s_gb.h, k.s_gb.n)
            k.s_gb.inc(e.dma_start(out=bsm[0:nb_rows, :], in_=b_dram), 16)
        blk.sync(f_b)
        n_b = k.s_gb.n

        def f_e(v):
            v.wait_ge(s_pe.h, pe_c[3])
            v.wait_ge(k.s_gb.h, n_b)
            s_dv.inc(v.tensor_tensor(bsm[0:nb_rows, :], psl, bsm[0:nb_rows, :], ALU.add))
        blk.vector(f_e)
        return s_dv.n

    n1 = mlp_in(din["sw1T"], din["sb1"], n_xm)

    def f_g(s):
        s.wait_ge(s_dv.h, n1)
        s_ac.inc(s.activation(act1[:], act1[:], AF.Gelu))
    blk.scalar(f_g)
    n_g = s_ac.n
    n_lg = logits16(din["sw2T"], din["sb2"], 3, act1, n_g, ps_sm[0:3, 8:9])

    def f_t3(t):
        t.wait_ge(s_dv.h, n_lg)
        s_pe.inc(t.transpose(ps_sm[0:1, 12:15], bsm[0:3, :], k.identf[0:3, 0:3]))
    blk.tensor(f_t3)
    n_t3 = s_pe.n
    n_pol = softmax_row(ps_sm[0:1, 12:15], 3, pol[0:1, 0:3], n_t3)
    k.ps_sm_free[0] = max(k.ps_sm_free[0], n_pol)

    last = n_pol
    for p in range(P3):
        np1 = mlp_in(din["pw1T"][p * D:(p + 1) * D, :],
                     din["pb1"][p * DH:(p + 1) * DH], max(last, n_xm))

        def f_gp(s, np1=np1):
            s.wait_ge(s_dv.h, np1)
            s_ac.inc(s.activation(act1[:], act1[:], AF.Gelu))
        blk.scalar(f_gp)
        n_gp = s_ac.n
        n_gl = logits16(din["pw2T"][p * DH:(p + 1) * DH, :],
                        din["pb2"][p * H:(p + 1) * H].rearrange("(p i) -> p i", p=16),
                        16, act1, n_gp, ps_sm[0:16, 8:9])

        def f_th(t, n_gl=n_gl):
            t.wait_ge(s_dv.h, n_gl)
            s_pe.inc(t.transpose(ps_sm[0:1, 16:32], bsm[0:16, :], k.identf[0:16, 0:16]))
        blk.tensor(f_th)
        n_th = s_pe.n
        n_gt = softmax_row(ps_sm[0:1, 16:32], 16, gsc[0:1, 40:56], n_th)

        def f_acc(v, p=p, n_gt=n_gt):
            v.wait_ge(s_dv.h, n_gt)
            if p == 0:
                s_dv.inc(v.tensor_scalar(k.hg[:], gsc[0:1, 40:56], pol[0:1, p:p + 1], None, ALU.mult))
            else:
                s_dv.inc(v.tensor_scalar(gsc[0:1, 16:32], gsc[0:1, 40:56], pol[0:1, p:p + 1],
                                         None, ALU.mult))
                v.wait_ge(s_dv.h, s_dv.n)
                s_dv.inc(v.tensor_tensor(k.hg[:], k.hg[:], gsc[0:1, 16:32], ALU.add))
        blk.vector(f_acc)
        last = s_dv.n
    k.n_hg = last
    k.ps_sm_free[0] = max(k.ps_sm_free[0], last)


def _myhg_sel(nc, k, en):
    blk, din = k.blk, k.din
    s_sy, s_pe, s_dv = k.s_sy, k.s_pe, k.s_dv
    ps_sm, stat = k.ps_sm, k.stat
    hsel = k.hsel
    k.myhg = en(nc.sbuf_tensor("bh_myhg", [1, 8], F32))
    n_hs = k.n_cst

    def f_hgt(t):
        t.wait_ge(s_dv.h, max(k.n_hg, k.ps_sm_free[0]))
        s_pe.inc(t.transpose(ps_sm[0:16, 0:1], k.hg[:], k.identf[0:1, 0:1]))
    blk.tensor(f_hgt)
    n_hgt = s_pe.n

    def f_e1(v):
        v.wait_ge(s_pe.h, n_hgt)
        s_dv.inc(v.tensor_copy(stat[0:16, 6:7], ps_sm[0:16, 0:1]))
    blk.vector(f_e1)
    n_e1 = s_dv.n

    def f_sel(t):
        t.wait_ge(s_dv.h, n_e1)
        t.wait_ge(k.s_cst.h, n_hs)
        s_pe.inc(t.matmul(ps_sm[0:8, 2:3], hsel[:], stat[0:16, 6:7], start=True, stop=True))
    blk.tensor(f_sel)
    n_sel = s_pe.n

    def f_e2(v):
        v.wait_ge(s_pe.h, n_sel)
        s_dv.inc(v.tensor_copy(stat[0:8, 7:8], ps_sm[0:8, 2:3]))
    blk.vector(f_e2)
    n_e2 = s_dv.n

    def f_t8(t):
        t.wait_ge(s_dv.h, n_e2)
        s_pe.inc(t.transpose(ps_sm[0:1, 4:12], stat[0:8, 7:8], k.identf[0:8, 0:8]))
    blk.tensor(f_t8)
    n_t8 = s_pe.n

    def f_e3(v):
        v.wait_ge(s_pe.h, n_t8)
        s_dv.inc(v.tensor_scalar(k.myhg[:], ps_sm[0:1, 4:12], 0.125, None, ALU.mult))
    blk.vector(f_e3)
    k.ps_sm_free[0] = max(k.ps_sm_free[0], s_dv.n)
    return s_dv.n


def _bh(nc, k, en):
    blk = k.blk
    s_pe, s_dv, s_ac = k.s_pe, k.s_dv, k.s_ac
    ps_sm = k.ps_sm
    mxq = en(nc.sbuf_tensor("bh_mxq", [128, 4], F32))
    mxk = en(nc.sbuf_tensor("bh_mxk", [128, 4], F32))
    nrm = en(nc.sbuf_tensor("bh_nrm", [4, 256], F32))
    n2 = en(nc.sbuf_tensor("bh_n2", [4, 8], F32))
    brow = en(nc.sbuf_tensor("bh_brow", [1, 4], F32))

    def f_rm(v):
        v.wait_ge(s_dv.h, k.n_qkv)
        v.wait_ge(s_ac.h, s_ac.n)
        for ch in range(4):
            s_dv.inc(v.reduce_max(mxq[:, ch:ch + 1], k.Qfm[:, ch * L:(ch + 1) * L],
                                  axis=AX.X, apply_absolute_value=True))
            s_dv.inc(v.reduce_max(mxk[:, ch:ch + 1], k.Kfm[:, ch * L:(ch + 1) * L],
                                  axis=AX.X, apply_absolute_value=True))
    blk.vector(f_rm)
    n_rm = s_dv.n

    def f_tq(t):
        t.wait_ge(s_dv.h, max(n_rm, k.ps_sm_free[0]))
        s_pe.inc(t.transpose(ps_sm[0:4, 0:128], mxq[:, 0:4], k.identf[:]))
        s_pe.inc(t.transpose(ps_sm[0:4, 128:256], mxk[:, 0:4], k.identf[:]))
    blk.tensor(f_tq)
    n_tq = s_pe.n

    def f_ev(v):
        v.wait_ge(s_pe.h, n_tq)
        s_dv.inc(v.tensor_copy(nrm[:], ps_sm[0:4, 0:256]))
        v.wait_ge(s_dv.h, s_dv.n)
        s_dv.inc(v.tensor_tensor(nrm[:], nrm[:], nrm[:], ALU.mult))
        v.wait_ge(s_dv.h, s_dv.n)
        for q4 in range(4):
            s_dv.inc(v.reduce_sum(n2[:, q4:q4 + 1], nrm[:, q4 * 64:(q4 + 1) * 64],
                                  axis=AX.X))
        v.wait_ge(s_dv.h, s_dv.n)
        s_dv.inc(v.tensor_tensor(n2[:, 0:2], n2[:, 0:2], n2[:, 2:4], ALU.mult))
        v.wait_ge(s_dv.h, s_dv.n)
        s_dv.inc(v.reduce_max(n2[:, 4:5], n2[:, 0:2], axis=AX.X))
    blk.vector(f_ev)
    n_ev = s_dv.n
    k.ps_sm_free[0] = max(k.ps_sm_free[0], n_ev)

    def f_sq(s):
        s.wait_ge(s_dv.h, n_ev)
        s_ac.inc(s.activation(n2[:, 5:6], n2[:, 4:5], AF.Sqrt))
    blk.scalar(f_sq)
    n_sq = s_ac.n

    def f_ng(v):
        v.wait_ge(s_ac.h, n_sq)
        s_dv.inc(v.tensor_scalar(n2[:, 6:7], n2[:, 5:6], -1.0, None, ALU.mult))
    blk.vector(f_ng)
    n_ng = s_dv.n

    def f_tp2(t):
        t.wait_ge(s_dv.h, max(n_ng, k.ps_sm_free[0]))
        s_pe.inc(t.transpose(ps_sm[0:1, 300:304], n2[:, 6:7], k.identf[0:4, 0:4]))
    blk.tensor(f_tp2)
    n_tp2 = s_pe.n

    def f_ev2(v):
        v.wait_ge(s_pe.h, n_tp2)
        s_dv.inc(v.tensor_copy(brow[:], ps_sm[0:1, 300:304]))
    blk.vector(f_ev2)
    n_ev2 = s_dv.n

    def f_bc(t):
        t.wait_ge(s_dv.h, n_ev2)
        s_pe.inc(t.matmul(ps_sm[:, 304:308], k.onesr[:], brow[:], start=True, stop=True))
    blk.tensor(f_bc)
    n_bc = s_pe.n

    def f_bc2(v):
        v.wait_ge(s_pe.h, n_bc)
        s_dv.inc(v.tensor_copy(k.negB[:, 0:4], ps_sm[:, 304:308]))
    blk.vector(f_bc2)
    k.n_bh = s_dv.n
    k.ps_sm_free[0] = max(k.ps_sm_free[0], s_dv.n)


def _p3(nc, k, en):
    blk = k.blk
    s_sy, s_gp, s_cc = k.s_sy, k.s_gp, k.s_cc
    s_pe, s_dv, s_ac = k.s_pe, k.s_dv, k.s_ac
    Qfm, Kfm, Vtm = k.Qfm, k.Kfm, k.Vtm

    ps_s = [en(nc.psum_tensor(f"ps_s{j}", [128, 1024], F32)) for j in range(2)]
    ps_pv = [[en(nc.psum_tensor(f"ps_pv{h}_{j}", [128, 260], F32)) for j in range(2)]
             for h in range(2)]
    probs = [en(nc.sbuf_tensor(f"probs{j}", [128, 1024], BF16)) for j in range(2)]
    aoh2 = en(nc.sbuf_tensor("aoh2", [128, NT * 128], BF16))
    aT = en(nc.sbuf_tensor("aT", [128, 4 * L], BF16))
    rcp = en(nc.sbuf_tensor("rcp", [128, 8], F32))
    k.aT = aT
    QC = 8

    hstr2 = [en(nc.sbuf_tensor(f"hstr2{j}", [128, 1024], F32)) for j in range(2)]
    h_qa = en(nc.sbuf_tensor("h_qa", [128, 1024], F32))
    h_qi = en(nc.sbuf_tensor("h_qi", [128, 1024], I32))
    h_qo = [en(nc.sbuf_tensor(f"h_qo{j}", [128, 1024], BF16)) for j in range(2)]
    k.hq_out = {}
    hq_ld, hq_dv1, hq_dvc = {}, {}, {}

    def emit_hquant(ci):
        th2, ic = divmod(ci, IC)
        j2 = ci % 2
        sh = k.s_h0 if j2 == 0 else k.s_h1

        def f_hl(gp, ci=ci, th2=th2, ic=ic, j2=j2, sh=sh):
            if ci < 2:
                gp.wait_ge(k.s_sp0.h, k.n_sp0)
                gp.wait_ge(k.s_sp1.h, k.n_sp1)
            else:
                gp.wait_ge(s_dv.h, hq_dv1[ci - 2])
            sh.inc(gp.dma_start(
                out=hstr2[j2][:],
                in_=k.h_spill.ap()[ic * 128:(ic + 1) * 128,
                                   th2 * 1024:(th2 + 1) * 1024]), 16)
            hq_ld[ci] = (sh, sh.n)
        blk.gpsimd(f_hl)

        def f_hq(v, ci=ci, j2=j2):
            v.wait_ge(hq_ld[ci][0].h, hq_ld[ci][1])
            if ci < 2:
                v.wait_ge(s_dv.h, k.n_rsh)
            if ci >= 2:
                st, n = k.hq_out[ci - 2]
                v.wait_ge(st.h, n)
            s_dv.inc(v.tensor_scalar(h_qa[:], hstr2[j2][:], k.rsh[:], QHI,
                                     ALU.mult, ALU.min))
            hq_dv1[ci] = s_dv.n
            v.wait_ge(s_dv.h, s_dv.n)
            s_dv.inc(v.tensor_scalar(h_qa[:], h_qa[:], QLO, RC2C, ALU.max, ALU.add))
            v.wait_ge(s_dv.h, s_dv.n)
            s_dv.inc(v.tensor_scalar(h_qi[:], h_qa[:], RC2, EPS5, ALU.subtract, ALU.add))
            v.wait_ge(s_dv.h, s_dv.n)
            s_dv.inc(v.tensor_copy(h_qo[j2][:], h_qi[:]))
            v.wait_ge(s_dv.h, s_dv.n)
            s_dv.inc(v.tensor_scalar(h_qo[j2][:], h_qo[j2][:], 129.0, None, ALU.subtract))
            hq_dvc[ci] = s_dv.n
        blk.vector(f_hq)

        def f_ho(e, ci=ci, j2=j2):
            e.wait_ge(s_dv.h, hq_dvc[ci])
            so = k.s_ho if j2 == 0 else k.s_ho2
            so.inc(e.dma_start(
                out=k.ihg_d.ap()[ci * 128:(ci + 1) * 128, :], in_=h_qo[j2][:]), 16)
            k.hq_out[ci] = (so, so.n)
        blk.sync(f_ho)

    hq_next = [0]

    ac_ex = {}
    dv_mk = {}
    dv_fin = {}
    sy_tp = {}
    prev_ac_end = {-1: 0}
    prev_pe_end = {-1: 0}

    for pair in range(4):
        ch = pair
        for qc in range(QC):
            bi = pair * QC + qc
            kts = (qc + 1) * 4
            diag0 = qc * 4
            setid = bi % 2

            def qlo(kt, diag0=diag0):
                return max(0, kt - diag0) * 128

            pe0 = s_pe.n
            peQK, pePV = {}, {}
            for kt in range(kts):
                peQK[kt] = pe0 + (1 if kt == 0 else 2 * kt)
                pePV[kt] = pe0 + (2 * kt + 3 if kt < kts - 1 else 2 * kts)

            def f_ex(s, pair=pair, qc=qc, bi=bi, kts=kts, diag0=diag0,
                     peQK=dict(peQK)):
                for kt in range(kts):
                    s.wait_ge(s_pe.h, peQK[kt])
                    if kt < 2:
                        s.wait_ge(s_pe.h, prev_pe_end[bi - 1])
                    ql = qlo(kt)
                    s_ac.inc(s.activation(
                        probs[kt % 2].rearrange("p (h q) -> p h q", h=2)[:, :, ql:512],
                        ps_s[kt % 2].rearrange("p (h q) -> p h q", h=2)[:, :, ql:512],
                        AF.Exp, bias=k.negB[:, pair:pair + 1]))
                    ac_ex[(bi, kt)] = s_ac.n
            blk.scalar(f_ex)
            prev_ac_end[bi] = s_ac.n

            def f_mk(v, bi=bi, kts=kts, diag0=diag0):
                for kt in range(diag0, kts):
                    qsub = kt - diag0
                    v.wait_ge(s_ac.h, ac_ex[(bi, kt)])
                    pb = probs[kt % 2]
                    for hh2 in range(2):
                        o = hh2 * 512 + qsub * 128
                        s_dv.inc(v.tensor_tensor(pb[:, o:o + 128], pb[:, o:o + 128],
                                                 k.trimask[:], ALU.mult))
                    dv_mk[(bi, kt)] = s_dv.n
            blk.vector(f_mk)

            def f_pe(t, pair=pair, qc=qc, bi=bi, kts=kts, diag0=diag0, ch=ch,
                     setid=setid, peQK=dict(peQK), pePV=dict(pePV)):
                if bi == 0:
                    t.wait_ge(s_dv.h, k.n_bh)
                if bi >= 2:
                    t.wait_ge(s_dv.h, dv_fin[bi - 2])
                def emit_qk(kt):
                    if kt >= 2:
                        t.wait_ge(s_ac.h, ac_ex[(bi, kt - 2)])
                    elif bi > 0:
                        t.wait_ge(s_ac.h, prev_ac_end[bi - 1])
                    ql = qlo(kt)
                    assert s_pe.n == peQK[kt] - 1
                    for hh2 in range(2):
                        mm = t.matmul(
                            ps_s[kt % 2][:, hh2 * 512 + ql:(hh2 + 1) * 512],
                            Kfm[hh2 * 64:hh2 * 64 + 64,
                                ch * L + kt * 128:ch * L + (kt + 1) * 128],
                            Qfm[hh2 * 64:hh2 * 64 + 64,
                                ch * L + qc * 512 + ql:ch * L + (qc + 1) * 512],
                            start=True, stop=True, tile_position=(hh2 * 64, 0))
                    s_pe.inc(mm)
                    assert s_pe.n == peQK[kt]

                def emit_pv(kt):
                    if kt >= diag0:
                        t.wait_ge(s_dv.h, dv_mk[(bi, kt)])
                    else:
                        t.wait_ge(s_ac.h, ac_ex[(bi, kt)])
                    pb = probs[kt % 2]
                    mm = None
                    for hh2 in range(2):
                        h = pair * 2 + hh2
                        for qsub in range(max(0, kt - diag0), 4):
                            mm = t.matmul(
                                ps_pv[hh2][setid][:, qsub * 65:qsub * 65 + 65],
                                pb[:, hh2 * 512 + qsub * 128:hh2 * 512 + (qsub + 1) * 128],
                                Vtm[:, kt * 520 + h * 65:kt * 520 + (h + 1) * 65],
                                start=(kt == 0 and qsub == 0),
                                stop=(kt == diag0 + qsub), skip_group_check=True)
                    s_pe.inc(mm)
                    assert s_pe.n == pePV[kt]

                emit_qk(0)
                for kt in range(1, kts):
                    emit_qk(kt)
                    emit_pv(kt - 1)
                emit_pv(kts - 1)
            blk.tensor(f_pe)
            prev_pe_end[bi] = s_pe.n
            n_pv = s_pe.n

            def f_fin(v, pair=pair, qc=qc, bi=bi, setid=setid, n_pv=n_pv):
                v.wait_ge(s_pe.h, n_pv)
                if pair >= 1 and qc == 0:
                    v.wait_ge(k.s_tp.h, sy_tp[pair - 1])
                for hh2 in range(2):
                    for qsub in range(4):
                        s_dv.inc(v.reciprocal(
                            rcp[:, hh2 * 4 + qsub:hh2 * 4 + qsub + 1],
                            ps_pv[hh2][setid][:, qsub * 65 + 64:qsub * 65 + 65]))
                v.wait_ge(s_dv.h, s_dv.n)
                for hh2 in range(2):
                    for qsub in range(4):
                        qt = qc * 4 + qsub
                        s_dv.inc(v.tensor_scalar(
                            aoh2[:, qt * 128 + hh2 * 64:qt * 128 + hh2 * 64 + 64],
                            ps_pv[hh2][setid][:, qsub * 65:qsub * 65 + 64],
                            rcp[:, hh2 * 4 + qsub:hh2 * 4 + qsub + 1], None, ALU.mult))
                dv_fin[bi] = s_dv.n
            blk.vector(f_fin)

            for _hq in range(2):
                if hq_next[0] < 2 * IC:
                    emit_hquant(hq_next[0])
                    hq_next[0] += 1

        n_pair = s_dv.n

        def f_tp(e, pair=pair, ch=ch, n_pair=n_pair):
            e.wait_ge(s_dv.h, n_pair)
            for qt in range(NT):
                k.s_tp.inc(e.dma_start_transpose(
                    aT[:, ch * L + qt * 128:ch * L + (qt + 1) * 128],
                    aoh2[:, qt * 128:(qt + 1) * 128]), 16)
        blk.sync(f_tp)
        sy_tp[pair] = k.s_tp.n

        def f_st(gp, pair=pair, ch=ch):
            gp.wait_ge(k.s_tp.h, sy_tp[pair])
            k.s_cca.inc(gp.dma_start(out=k.cc_ain.ap()[ch * 128:(ch + 1) * 128, :],
                                     in_=aT[:, ch * L:(ch + 1) * L]), 16)
            gp.wait_ge(k.s_cca.h, k.s_cca.n)
            s_cc.inc(gp.collective_compute(
                "AllGather", ALU.bypass,
                replica_groups=[[0, 1], [2, 3], [4, 5], [6, 7]],
                ins=[k.cc_ain.ap()[ch * 128:(ch + 1) * 128, :].opt()],
                outs=[k.cc_aout.ap()[ch * 256:(ch + 1) * 256, :].opt()]))
        blk.gpsimd(f_st)
    k.n_ccag = s_cc.n

    if k.debug:
        def f_dbg(e):
            e.wait_ge(s_dv.h, s_dv.n)
            e.wait_ge(k.s_tp.h, n_at)
            k.s_st.inc(e.dma_start(out=k.dbg["d_ao"], in_=aoh2[:, 0:512]), 16)
            k.s_st.inc(e.dma_start(out=k.dbg["d_at"], in_=aT[:, 0:256]), 16)
        blk.sync(f_dbg)


def _p4(nc, k, en):
    blk, din = k.blk, k.din
    s_sy, s_gp, s_pe, s_dv, s_ac = k.s_sy, k.s_gp, k.s_pe, k.s_dv, k.s_ac
    stat = k.stat

    iwd = en(nc.sbuf_tensor("iwd", [128, IC * D], BF16))
    sdb = en(nc.sbuf_tensor("sdb", [128, D], F32))

    with ExitStack() as rd:
        enD = rd.enter_context
        wdl = [enD(nc.sbuf_tensor(f"wdl{j}", [128, II], F32)) for j in range(2)]
        qa_d = enD(nc.sbuf_tensor("qa_d", [128, II], F32))
        qi_d = enD(nc.sbuf_tensor("qi_d", [128, II], I32))
        qn_d = enD(nc.sbuf_tensor("qn_d", [128, II], BF16))
        qt_d = enD(nc.sbuf_tensor("qt_d", [128, II], BF16))
        scdw = enD(nc.sbuf_tensor("scdw", [128, 8], F32))
        scds = enD(nc.sbuf_tensor("scds", [128, 8], F32))
        ps_sm = enD(nc.psum_tensor("ps_sm4", [128, 512], F32))
        ps_td = [enD(nc.psum_tensor(f"ps_td{j}", [128, 1024], BF16)) for j in range(2)]

        sy_d, dv_m = {}, {}
        for db in range(DC):
            j = db % 2

            def f_l(e, db=db, j=j):
                if db >= 2:
                    e.wait_ge(s_dv.h, dv_m[db - 2])
                sw = k.s_wst if j == 0 else k.s_ws2
                sw.inc(e.dma_start(out=wdl[j][:],
                                   in_=din["wd"][db * 128:(db + 1) * 128, :]), 16)
                sy_d[db] = (sw, sw.n)
            blk.sync(f_l)

            def f_m(v, db=db, j=j):
                v.wait_ge(sy_d[db][0].h, sy_d[db][1])
                s_dv.inc(v.reduce_max(scdw[:, db:db + 1], wdl[j][:], axis=AX.X,
                                      apply_absolute_value=True))
                dv_m[db] = s_dv.n
            blk.vector(f_m)
        n_wdl_free = dv_m[DC - 1]

        def f_sc(v):
            v.wait_ge(s_dv.h, n_wdl_free)
            s_dv.inc(v.tensor_scalar(scdw[:], scdw[:], 1.0 / QMAX, None, ALU.mult))
        blk.vector(f_sc)

        shv = enD(nc.sbuf_tensor("shv", [128, 1], F32))
        k.ps_sm = ps_sm
        k.ps_sm_free = [0]
        bcast_scalar(k, blk, k.sxh[0:1, 1:2], shv[:])

        def f_ss(v):
            v.wait_ge(s_dv.h, s_dv.n)
            s_dv.inc(v.tensor_scalar(scds[:], scdw[:], shv[:], None, ALU.mult))
            v.wait_ge(s_dv.h, s_dv.n)
            s_dv.inc(v.reciprocal(scdw[:], scdw[:]))
        blk.vector(f_ss)
        n_scr = s_dv.n

        dv_bc = {}
        for db in range(DC):
            def f_t1(t, db=db):
                t.wait_ge(s_dv.h, n_scr if db == 0 else dv_bc[db - 1])
                t.wait_ge(s_dv.h, k.ps_sm_free[0])
                s_pe.inc(t.transpose(ps_sm[0:1, 128:256], scds[:, db:db + 1], k.identf[:]))
                pe_a[db] = s_pe.n
            pe_a = {}
            blk.tensor(f_t1)

            def f_e1(v, db=db):
                v.wait_ge(s_pe.h, pe_a[db])
                s_dv.inc(v.tensor_copy(k.trow[:], ps_sm[0:1, 128:256]))
                dv_a[db] = s_dv.n
            dv_a = {}
            blk.vector(f_e1)

            def f_t2(t, db=db):
                t.wait_ge(s_dv.h, dv_a[db])
                s_pe.inc(t.matmul(ps_sm[:, 128:256], k.onesr[:], k.trow[:],
                                  start=True, stop=True))
                pe_b[db] = s_pe.n
            pe_b = {}
            blk.tensor(f_t2)

            def f_e2(v, db=db):
                v.wait_ge(s_pe.h, pe_b[db])
                s_dv.inc(v.tensor_copy(sdb[:, db * 128:(db + 1) * 128], ps_sm[:, 128:256]))
                dv_bc[db] = s_dv.n
            blk.vector(f_e2)
        k.ps_sm_free[0] = dv_bc[DC - 1]

        sy_d2, dv_q1, dv_q2, pe_td, dv_te = {}, {}, {}, {}, {}
        for db in range(DC):
            j = db % 2

            def f_l2(e, db=db, j=j):
                if db < 2:
                    e.wait_ge(s_dv.h, n_wdl_free)
                else:
                    e.wait_ge(s_dv.h, dv_q1[db - 2])
                sw = k.s_wst if j == 0 else k.s_ws2
                sw.inc(e.dma_start(out=wdl[j][:],
                                   in_=din["wd"][db * 128:(db + 1) * 128, :]), 16)
                sy_d2[db] = (sw, sw.n)
            blk.sync(f_l2)

            def f_q(v, db=db, j=j):
                v.wait_ge(sy_d2[db][0].h, sy_d2[db][1])
                if db >= 1:
                    v.wait_ge(s_pe.h, pe_td[(db - 1, 2)])
                s_dv.inc(v.tensor_scalar(qa_d[:], wdl[j][:], scdw[:, db:db + 1],
                                         QHI, ALU.mult, ALU.min))
                dv_q1[db] = s_dv.n
                v.wait_ge(s_dv.h, s_dv.n)
                s_dv.inc(v.tensor_scalar(qa_d[:], qa_d[:], QLO, RC2C, ALU.max, ALU.add))
                v.wait_ge(s_dv.h, s_dv.n)
                s_dv.inc(v.tensor_scalar(qi_d[:], qa_d[:], RC2, EPS5, ALU.subtract, ALU.add))
                v.wait_ge(s_dv.h, s_dv.n)
                s_dv.inc(v.tensor_copy(qn_d[:], qi_d[:]))
                v.wait_ge(s_dv.h, s_dv.n)
                s_dv.inc(v.tensor_scalar(qt_d[:], qn_d[:], 129.0, None, ALU.subtract))
                dv_q2[db] = s_dv.n
            blk.vector(f_q)

            for r3 in range(3):
                i0, i1 = r3 * 8, min((r3 + 1) * 8, IC)
                gr = db * 3 + r3
                pj = gr % 2

                def f_td(t, db=db, r3=r3, i0=i0, i1=i1, gr=gr, pj=pj):
                    t.wait_ge(s_dv.h, dv_q2[db])
                    if gr >= 2:
                        t.wait_ge(s_dv.h, dv_te[divmod(gr - 2, 3)])
                    for ic in range(i0, i1):
                        s_pe.inc(t.transpose(
                            ps_td[pj][:, (ic - i0) * 128:(ic - i0 + 1) * 128],
                            qt_d[:, ic * 128:(ic + 1) * 128], k.identb[:]))
                    pe_td[(db, r3)] = s_pe.n
                blk.tensor(f_td)

                def f_te(v, db=db, r3=r3, i0=i0, i1=i1, pj=pj):
                    v.wait_ge(s_pe.h, pe_td[(db, r3)])
                    nn = i1 - i0
                    s_dv.inc(v.tensor_copy(
                        iwd.rearrange("p (ic d) -> p ic d", d=D)[:, i0:i1,
                                                                 db * 128:(db + 1) * 128],
                        ps_td[pj].rearrange("p (c t) -> p c t", c=8)[:, 0:nn, :]))
                    dv_te[(db, r3)] = s_dv.n
                blk.vector(f_te)
        n_iwd = dv_te[(DC - 1, 2)]
        k.ps_sm = None
    k.barrier()

    ihg = en(nc.sbuf_tensor("ihg", [128, IC * 1024], BF16))
    ps_d = [en(nc.psum_tensor(f"ps_d{j}", [128, 512], F32)) for j in range(8)]

    pe_dn, dv_dn = {}, {}
    for th2 in range(2):
        gp_h = {}
        for ic in range(IC):
            j = ic % 2
            ci = th2 * IC + ic

            def f_hl(gp, th2=th2, ic=ic, j=j, ci=ci):
                if th2 >= 1 and ic < 2:
                    gp.wait_ge(s_pe.h, pe_dn[(th2 - 1, 3)])
                st, n = k.hq_out[ci]
                gp.wait_ge(st.h, n)
                sh = k.s_h0 if j == 0 else k.s_h1
                sh.inc(gp.dma_start(
                    out=ihg[:, ic * 1024:(ic + 1) * 1024],
                    in_=k.ihg_d.ap()[ci * 128:(ci + 1) * 128, :]), 16)
                gp_h[ic] = (sh, sh.n)
            blk.gpsimd(f_hl)
        ld_h0, ld_h1 = k.s_h0.n, k.s_h1.n

        for g4 in range(4):
            tg = th2 * 4 + g4
            bs = (g4 % 2) * 4

            def f_mm(t, th2=th2, g4=g4, tg=tg, bs=bs, ld_h0=ld_h0, ld_h1=ld_h1):
                if g4 == 0:
                    t.wait_ge(k.s_h0.h, ld_h0)
                    t.wait_ge(k.s_h1.h, ld_h1)
                if tg == 0:
                    t.wait_ge(s_dv.h, n_iwd)
                if g4 >= 2 or th2 >= 1:
                    pt = (th2, g4 - 2) if g4 >= 2 else (th2 - 1, g4 + 2)
                    t.wait_ge(s_dv.h, dv_dn[pt])
                mm = None
                for tt in range(2):
                    for half in range(2):
                        for ic in range(IC):
                            mm = t.matmul(
                                ps_d[bs + tt * 2 + half][:],
                                ihg[:, ic * 1024 + g4 * 256 + tt * 128:
                                    ic * 1024 + g4 * 256 + (tt + 1) * 128],
                                iwd[:, ic * D + half * 512:ic * D + (half + 1) * 512],
                                start=(ic == 0), stop=(ic == IC - 1))
                s_pe.inc(mm)
                pe_dn[(th2, g4)] = s_pe.n
            blk.tensor(f_mm)

            def f_de(v, th2=th2, g4=g4, tg=tg, bs=bs):
                v.wait_ge(s_pe.h, pe_dn[(th2, g4)])
                for tt in range(2):
                    for half in range(2):
                        col = (tg * 2 + tt) * D + half * 512
                        s_dv.inc(v.tensor_tensor(k.down_out[:, col:col + 512],
                                                 ps_d[bs + tt * 2 + half][:],
                                                 sdb[:, half * 512:(half + 1) * 512], ALU.mult))
                dv_dn[(th2, g4)] = s_dv.n
            blk.vector(f_de)
    k.n_down = dv_dn[(1, 3)]

    if k.debug:
        def f_dbg(e):
            e.wait_ge(s_dv.h, k.n_down)
            k.s_st.inc(e.dma_start(out=k.dbg["d_dn"], in_=k.down_out[:, 0:1024]), 16)
            k.s_st.inc(e.dma_start(out=k.dbg["d_sc"][0:1, 0:12], in_=k.sxh[:]), 16)
        blk.sync(f_dbg)


def _p5(nc, k, en):
    blk, din = k.blk, k.din
    s_sy, s_gp, s_cc = k.s_sy, k.s_gp, k.s_cc
    s_pe, s_dv, s_st = k.s_pe, k.s_dv, k.s_st

    paT = en(nc.sbuf_tensor("paT", [128, DC * TH], BF16))
    wobf = en(nc.sbuf_tensor("wobf", [128, DC * D], BF16))
    wos = [en(nc.sbuf_tensor(f"wos{j}", [128, D], F32)) for j in range(2)]
    xfr = [en(nc.sbuf_tensor(f"xfr{j}", [128, D], F32)) for j in range(2)]
    acc = [en(nc.sbuf_tensor(f"acc{j}", [128, D], F32)) for j in range(2)]
    offt = en(nc.sbuf_tensor("offt", [1, 1], I32))
    ps_o = [en(nc.psum_tensor(f"ps_o{j}", [128, 512], F32)) for j in range(4)]

    sy_wo, dv_wo = {}, {}
    for dc in range(DC):
        j = dc % 2

        def f_l(e, dc=dc, j=j):
            if dc >= 2:
                e.wait_ge(s_dv.h, dv_wo[dc - 2])
            sw = k.s_w0 if j == 0 else k.s_w1
            sw.inc(e.dma_start(out=wos[j][:], in_=din["woT"][dc * 128:(dc + 1) * 128, :]), 16)
            sy_wo[dc] = (sw, sw.n)
        blk.sync(f_l)

        def f_c(v, dc=dc, j=j):
            v.wait_ge(sy_wo[dc][0].h, sy_wo[dc][1])
            s_dv.inc(v.tensor_copy(wobf[:, dc * D:(dc + 1) * D], wos[j][:]))
            dv_wo[dc] = s_dv.n
        blk.vector(f_c)
    n_wo = dv_wo[DC - 1]

    def f_pa(gp):
        gp.wait_ge(s_cc.h, k.n_ccag)
        k.s_pa.inc(gp.dma_start(out=offt[:], in_=din["toffr"]), 16)
        gp.wait_ge(k.s_pa.h, k.s_pa.n)
        off = gp.value_load(offt[0:1, 0:1])
        for c in range(DC):
            row = (c % 4) * 256 + (c // 4) * 128
            k.s_pa.inc(gp.dma_start(
                out=paT[:, c * TH:(c + 1) * TH],
                in_=k.cc_aout.ap()[row:row + 128, bass.ts(off, TH)]), 16)
    blk.gpsimd(f_pa)
    n_pa = k.s_pa.n

    pe_op, dv_fin, st_y, sy_xr = {}, {}, {}, {}
    for tt in range(NTH):
        j = tt % 2
        pss = ps_o[j * 2:j * 2 + 2]

        def f_xl(e, tt=tt, j=j):
            if tt >= 2:
                e.wait_ge(s_dv.h, dv_fin[tt - 2])
            sx = k.s_x0 if j == 0 else k.s_x1
            sx.inc(e.dma_start(out=xfr[j][:], in_=din["x_my"][tt * 128:(tt + 1) * 128, :]), 16)
            sy_xr[tt] = (sx, sx.n)
        blk.sync(f_xl)

        def f_mm(t, tt=tt, pss=pss):
            if tt == 0:
                t.wait_ge(k.s_pa.h, n_pa)
                t.wait_ge(s_dv.h, n_wo)
            if tt >= 2:
                t.wait_ge(s_dv.h, dv_fin[tt - 2])
            mm = None
            for half in range(2):
                for oc in range(DC):
                    mm = t.matmul(pss[half][:],
                                  paT[:, oc * TH + tt * 128:oc * TH + (tt + 1) * 128],
                                  wobf[:, oc * D + half * 512:oc * D + (half + 1) * 512],
                                  start=(oc == 0), stop=(oc == DC - 1))
            s_pe.inc(mm)
            pe_op[tt] = s_pe.n
        blk.tensor(f_mm)

        def f_fin(v, tt=tt, j=j, pss=pss):
            v.wait_ge(s_pe.h, pe_op[tt])
            v.wait_ge(sy_xr[tt][0].h, sy_xr[tt][1])
            v.wait_ge(s_dv.h, k.n_down)
            if tt >= 2:
                v.wait_ge(st_y[tt - 2][0].h, st_y[tt - 2][1])
            for half in range(2):
                sl = slice(half * 512, (half + 1) * 512)
                s_dv.inc(v.tensor_tensor(acc[j][:, sl], pss[half][:], k.bo_bc[:, sl], ALU.add))
                v.wait_ge(s_dv.h, s_dv.n)
                s_dv.inc(v.tensor_tensor(acc[j][:, sl], acc[j][:, sl],
                                         k.down_out[:, tt * D + half * 512:tt * D + (half + 1) * 512],
                                         ALU.add))
                v.wait_ge(s_dv.h, s_dv.n)
                s_dv.inc(v.tensor_tensor(acc[j][:, sl], acc[j][:, sl], xfr[j][:, sl], ALU.add))
            dv_fin[tt] = s_dv.n
        blk.vector(f_fin)

        def f_st(e, tt=tt, j=j):
            e.wait_ge(s_dv.h, dv_fin[tt])
            ss = k.s_st0 if j == 0 else k.s_st1
            ss.inc(e.dma_start(out=k.y[tt * 128:(tt + 1) * 128, :], in_=acc[j][:]), 16)
            st_y[tt] = (ss, ss.n)
        blk.sync(f_st)


def _host_inputs(inputs):
    f32 = np.float32
    import ml_dtypes
    bf16 = ml_dtypes.bfloat16
    x = np.ascontiguousarray(np.asarray(inputs["L0_attn_x"], f32))
    wq = np.asarray(inputs["L0_attn_wq"], f32)
    wk = np.asarray(inputs["L0_attn_wk"], f32)
    wv = np.asarray(inputs["L0_attn_wv"], f32)
    wo = np.asarray(inputs["L0_attn_wo"], f32)
    bq = np.asarray(inputs["L0_attn_bq"], f32)
    bk = np.asarray(inputs["L0_attn_bk"], f32)
    bv = np.asarray(inputs["L0_attn_bv"], f32)
    bo = np.asarray(inputs["L0_attn_bo"], f32)
    pw1 = np.asarray(inputs["L0_attn_pw1"], f32)
    pb1 = np.asarray(inputs["L0_attn_pb1"], f32)
    pw2 = np.asarray(inputs["L0_attn_pw2"], f32)
    pb2 = np.asarray(inputs["L0_attn_pb2"], f32)
    sw1 = np.asarray(inputs["L0_attn_sw1"], f32)
    sb1 = np.asarray(inputs["L0_attn_sb1"], f32)
    sw2 = np.asarray(inputs["L0_attn_sw2"], f32)
    sb2 = np.asarray(inputs["L0_attn_sb2"], f32)

    common = {
        "woT": np.ascontiguousarray(wo.T), "bo": bo,
        "wg": np.ascontiguousarray(np.asarray(inputs["L0_ffn_w_gate"], f32)),
        "wu": np.ascontiguousarray(np.asarray(inputs["L0_ffn_w_up"], f32)),
        "wd": np.ascontiguousarray(np.asarray(inputs["L0_ffn_w_down"], f32)),
        "clipx": np.asarray(inputs["L0_ffn_clip_x"], f32).reshape(1, 1),
        "clipd": np.asarray(inputs["L0_ffn_clip_down"], f32).reshape(1, 1),
        "sw1T": np.ascontiguousarray(sw1.T), "sb1": sb1,
        "sw2T": np.ascontiguousarray(sw2.T), "sb2": sb2.reshape(P3, 1),
        "pw1T": np.ascontiguousarray(pw1.transpose(0, 2, 1).reshape(P3 * D, DH)),
        "pb1": np.ascontiguousarray(pb1.reshape(P3 * DH)),
        "pw2T": np.ascontiguousarray(pw2.transpose(0, 2, 1).reshape(P3 * DH, H)),
        "pb2": np.ascontiguousarray(pb2.reshape(P3 * H)),
        "identf": np.eye(128, dtype=f32),
        "identb": np.eye(128).astype(bf16),
        "onesf": np.ones((128, 1), f32),
        "onesr": np.ones((1, 128), f32),
        "trimask": np.triu(np.ones((128, 128))).astype(bf16),
    }
    maps = []
    for core in range(8):
        bi, hh = divmod(core, 2)
        osl = slice(hh * 512, (hh + 1) * 512)
        hsel = np.zeros((16, 8), f32)
        for jj in range(8):
            hsel[hh * 8 + jj, jj] = 1.0
        m = dict(common)
        m.update({
            "x_full": np.ascontiguousarray(x[bi]),
            "x_my": np.ascontiguousarray(x[bi, hh * TH:(hh + 1) * TH]),
            "wqT": np.ascontiguousarray(wq[osl].T),
            "wkT": np.ascontiguousarray(wk[osl].T),
            "wvT": np.ascontiguousarray(wv[osl].T),
            "bq": np.ascontiguousarray(bq[osl]),
            "bk": np.ascontiguousarray(bk[osl]),
            "bv": np.ascontiguousarray(bv[osl]),
            "toffr": np.array([[hh]], np.int32),
            "hsel": hsel,
        })
        maps.append(m)
    return maps


def build(debug=False):
    return _build(debug=debug)


def kernel(**inputs):
    if "nc" not in _CACHE:
        _CACHE["nc"] = _build(debug=False)
    nc = _CACHE["nc"]
    maps = _host_inputs(inputs)
    res = run_bass_kernel_spmd(nc, maps, core_ids=list(range(8)))
    out = np.empty((B, L, D), np.float32)
    for core in range(8):
        bi, hh = divmod(core, 2)
        out[bi, hh * TH:(hh + 1) * TH] = res.results[core]["y"]
    return out
